# version 06
# baseline (speedup 1.0000x reference)
import math
from contextlib import ExitStack

import numpy as np
import ml_dtypes

import concourse.bass as bass
import concourse.mybir as mybir
from concourse.bass_utils import run_bass_kernel_spmd

F32 = mybir.dt.float32
BF16 = mybir.dt.bfloat16
FP16 = mybir.dt.float16
AF = mybir.ActivationFunctionType
ALU = mybir.AluOpType
AX = mybir.AxisListType

D = 2048
SEQ = 4096
HD = 128
HPC = 4
E = 8
NCH = 32
NG = 8
OCH = 8
NCT = 32
CEXP = 12.0
BIG = 60000.0
EPS = 1e-5
ISCALE = 1.0 / math.sqrt(HD)

ENGS = ("tensor", "vector", "scalar", "gpsimd", "sync")

PHASES = 4
LAST_PROG = None
NO_SMALLK = False
DEBUG = 0


class Prog:
    def __init__(self, nc, ctx):
        self.nc = nc
        self.q = {e: [] for e in ENGS}
        self.cnt = {}
        self.sems = {}
        self.ctx = ctx

    def sem(self, name):
        h = self.ctx.enter_context(self.nc.semaphore(name))
        self.sems[name] = h
        self.cnt[name] = 0

    def op(self, eng, fn, waits=(), inc=None, amt=1):
        ws = tuple((s, v) for (s, v) in waits if s is not None and v and v > 0)
        if inc is not None:
            self.cnt[inc] += amt
        self.q[eng].append((ws, fn, inc, amt))
        return self.cnt[inc] if inc is not None else None

    def c(self, sem):
        return self.cnt[sem]

    def opd(self, eng, fn, waits=(), inc=None, amt=1):
        self.op(eng, fn, waits=waits)
        return self.op(eng, lambda e: e.drain(), inc=inc, amt=amt)

    def run(self, eng, e, final_waits=()):
        seen = {}
        for ws, fn, inc, amt in self.q[eng]:
            for s, v in ws:
                if seen.get(s, 0) >= v:
                    continue
                seen[s] = v
                e.wait_ge(self.sems[s], v)
            ins = fn(e)
            if inc is not None:
                ins.then_inc(self.sems[inc], amt)
        for s, v in final_waits:
            if v > 0 and seen.get(s, 0) < v:
                e.wait_ge(self.sems[s], v)


class Slots:
    def __init__(self, n):
        self.n = n
        self.free_at = [None] * n

    def acquire(self, i):
        w = self.free_at[i % self.n]
        return [w] if w is not None else []

    def release(self, i, sem, val):
        self.free_at[i % self.n] = (sem, val)


def build(debug=0, phases=4, no_smallk=None):
    if no_smallk is None:
        no_smallk = NO_SMALLK
    nc = bass.Bass("TRN2", target_bir_lowering=False, debug=False)

    xb = nc.dram_tensor("xb", [SEQ, D], F32, kind="ExternalInput").ap()
    xo = nc.dram_tensor("xo", [1024, D], F32, kind="ExternalInput").ap()
    wqkvT = nc.dram_tensor("wqkvT", [D + 128, 1536], F32, kind="ExternalInput").ap()
    bqkv_fm = nc.dram_tensor("bqkv_fm", [128, 8], F32, kind="ExternalInput").ap()
    woTs = nc.dram_tensor("woTs", [512, D], F32, kind="ExternalInput").ap()
    wrT = nc.dram_tensor("wrT", [D + 128, E * D], F32, kind="ExternalInput").ap()
    weT = nc.dram_tensor("weT", [E, D, D], F32, kind="ExternalInput").ap()
    ebp_i = nc.dram_tensor("ebp", [128, D], F32, kind="ExternalInput").ap()
    masks_i = nc.dram_tensor("masks", [128, 4 * 512], F32, kind="ExternalInput").ap()
    ident_i = nc.dram_tensor("ident", [128, 128], F32, kind="ExternalInput").ap()
    consts_i = nc.dram_tensor("consts", [128, 1], F32, kind="ExternalInput").ap()

    out = nc.dram_tensor("out", [1024, D], F32, kind="ExternalOutput").ap()

    attn_tm = nc.dram_tensor("attn_tm", [SEQ, HPC * HD], FP16)
    xn_own = nc.dram_tensor("xn_own", [1024, D], FP16)
    rs_in = nc.dram_tensor("rs_in", [SEQ, D], FP16)
    rs_out_d = nc.dram_tensor("rs_out", [1024, D], FP16)

    dbg = {}
    if debug & 1:
        dbg["q"] = nc.dram_tensor("q_dbg", [512, SEQ], F32, kind="ExternalOutput").ap()
        dbg["k"] = nc.dram_tensor("k_dbg", [512, SEQ], F32, kind="ExternalOutput").ap()
        dbg["v"] = nc.dram_tensor("v_dbg", [512, 32 * 129], F32, kind="ExternalOutput").ap()
    if debug & 2:
        dbg["attn"] = nc.dram_tensor("attn_dbg", [SEQ, HPC * HD], F32, kind="ExternalOutput").ap()
        dbg["rs"] = nc.dram_tensor("rs_dbg", [1024, D], F32, kind="ExternalOutput").ap()
    if debug & 16:
        dbg["lg"] = nc.dram_tensor("lg_dbg", [128, 1024], F32, kind="ExternalOutput").ap()
        dbg["t2"] = nc.dram_tensor("t2_dbg", [128, 1536], F32, kind="ExternalOutput").ap()
    if debug & 8:
        dbg["p"] = nc.dram_tensor("p_dbg", [128, 1024], F32, kind="ExternalOutput").ap()
        dbg["l"] = nc.dram_tensor("l_dbg", [128, 8], F32, kind="ExternalOutput").ap()
        dbg["s"] = nc.dram_tensor("s_dbg", [128, 1024], F32, kind="ExternalOutput").ap()
        dbg["av"] = nc.dram_tensor("av_dbg", [128, 1024], F32, kind="ExternalOutput").ap()
    if debug & 4:
        dbg["c"] = nc.dram_tensor("c_dbg", [128, 64], F32, kind="ExternalOutput").ap()
        dbg["acc"] = nc.dram_tensor("acc_dbg", [128, OCH * D], F32, kind="ExternalOutput").ap()

    ctx = ExitStack()
    inner_scopes = []

    def _close_all():
        for sc in reversed(inner_scopes):
            sc.close()

    with ctx:
        P = Prog(nc, ctx)
        SBP = lambda n, s, d: ctx.enter_context(nc.sbuf_tensor(n, s, d))

        identb = SBP("identb", [128, 128], FP16)
        onesM = SBP("onesM", [128, 128], FP16)
        bqv = SBP("bqv", [128, 8], F32)
        negC = SBP("negC", [128, 1], F32)
        c_all = SBP("c_all", [128, 64], F32)
        cbf = SBP("cbf", [128, 64], FP16)
        recs = SBP("recs", [128, 2], F32)
        if debug & 8:
            lsave = SBP("lsave", [128, 8], F32)
            ssave = SBP("ssave", [128, 1024], F32)

        qkv_scope = ExitStack()
        inner_scopes.append(qkv_scope)
        SBq = lambda n, s, d: qkv_scope.enter_context(nc.sbuf_tensor(n, s, d))
        Qh = [SBq(f"Qh{h}", [128, SEQ], FP16) for h in range(HPC)]
        Kh = [SBq(f"Kh{h}", [128, SEQ], FP16) for h in range(HPC)]
        Vh = [SBq(f"Vh{h}", [128, 32 * 129], FP16) for h in range(HPC)]

        p1_scope = ExitStack()
        inner_scopes.append(p1_scope)
        SB1 = lambda n, s, d: p1_scope.enter_context(nc.sbuf_tensor(n, s, d, side="right"))
        x_sl = SB1("x_sl", [128, 2 * D], F32)
        stat1 = SB1("stat1", [128, 2 * 24], F32)
        mv1 = SB1("mv1", [128, 2 * 2], F32)
        sm1 = SB1("sm1", [128, 2 * 4], F32)
        xn_tm = SB1("xn_tm", [128, 2 * D], FP16)
        wqT = SB1("wqT", [128, 17 * 1536], FP16)
        xnfm = SB1("xnfm", [128, 2 * 8192], FP16)

        SAp = ctx.enter_context(nc.psum_tensor("SA", [128, 1024], F32))
        SBb = ctx.enter_context(nc.psum_tensor("SBb", [128, 1024], F32))
        B4 = [ctx.enter_context(nc.psum_tensor(f"B{i}", [128, 512], F32)) for i in range(4)]
        SAb16 = SAp.ap().bitcast(FP16)
        SBb16 = SBb.ap().bitcast(FP16)
        BANKS = [SAp[:, 0:512], SAp[:, 512:1024], SBb[:, 0:512], SBb[:, 512:1024],
                 B4[0][:, :], B4[1][:, :], B4[2][:, :], B4[3][:, :]]

        for s in ("su", "sx0", "sx1", "dxn", "arstd", "trc", "cpc", "mmq", "evq",
                  "vone", "smm", "sexp", "smsk", "avp", "ep", "sto", "su2",
                  "sxo", "dxo", "aro", "xns", "trx",
                  "p25", "wsl", "omm", "oev", "rsst", "cc",
                  "rmm", "rev", "lrel", "du", "asig", "dge2", "pw", "dself",
                  "dcb", "tcT", "dcT", "emm", "eev",
                  "fev", "outd", "dbgs", "rsod", "dbgp"):
            P.sem(s)

        P.op("gpsimd", lambda e: e.dma_start(out=identb[:, :], in_=ident_i), inc="su", amt=16)
        P.op("sync", lambda e: e.dma_start(out=bqv[:, :], in_=bqkv_fm), inc="su", amt=16)
        P.op("sync", lambda e: e.dma_start(out=negC[:, :], in_=consts_i), inc="su", amt=16)
        P.op("gpsimd", lambda e: e.dma_start(
            out=wqT.ap().rearrange("p (c n) -> p c n", n=1536),
            in_=wqkvT.rearrange("(c p) n -> p c n", p=128)), inc="su", amt=16)
        SU_ALL = P.c("su")

        for h in range(HPC):
            P.op("vector", lambda e, h=h: e.memset(
                Vh[h].ap().rearrange("p (k c) -> p k c", c=129)[:, :, 128:129], 1.0),
                inc="vone")
        P.op("vector", lambda e: e.memset(c_all[:, :], 0.0), inc="vone")
        P.op("vector", lambda e: e.memset(onesM[:, :], 1.0), inc="vone")
        VONE_ALL = P.c("vone")

        xslot = Slots(2)
        xnslot = Slots(2)
        trslot = Slots(2)
        fmslot = Slots(2)
        qkv_bank = Slots(4)

        for ch in range(NCH):
            g, sub = ch // 4, ch % 4
            sl = ch % 2
            xsem = "sx0" if sl == 0 else "sx1"
            P.op("sync", lambda e, ch=ch, sl=sl: e.dma_start(
                out=x_sl[:, sl * D:(sl + 1) * D], in_=xb[ch * 128:(ch + 1) * 128, :]),
                waits=xslot.acquire(ch), inc=xsem, amt=16)
            xv = P.c(xsem)

            def ln_stats(e, sl=sl):
                for i in range(4):
                    ins = e.bn_stats(stat1[:, sl * 24 + 6 * i: sl * 24 + 6 * i + 6],
                                     x_sl[:, sl * D + 512 * i: sl * D + 512 * i + 512])
                return ins
            P.op("vector", ln_stats, waits=[(xsem, xv)], inc="dself")
            sv = P.c("dself")
            P.op("vector", lambda e, sl=sl: e.bn_aggr(
                mv1[:, sl * 2: sl * 2 + 2], stat1[:, sl * 24: sl * 24 + 24]),
                waits=[("dself", sv)], inc="dself")
            sv = P.c("dself")
            P.op("vector", lambda e, sl=sl: e.tensor_scalar(
                sm1[:, sl * 4: sl * 4 + 1], mv1[:, sl * 2: sl * 2 + 1], -1.0, None, ALU.mult),
                waits=[("dself", sv)])
            P.op("vector", lambda e, sl=sl: e.tensor_scalar(
                sm1[:, sl * 4 + 1: sl * 4 + 2], mv1[:, sl * 2 + 1: sl * 2 + 2], EPS, None, ALU.add),
                inc="dself")
            sv = P.c("dself")
            P.op("scalar", lambda e, sl=sl: e.activation(
                sm1[:, sl * 4 + 2: sl * 4 + 3], sm1[:, sl * 4 + 1: sl * 4 + 2], AF.Ln),
                waits=[("dself", sv)])
            P.op("scalar", lambda e, sl=sl: e.activation(
                sm1[:, sl * 4 + 3: sl * 4 + 4], sm1[:, sl * 4 + 2: sl * 4 + 3], AF.Exp, scale=-0.5),
                inc="arstd")
            av = P.c("arstd")
            P.op("vector", lambda e, sl=sl: e.tensor_scalar(
                xn_tm[:, sl * D:(sl + 1) * D], x_sl[:, sl * D:(sl + 1) * D],
                sm1[:, sl * 4: sl * 4 + 1], sm1[:, sl * 4 + 3: sl * 4 + 4],
                ALU.add, ALU.mult),
                waits=[("arstd", av)] + xnslot.acquire(ch), inc="dxn")
            xslot.release(ch, "dxn", P.c("dxn"))
            dv = P.c("dxn")

            fw = fmslot.acquire(g) if sub == 0 else []
            for dk in range(16):
                gj = ch * 16 + dk
                P.op("tensor", lambda e, sl=sl, dk=dk, gj=gj: e.transpose(
                    SAb16[:, (gj % 2) * 1024:(gj % 2) * 1024 + 128],
                    xn_tm[:, sl * D + dk * 128: sl * D + dk * 128 + 128], identb[:, :]),
                    waits=[("dxn", dv), ("su", SU_ALL)] + trslot.acquire(gj), inc="trc")
                tv = P.c("trc")
                P.op("vector", lambda e, g=g, sub=sub, dk=dk, gj=gj: e.tensor_copy(
                    xnfm[:, (g % 2) * 8192 + dk * 512 + sub * 128:
                         (g % 2) * 8192 + dk * 512 + sub * 128 + 128],
                    SAb16[:, (gj % 2) * 1024:(gj % 2) * 1024 + 128]),
                    waits=[("trc", tv)] + (fw if dk == 0 else []), inc="cpc")
                trslot.release(gj, "cpc", P.c("cpc"))
            xnslot.release(ch, "trc", P.c("trc"))

            if sub == 3:
                cpv = P.c("cpc")
                for part in range(12):
                    bank = B4[part % 4]
                    bw = qkv_bank.acquire(part)
                    kind, p = part // 4, part % 4
                    for dk in range(16):
                        if kind < 2:
                            col0 = kind * 512 + p * 128
                            P.op("tensor", lambda e, bank=bank, dk=dk, col0=col0, g=g: e.matmul(
                                bank[:, :],
                                wqT[:, dk * 1536 + col0: dk * 1536 + col0 + 128],
                                xnfm[:, (g % 2) * 8192 + dk * 512:(g % 2) * 8192 + dk * 512 + 512],
                                start=(dk == 0), stop=(dk == 15)),
                                waits=([("cpc", cpv)] + bw) if dk == 0 else [],
                                inc="mmq" if dk == 15 else None)
                        else:
                            P.op("tensor", lambda e, bank=bank, dk=dk, p=p, g=g: e.matmul(
                                bank[:, :],
                                xnfm[:, (g % 2) * 8192 + dk * 512 + p * 128:
                                     (g % 2) * 8192 + dk * 512 + p * 128 + 128],
                                wqT[:, dk * 1536 + 1024: dk * 1536 + 1536],
                                start=(dk == 0), stop=False),
                                waits=([("cpc", cpv)] + bw) if dk == 0 else [])
                    if kind == 2:
                        P.op("tensor", lambda e, bank=bank: e.matmul(
                            bank[:, :], onesM[:, :],
                            wqT[:, 16 * 1536 + 1024: 16 * 1536 + 1536],
                            start=False, stop=True),
                            waits=[("vone", VONE_ALL)],
                            inc="mmq")
                    mv_ = P.c("mmq")
                    if kind < 2:
                        dst = (Qh if kind == 0 else Kh)[p]
                        bcol = kind * 4 + p
                        P.op("scalar", lambda e, dst=dst, g=g, bank=bank, bcol=bcol: e.activation(
                            dst[:, g * 512:(g + 1) * 512], bank[:, :], AF.Identity,
                            bias=bqv[:, bcol:bcol + 1], scale=1.0),
                            waits=[("mmq", mv_), ("su", SU_ALL)], inc="evq")
                    else:
                        kc = g * 4 + p
                        for h in range(HPC):
                            P.op("vector", lambda e, h=h, kc=kc, bank=bank: e.tensor_copy(
                                Vh[h][:, kc * 129: kc * 129 + 128],
                                bank[:, h * 128:(h + 1) * 128]),
                                waits=[("mmq", mv_)] if h == 0 else [],
                                inc="evq" if h == 3 else None)
                    qkv_bank.release(part, "evq", P.c("evq"))
                fmslot.release(g, "mmq", P.c("mmq"))
        EVQ_ALL = P.c("evq")
        assert EVQ_ALL == 96
        MMQ_ALL = P.c("mmq")
        TRC_ALL = P.c("trc")
        DXN_ALL = P.c("dxn")

        if debug & 1:
            for h in range(HPC):
                P.op("gpsimd", lambda e, h=h: e.dma_start(
                    out=dbg["q"][h * 128:(h + 1) * 128, :], in_=Qh[h][:, :]),
                    waits=[("evq", EVQ_ALL)], inc="dbgs", amt=16)
                P.op("gpsimd", lambda e, h=h: e.dma_start(
                    out=dbg["k"][h * 128:(h + 1) * 128, :], in_=Kh[h][:, :]),
                    inc="dbgs", amt=16)
                P.op("gpsimd", lambda e, h=h: e.dma_start(
                    out=dbg["v"][h * 128:(h + 1) * 128, :], in_=Vh[h][:, :]),
                    waits=[("vone", VONE_ALL)], inc="dbgs", amt=16)

        p1_scope.close()
        r1_scope = ExitStack()
        inner_scopes.append(r1_scope)
        SBr1 = lambda n, s, d: r1_scope.enter_context(nc.sbuf_tensor(n, s, d, side="right"))
        maskb = SBr1("maskb", [128, 4 * 512], FP16)
        acc = SBr1("acc", [128, OCH * D], F32)
        Pb = SBr1("Pb", [128, 4 * 1024], FP16)
        stg = SBr1("stg", [128, 4 * 128], FP16)
        cT = SBr1("cT", [128, 1024], FP16)
        xo_tm = SBr1("xo_tm", [128, 2 * D], FP16)
        stat2 = SBr1("stat2", [128, 2 * 24], F32)
        mv2 = SBr1("mv2", [128, 2 * 2], F32)
        sm2 = SBr1("sm2", [128, 2 * 4], F32)

        R1GATE = [("mmq", MMQ_ALL), ("trc", TRC_ALL), ("dxn", DXN_ALL)]
        P.op("gpsimd", lambda e: e.dma_start(out=maskb[:, :], in_=masks_i),
             waits=R1GATE, inc="su2", amt=16)
        MASK_V = P.c("su2")

        if phases < 2:
            P.op("sync", lambda e: e.dma_start(out=out, in_=xo),
                 waits=[("evq", EVQ_ALL)], inc="outd", amt=16)
            _emit(nc, P, [("outd", P.c("outd")), ("dbgs", P.c("dbgs"))])
            _close_all()
            return nc

        for tc in range(OCH):
            P.op("sync", lambda e, tc=tc: e.dma_start(
                out=acc[:, tc * D:(tc + 1) * D], in_=xo[tc * 128:(tc + 1) * 128, :]),
                waits=R1GATE if tc == 0 else [], inc="sxo", amt=16)

        sbank = Slots(2)
        avbank = Slots(4)
        stslot = Slots(4)
        pbslot = Slots(4)
        xoslot = Slots(2)

        def emit_own_ln(tc):
            sl = tc % 2

            def ln2(e, sl=sl, tc=tc):
                for i in range(4):
                    ins = e.bn_stats(stat2[:, sl * 24 + 6 * i: sl * 24 + 6 * i + 6],
                                     acc[:, tc * D + 512 * i: tc * D + 512 * i + 512])
                return ins
            P.op("vector", ln2, waits=[("sxo", 16 * (tc + 1))], inc="dself")
            sv = P.c("dself")
            P.op("vector", lambda e, sl=sl: e.bn_aggr(
                mv2[:, sl * 2: sl * 2 + 2], stat2[:, sl * 24: sl * 24 + 24]),
                waits=[("dself", sv)], inc="dself")
            sv = P.c("dself")
            P.op("vector", lambda e, sl=sl: e.tensor_scalar(
                sm2[:, sl * 4: sl * 4 + 1], mv2[:, sl * 2: sl * 2 + 1], -1.0, None, ALU.mult),
                waits=[("dself", sv)])
            P.op("vector", lambda e, sl=sl: e.tensor_scalar(
                sm2[:, sl * 4 + 1: sl * 4 + 2], mv2[:, sl * 2 + 1: sl * 2 + 2], EPS, None, ALU.add),
                inc="dself")
            sv = P.c("dself")
            P.op("scalar", lambda e, sl=sl: e.activation(
                sm2[:, sl * 4 + 2: sl * 4 + 3], sm2[:, sl * 4 + 1: sl * 4 + 2], AF.Ln),
                waits=[("dself", sv)])
            P.op("scalar", lambda e, sl=sl: e.activation(
                sm2[:, sl * 4 + 3: sl * 4 + 4], sm2[:, sl * 4 + 2: sl * 4 + 3], AF.Exp, scale=-0.5),
                inc="aro")
            av = P.c("aro")
            P.opd("vector", lambda e, sl=sl, tc=tc: e.tensor_scalar(
                xo_tm[:, sl * D:(sl + 1) * D], acc[:, tc * D:(tc + 1) * D],
                sm2[:, sl * 4: sl * 4 + 1], sm2[:, sl * 4 + 3: sl * 4 + 4],
                ALU.add, ALU.mult),
                waits=[("aro", av)] + xoslot.acquire(tc), inc="dxo")
            dv = P.c("dxo")
            P.op("sync", lambda e, tc=tc, sl=sl: e.dma_start(
                out=xn_own.ap()[tc * 128:(tc + 1) * 128, :],
                in_=xo_tm[:, sl * D:(sl + 1) * D]),
                waits=[("dxo", dv)], inc="xns", amt=16)
            xoslot.release(tc, "xns", P.c("xns"))

        pairs = []
        for h in range(HPC):
            for qt in range(NG):
                nkc = 4 * (qt + 1)
                for kp in range(nkc // 2):
                    pairs.append((h, qt, kp, nkc))

        def emit_av(job):
            h, qt, kp, nkc, pidx, sev, mwaits = job
            kc0, kc1 = 2 * kp, 2 * kp + 1
            par = pidx % 4
            for j, kc in enumerate((kc0, kc1)):
                for qs in range(4):
                    first = (kc == 0)
                    lastmm = (kc == nkc - 1)
                    w = []
                    if qs == 0:
                        w = [("sexp", sev)] + mwaits
                    if first and qs == 0:
                        w = w + [("vone", VONE_ALL)]
                    if first:
                        w = w + avbank.acquire(qs)
                    P.op("tensor", lambda e, qs=qs, par=par, j=j, kc=kc, h=h,
                         first=first, lastmm=lastmm: e.matmul(
                             B4[qs][:, 0:129],
                             Pb[:, par * 1024 + j * 512 + qs * 128:
                                par * 1024 + j * 512 + qs * 128 + 128],
                             Vh[h][:, kc * 129: kc * 129 + 129],
                             start=first, stop=lastmm),
                        waits=w,
                        inc="avp" if (j == 1 and qs == 3) else None)
            pbslot.release(pidx, "avp", P.c("avp"))

        def emit_epilogue(h, qt):
            n = h * NG + qt
            apv = P.c("avp")
            for qs in range(4):
                ssl = (n * 4 + qs) % 2
                sidx = (n * 4 + qs) % 4
                P.op("vector", lambda e, qs=qs, ssl=ssl: e.reciprocal(
                    recs[:, ssl:ssl + 1], B4[qs][:, 128:129]),
                    waits=[("avp", apv)], inc="dself")
                sv2 = P.c("dself")
                stw = stslot.acquire(n * 4 + qs)
                P.opd("vector", lambda e, qs=qs, ssl=ssl, sidx=sidx: e.tensor_scalar(
                    stg[:, sidx * 128:(sidx + 1) * 128], B4[qs][:, 0:128],
                    recs[:, ssl:ssl + 1], None, ALU.mult),
                    waits=[("dself", sv2)] + stw, inc="ep")
                epv = P.c("ep")
                avbank.release(qs, "ep", epv)
                P.op("sync", lambda e, h=h, qt=qt, qs=qs, sidx=sidx: e.dma_start(
                    out=attn_tm.ap()[qt * 512 + qs * 128: qt * 512 + qs * 128 + 128,
                                     h * 128:(h + 1) * 128],
                    in_=stg[:, sidx * 128:(sidx + 1) * 128]),
                    waits=[("ep", epv)], inc="sto", amt=16)
                stslot.release(n * 4 + qs, "sto", P.c("sto"))

        pair_idx = 0
        pending = None
        for h, qt, kp, nkc in pairs:
            kc0, kc1 = 2 * kp, 2 * kp + 1
            par = pair_idx % 4
            Sp = SAp if pair_idx % 2 == 0 else SBb
            bw = sbank.acquire(pair_idx)
            for j, kc in enumerate((kc0, kc1)):
                P.op("tensor", lambda e, Sp=Sp, j=j, kc=kc, h=h, qt=qt: e.matmul(
                    Sp[:, j * 512:(j + 1) * 512],
                    Kh[h][:, kc * 128: kc * 128 + 128],
                    Qh[h][:, qt * 512: qt * 512 + 512],
                    start=True, stop=True),
                    waits=([("evq", EVQ_ALL)] + bw) if j == 0 else [],
                    inc="smm" if j == 1 else None)
            smv = P.c("smm")
            pw_ = pbslot.acquire(pair_idx)
            P.op("scalar", lambda e, Sp=Sp, par=par: e.activation(
                Pb[:, par * 1024:(par + 1) * 1024], Sp[:, :], AF.Exp,
                bias=negC[:, :], scale=ISCALE),
                waits=[("smm", smv), ("su", SU_ALL), ("su2", MASK_V)] + pw_,
                inc="sexp")
            sev = P.c("sexp")
            sbank.release(pair_idx, "sexp", sev)
            mwaits = []
            for j, kc in enumerate((kc0, kc1)):
                c = kc - 4 * qt
                if 0 <= c <= 3:
                    P.op("vector", lambda e, par=par, j=j, c=c: e.tensor_tensor(
                        Pb[:, par * 1024 + j * 512: par * 1024 + j * 512 + 512],
                        Pb[:, par * 1024 + j * 512: par * 1024 + j * 512 + 512],
                        maskb[:, c * 512:(c + 1) * 512], ALU.mult),
                        waits=[("sexp", sev), ("su2", MASK_V)], inc="smsk")
                    mwaits = [("smsk", P.c("smsk"))]
            if pending is not None:
                emit_av(pending)
                if (pending[0], pending[1]) != (h, qt):
                    emit_epilogue(pending[0], pending[1])
            pending = (h, qt, kp, nkc, pair_idx, sev, mwaits)
            pair_idx += 1
        emit_av(pending)
        emit_epilogue(pending[0], pending[1])

        if phases >= 2.2:
            for tc_ in range(OCH):
                emit_own_ln(tc_)
        STO_ALL = P.c("sto")
        assert STO_ALL == 32 * 4 * 16
        XNS_ALL = P.c("xns")
        EP_ALL = P.c("ep")
        if debug & 8:
            P.op("gpsimd", lambda e: e.dma_start(out=dbg["l"], in_=lsave[:, :]),
                 waits=[("dbgp", P.c("dbgp")), ("ep", EP_ALL)], inc="dbgs", amt=16)
        if phases < 2.2:
            P.op("sync", lambda e: e.dma_start(out=out, in_=xo),
                 waits=[("sto", STO_ALL), ("ep", EP_ALL)], inc="outd", amt=16)
            _emit(nc, P, [("outd", P.c("outd")), ("dbgs", P.c("dbgs"))])
            _close_all()
            return nc
        assert XNS_ALL == OCH * 16

        qkv_scope.close()
        r2_scope = ExitStack()
        inner_scopes.append(r2_scope)
        xnfo = r2_scope.enter_context(nc.sbuf_tensor("xnfo", [128, 16 * 1024], FP16))
        p25_scope = ExitStack()
        inner_scopes.append(p25_scope)
        SB25 = lambda n, s, d: p25_scope.enter_context(nc.sbuf_tensor(n, s, d))
        attnfm = SB25("attnfm", [128, 4 * SEQ], FP16)
        woslab = SB25("woslab", [128, 4 * D], FP16)
        rsbuf = SB25("rsbuf", [128, 4 * 512], FP16)

        for dk in range(16):
            P.op("sync", lambda e, dk=dk: e.dma_start_transpose(
                out=xnfo[:, dk * 1024:(dk + 1) * 1024],
                in_=xn_own.ap()[:, dk * 128:(dk + 1) * 128]),
                waits=[("xns", XNS_ALL), ("ep", EP_ALL)], inc="trx", amt=16)
        TRX_ALL = P.c("trx")
        if phases < 2.3:
            P.op("sync", lambda e: e.dma_start(out=out, in_=xo),
                 waits=[("sto", STO_ALL), ("trx", TRX_ALL)], inc="outd", amt=16)
            _emit(nc, P, [("outd", P.c("outd")), ("dbgs", P.c("dbgs"))])
            _close_all()
            return nc

        for fk in range(4):
            P.op("sync", lambda e, fk=fk: e.dma_start_transpose(
                out=attnfm[:, fk * SEQ:(fk + 1) * SEQ],
                in_=attn_tm.ap()[:, fk * 128:(fk + 1) * 128]),
                waits=[("sto", STO_ALL), ("ep", EP_ALL)], inc="p25", amt=16)
        P25_ALL = P.c("p25")
        if debug & 2:
            P.op("gpsimd", lambda e: e.dma_start(out=dbg["attn"], in_=attn_tm.ap()),
                 waits=[("sto", STO_ALL)], inc="dbgs", amt=16)

        P.op("gpsimd", lambda e: e.dma_start(
            out=woslab.ap().rearrange("p (c n) -> p c n", n=D),
            in_=woTs.rearrange("(c p) n -> p c n", p=128)),
            waits=[("ep", EP_ALL)], inc="wsl", amt=16)
        WO_V = P.c("wsl")

        obank = avbank
        oslot = Slots(4)
        for tt in range(NCH):
            for ot in range(4):
                oi = tt * 4 + ot
                bank = B4[oi % 4]
                bw = obank.acquire(oi)
                for fk in range(4):
                    P.op("tensor", lambda e, bank=bank, fk=fk, tt=tt, ot=ot: e.matmul(
                        bank[:, :],
                        attnfm[:, fk * SEQ + tt * 128: fk * SEQ + tt * 128 + 128],
                        woslab[:, fk * D + ot * 512: fk * D + ot * 512 + 512],
                        start=(fk == 0), stop=(fk == 3)),
                        waits=([("p25", P25_ALL), ("wsl", WO_V)] + bw) if fk == 0 else [],
                        inc="omm" if fk == 3 else None)
                omv = P.c("omm")
                sw = oslot.acquire(oi)
                if oi % 2 == 0:
                    P.opd("scalar", lambda e, bank=bank, osl=oi % 4: e.activation(
                        rsbuf[:, osl * 512:(osl + 1) * 512], bank[:, :], AF.Identity),
                        waits=[("omm", omv)] + sw, inc="oev")
                else:
                    P.opd("vector", lambda e, bank=bank, osl=oi % 4: e.tensor_copy(
                        rsbuf[:, osl * 512:(osl + 1) * 512], bank[:, :]),
                        waits=[("omm", omv)] + sw, inc="oev")
                oev = P.c("oev")
                obank.release(oi, "oev", oev)
                P.op("sync", lambda e, tt=tt, ot=ot, osl=oi % 4: e.dma_start(
                    out=rs_in.ap()[tt * 128:(tt + 1) * 128, ot * 512:(ot + 1) * 512],
                    in_=rsbuf[:, osl * 512:(osl + 1) * 512]),
                    waits=[("oev", oev)], inc="rsst", amt=16)
                oslot.release(oi, "rsst", P.c("rsst"))
        RSST_ALL = P.c("rsst")
        OMM_ALL = P.c("omm")
        if phases < 2.4:
            P.op("sync", lambda e: e.dma_start(out=out, in_=xo),
                 waits=[("rsst", RSST_ALL)], inc="outd", amt=16)
            _emit(nc, P, [("outd", P.c("outd")), ("dbgs", P.c("dbgs"))])
            _close_all()
            return nc

        if phases < 3:
            P.op("gpsimd", lambda e: e.collective_compute(
                "ReduceScatter", ALU.add,
                replica_groups=[[0, 1, 2, 3], [4, 5, 6, 7]],
                ins=[rs_in.ap().opt()], outs=[rs_out_d.ap().opt()]),
                waits=[("rsst", RSST_ALL)], inc="cc")
            CC_V = P.c("cc")
        if phases < 3:
            if debug & 2:
                P.op("gpsimd", lambda e: e.dma_start(out=dbg["rs"], in_=rs_out_d.ap()),
                     waits=[("cc", CC_V)], inc="dbgs", amt=16)
            P.op("sync", lambda e: e.dma_start(out=out, in_=xo),
                 waits=[("rsst", RSST_ALL)], inc="outd", amt=16)
            _emit(nc, P, [("outd", P.c("outd")), ("dbgs", P.c("dbgs")), ("cc", CC_V)])
            _close_all()
            return nc

        p25_scope.close()
        p3_scope = ExitStack()
        inner_scopes.append(p3_scope)
        SB3 = lambda n, s, d: p3_scope.enter_context(nc.sbuf_tensor(n, s, d))
        SLABW = 17 * 512
        wsl = SB3("wsl", [128, 2 * SLABW], FP16)
        NT2 = 3
        lb = SB3("lb", [128, 3 * 1024], FP16)
        m1b = SB3("m1b", [128, NT2 * 128], FP16)
        m2b = SB3("m2b", [128, NT2 * 128], FP16)
        s12 = SB3("s12", [128, NT2 * 128], FP16)
        ge1b = SB3("ge1b", [128, NT2 * 1024], FP16)
        lmb = SB3("lmb", [128, NT2 * 1024], FP16)
        ub = SB3("ub", [128, NT2 * 1024], FP16)
        sgb = SB3("sgb", [128, NT2 * 1024], FP16)
        wb_ = SB3("wb", [128, NT2 * 1024], FP16)
        tsc = SB3("tsc", [128, 1024], FP16)
        cw = SB3("cw", [128, 4 * 1024], FP16)
        rtmp = SB3("rtmp", [128, 2 * 16], F32)
        ebpb = SB3("ebpb", [128, D], FP16)

        P3GATE = [("omm", OMM_ALL)]
        P.op("gpsimd", lambda e: e.dma_start(out=ebpb[:, :], in_=ebp_i),
             waits=P3GATE, inc="su2", amt=16)
        SU2_V = P.c("su2")
        P.op("gpsimd", lambda e: e.memset(cw[:, :], 0.0), waits=P3GATE, inc="pw")

        p3bank = Slots(8)
        p3bank.free_at[0] = p3bank.free_at[1] = sbank.free_at[0]
        p3bank.free_at[2] = p3bank.free_at[3] = sbank.free_at[1]
        for i in range(4):
            p3bank.free_at[4 + i] = obank.free_at[i]

        wslslot = Slots(2)
        slab_ready = {}

        def load_router_slab(ct):
            slw = wslslot.acquire(ct) + (P3GATE if ct < 2 else [])
            P.op("gpsimd", lambda e, ct=ct: e.dma_start(
                out=wsl.ap().rearrange("p (s c n) -> p s c n", s=2, n=512)[:, ct % 2],
                in_=wrT[:, ct * 512:(ct + 1) * 512].rearrange("(c p) n -> p c n", p=128)),
                waits=slw, inc="wsl", amt=16)
            slab_ready[ct] = P.c("wsl")

        load_router_slab(0)
        load_router_slab(1)
        CC_V = None

        lbslot = Slots(3)
        t2slot = Slots(NT2)
        tile_idx = 0
        pair_cnt = 0

        for ct in range(NCT):
            for tc in range(OCH):
                bank = BANKS[tc]
                bw = p3bank.acquire(tc)
                slv = slab_ready[ct]
                for dk in range(16):
                    P.op("tensor", lambda e, bank=bank, dk=dk, ct=ct, tc=tc: e.matmul(
                        bank,
                        xnfo[:, dk * 1024 + tc * 128: dk * 1024 + tc * 128 + 128],
                        wsl[:, (ct % 2) * SLABW + dk * 512:(ct % 2) * SLABW + dk * 512 + 512],
                        start=(dk == 0), stop=False),
                        waits=([("trx", TRX_ALL), ("wsl", slv)] + bw) if dk == 0 else [])
                P.op("tensor", lambda e, bank=bank, ct=ct: e.matmul(
                    bank, onesM[:, :],
                    wsl[:, (ct % 2) * SLABW + 16 * 512:(ct % 2) * SLABW + 17 * 512],
                    start=False, stop=True),
                    inc="rmm")
                rmv = P.c("rmm")
                half = tile_idx % 2
                psl = (tile_idx // 2) % 3
                lw = lbslot.acquire(tile_idx // 2) if half == 0 else []
                P.opd("scalar", lambda e, bank=bank, psl=psl, half=half: e.activation(
                    lb[:, psl * 1024 + half * 512: psl * 1024 + half * 512 + 512],
                    bank, AF.Identity),
                    waits=[("rmm", rmv)] + lw, inc="rev")
                p3bank.release(tc, "rev", P.c("rev"))
                tile_idx += 1

                if half == 1:
                    pi = pair_cnt
                    pair_cnt += 1
                    revv = P.c("rev")
                    dbg_lb_hold = None
                    if debug & 16 and pi == 0:
                        P.op("gpsimd", lambda e: e.dma_start(
                            out=dbg["lg"], in_=lb[:, 0:1024]),
                            waits=[("rev", 2)], inc="dbgp", amt=16)
                        dbg_lb_hold = P.c("dbgp")
                    ts = pi % NT2
                    L = lb[:, psl * 1024:(psl + 1) * 1024]
                    Lv = lb.ap().rearrange("p (s g e) -> p s g e", s=3, e=8)[:, psl]
                    M1 = m1b[:, ts * 128: ts * 128 + 128]
                    M2 = m2b[:, ts * 128: ts * 128 + 128]
                    S12 = s12[:, ts * 128: ts * 128 + 128]
                    GE = ge1b[:, ts * 1024: ts * 1024 + 1024]
                    LM = lmb[:, ts * 1024: ts * 1024 + 1024]
                    U = ub[:, ts * 1024: ts * 1024 + 1024]
                    SG = sgb[:, ts * 1024: ts * 1024 + 1024]
                    W = wb_[:, ts * 1024: ts * 1024 + 1024]
                    TSC = tsc[:, 0:1024]
                    tw = t2slot.acquire(pi)

                    P.op("vector", lambda e, Lv=Lv, M1=M1: e.reduce_max(M1, Lv, axis=AX.X),
                         waits=[("rev", revv)] + tw, inc="dself")
                    sv = P.c("dself")
                    P.op("vector", lambda e, L=L, M1=M1, GE=GE: e.tensor_tensor(
                        GE.rearrange("p (g e) -> p g e", e=8),
                        L.rearrange("p (g e) -> p g e", e=8),
                        M1.unsqueeze(2).broadcast_to([128, 128, 8]), ALU.is_ge),
                        waits=[("dself", sv)])
                    P.op("vector", lambda e, GE=GE, TSC=TSC: e.tensor_scalar(
                        TSC, GE, -BIG, None, ALU.mult))
                    P.op("vector", lambda e, L=L, TSC=TSC, LM=LM: e.tensor_tensor(
                        LM, TSC, L, ALU.add), inc="dself")
                    sv = P.c("dself")
                    P.op("vector", lambda e, LM=LM, M2=M2: e.reduce_max(
                        M2, LM.rearrange("p (g e) -> p g e", e=8), axis=AX.X),
                        waits=[("dself", sv)], inc="dself")
                    sv = P.c("dself")
                    P.op("vector", lambda e, M1=M1, M2=M2, S12=S12: e.tensor_tensor(
                        S12, M1, M2, ALU.add),
                        waits=[("dself", sv)], inc="dself")
                    sv = P.c("dself")
                    P.op("vector", lambda e, L=L, TSC=TSC: e.tensor_scalar(
                        TSC, L, 2.0, None, ALU.mult),
                        waits=[("dself", sv)])
                    P.opd("vector", lambda e, TSC=TSC, S12=S12, U=U: e.tensor_tensor(
                        U.rearrange("p (g e) -> p g e", e=8),
                        TSC.rearrange("p (g e) -> p g e", e=8),
                        S12.unsqueeze(2).broadcast_to([128, 128, 8]), ALU.subtract),
                        inc="du")
                    duv = P.c("du")
                    P.opd("vector", lambda e, L=L, M2=M2, GE=GE: e.tensor_tensor(
                        GE.rearrange("p (g e) -> p g e", e=8),
                        L.rearrange("p (g e) -> p g e", e=8),
                        M2.unsqueeze(2).broadcast_to([128, 128, 8]), ALU.is_ge),
                        inc="dge2")
                    gev = P.c("dge2")
                    P.opd("scalar", lambda e, U=U, SG=SG: e.activation(SG, U, AF.Sigmoid),
                          waits=[("du", duv)], inc="asig")
                    sgv = P.c("asig")
                    P.op("gpsimd", lambda e, SG=SG, GE=GE, W=W: e.tensor_tensor(
                        W, SG, GE, ALU.mult),
                        waits=[("asig", sgv), ("dge2", gev)])
                    P.opd("gpsimd", lambda e, W=W, tcp=(tc // 2): e.tensor_tensor(
                        cw[:, tcp * 1024:(tcp + 1) * 1024],
                        cw[:, tcp * 1024:(tcp + 1) * 1024], W, ALU.add),
                        inc="pw", amt=2)
                    pwv = P.c("pw")
                    if debug & 16 and pi == 0:
                        P.op("gpsimd", lambda e, W=W: e.dma_start(
                            out=dbg["t2"][:, 0:1024], in_=W), waits=[("pw", pwv)],
                            inc="dbgp", amt=16)
                        P.op("gpsimd", lambda e, M1=M1: e.dma_start(
                            out=dbg["t2"][:, 1024:1152], in_=M1), inc="dbgp", amt=16)
                        P.op("gpsimd", lambda e, M2=M2: e.dma_start(
                            out=dbg["t2"][:, 1152:1280], in_=M2), inc="dbgp", amt=16)
                        P.op("gpsimd", lambda e, U=U: e.dma_start(
                            out=dbg["t2"][:, 1280:1536], in_=U[:, 0:256]), inc="dbgp", amt=16)
                        t2_hold = P.c("dbgp")
                    t2slot.release(pi, "pw", pwv)
                    lbslot.release(pi, "pw", pwv)
                    if dbg_lb_hold is not None:
                        lbslot.release(pi, "dbgp", dbg_lb_hold)
                    if debug & 16 and pi == 0:
                        t2slot.release(pi, "dbgp", t2_hold)
            wslslot.release(ct, "rmm", P.c("rmm"))
            if ct + 2 < NCT:
                load_router_slab(ct + 2)
            if ct == 3:
                P.op("gpsimd", lambda e: e.collective_compute(
                    "ReduceScatter", ALU.add,
                    replica_groups=[[0, 1, 2, 3], [4, 5, 6, 7]],
                    ins=[rs_in.ap().opt()], outs=[rs_out_d.ap().opt()]),
                    waits=[("rsst", RSST_ALL)], inc="cc")
        RMM_ALL = P.c("rmm")
        assert RMM_ALL == NCT * OCH
        CC_V = P.c("cc")
        PW_ALL = P.c("pw")
        for tcp in range(4):
            for hh in range(2):
                P.op("vector", lambda e, tcp=tcp, hh=hh: e.reduce_sum(
                    rtmp[:, hh * 8: hh * 8 + 8],
                    cw.ap().rearrange("p (q s g e) -> p q s e g", q=4, s=2, e=8)[:, tcp, hh],
                    axis=AX.X),
                    waits=[("pw", PW_ALL)], inc="dself")
                sv = P.c("dself")
                P.op("vector", lambda e, tcp=tcp, hh=hh: e.tensor_tensor(
                    c_all[:, (tcp * 2 + hh) * 8: (tcp * 2 + hh) * 8 + 8],
                    c_all[:, (tcp * 2 + hh) * 8: (tcp * 2 + hh) * 8 + 8],
                    rtmp[:, hh * 8: hh * 8 + 8], ALU.add),
                    waits=[("dself", sv), ("vone", VONE_ALL)], inc="lrel")
        LREL_ALL = P.c("lrel")

        if debug & 4:
            P.op("gpsimd", lambda e: e.dma_start(out=dbg["c"], in_=c_all[:, :]),
                 waits=[("lrel", LREL_ALL)], inc="dbgs", amt=16)

        P.op("vector", lambda e: e.memset(cT[:, :], 0.0), waits=R1GATE)
        P.op("vector", lambda e: e.memset(cT[64:65, :], 1.0))
        P.op("vector", lambda e: e.tensor_copy(cbf[:, :], c_all[:, :]),
             waits=[("lrel", LREL_ALL)], inc="dcb")
        DCB_V = P.c("dcb")
        REV_ALL = P.c("rev")
        ctslot = Slots(2)
        for tc in range(OCH):
            P.op("tensor", lambda e, tc=tc: e.transpose(
                SBb16[0:8, (tc % 2) * 1024:(tc % 2) * 1024 + 128],
                cbf[:, tc * 8: tc * 8 + 8], identb[:, :]),
                waits=[("dcb", DCB_V), ("rev", REV_ALL)] + ctslot.acquire(tc), inc="tcT")
            tv = P.c("tcT")
            P.op("vector", lambda e, tc=tc: e.tensor_copy(
                cT[0:8, tc * 128:(tc + 1) * 128],
                SBb.ap().bitcast(FP16)[0:8, (tc % 2) * 1024:(tc % 2) * 1024 + 128]),
                waits=[("tcT", tv)], inc="dcT")
            ctslot.release(tc, "dcT", P.c("dcT"))
        DCT_ALL = P.c("dcT")

        for ot in range(4):
            for tc in range(OCH):
                bank = BANKS[tc]
                bw = p3bank.acquire(tc)
                P.op("tensor", lambda e, bank=bank, tc=tc, ot=ot: e.matmul(
                    bank, cT[:, tc * 128:(tc + 1) * 128],
                    ebpb[:, ot * 512:(ot + 1) * 512],
                    start=True, stop=True),
                    waits=[("dcT", DCT_ALL), ("su2", SU2_V)] + bw,
                    inc="emm")
                emv = P.c("emm")
                P.op("vector", lambda e, bank=bank, tc=tc, ot=ot: e.tensor_tensor(
                    acc[:, tc * D + ot * 512: tc * D + ot * 512 + 512],
                    acc[:, tc * D + ot * 512: tc * D + ot * 512 + 512],
                    bank, ALU.add),
                    waits=[("emm", emv)], inc="eev")
                p3bank.release(tc, "eev", P.c("eev"))

        def load_expert_slab(s):
            ei, ot = s // 4, s % 4
            G = NCT + s
            slw = wslslot.acquire(G)
            P.op("gpsimd", lambda e, ei=ei, ot=ot, G=G: e.dma_start(
                out=wsl.ap().rearrange("p (s c n) -> p s c n", s=2, n=512)[:, G % 2, 0:16],
                in_=weT[ei, :, ot * 512:(ot + 1) * 512].rearrange("(c p) n -> p c n", p=128)),
                waits=slw, inc="wsl", amt=16)
            slab_ready[G] = P.c("wsl")

        load_expert_slab(0)
        load_expert_slab(1)
        for s in range(32):
            ei, ot = s // 4, s % 4
            G = NCT + s
            slv = slab_ready[G]
            for tc in range(OCH):
                bank = BANKS[tc]
                bw = p3bank.acquire(tc)
                for dk in range(16):
                    P.op("tensor", lambda e, bank=bank, dk=dk, G=G, tc=tc: e.matmul(
                        bank,
                        xnfo[:, dk * 1024 + tc * 128: dk * 1024 + tc * 128 + 128],
                        wsl[:, (G % 2) * SLABW + dk * 512:(G % 2) * SLABW + dk * 512 + 512],
                        start=(dk == 0), stop=(dk == 15)),
                        waits=([("wsl", slv)] + bw) if dk == 0 else [],
                        inc="emm" if dk == 15 else None)
                emv = P.c("emm")
                P.op("vector", lambda e, bank=bank, tc=tc, ot=ot, ei=ei: e.scalar_tensor_tensor(
                    acc[:, tc * D + ot * 512: tc * D + ot * 512 + 512],
                    bank, c_all[:, tc * 8 + ei: tc * 8 + ei + 1],
                    acc[:, tc * D + ot * 512: tc * D + ot * 512 + 512],
                    ALU.mult, ALU.add),
                    waits=[("emm", emv)], inc="eev")
                p3bank.release(tc, "eev", P.c("eev"))
            wslslot.release(G, "emm", P.c("emm"))
            if s + 2 < 32:
                load_expert_slab(s + 2)
        EEV_ALL = P.c("eev")
        assert EEV_ALL == 32 + 256

        if debug & 4:
            P.op("gpsimd", lambda e: e.dma_start(out=dbg["acc"], in_=acc[:, :]),
                 waits=[("eev", EEV_ALL)], inc="dbgs", amt=16)
        if debug & 2:
            P.op("gpsimd", lambda e: e.dma_start(out=dbg["rs"], in_=rs_out_d.ap()),
                 waits=[("cc", CC_V)], inc="dbgs", amt=16)

        if phases < 4:
            P.op("sync", lambda e: e.dma_start(out=out, in_=xo),
                 waits=[("eev", EEV_ALL)], inc="outd", amt=16)
            _emit(nc, P, [("outd", P.c("outd")), ("dbgs", P.c("dbgs")), ("cc", CC_V)])
            _close_all()
            return nc

        p3_scope.close()
        p4_scope = ExitStack()
        inner_scopes.append(p4_scope)
        rso = p4_scope.enter_context(nc.sbuf_tensor("rso", [128, 2 * D], FP16))
        fout = p4_scope.enter_context(nc.sbuf_tensor("fout", [128, 2 * D], F32))
        rsoslot = Slots(2)
        for tc in range(OCH):
            sl = tc % 2
            P.op("sync", lambda e, tc=tc, sl=sl: e.dma_start(
                out=rso[:, sl * D:(sl + 1) * D],
                in_=rs_out_d.ap()[tc * 128:(tc + 1) * 128, :]),
                waits=[("cc", CC_V), ("eev", EEV_ALL)] + rsoslot.acquire(tc),
                inc="rsod", amt=16)
            rv = P.c("rsod")
            P.opd("vector", lambda e, tc=tc, sl=sl: e.tensor_tensor(
                fout[:, sl * D:(sl + 1) * D], rso[:, sl * D:(sl + 1) * D],
                acc[:, tc * D:(tc + 1) * D], ALU.add),
                waits=[("rsod", rv), ("eev", EEV_ALL)], inc="fev")
            fv = P.c("fev")
            P.op("sync", lambda e, tc=tc, sl=sl: e.dma_start(
                out=out[tc * 128:(tc + 1) * 128, :], in_=fout[:, sl * D:(sl + 1) * D]),
                waits=[("fev", fv)], inc="outd", amt=16)
            rsoslot.release(tc, "outd", P.c("outd"))
        OUTD_ALL = P.c("outd")
        assert OUTD_ALL == OCH * 16

        _emit(nc, P, [("outd", OUTD_ALL), ("dbgs", P.c("dbgs")), ("cc", CC_V)])
        _close_all()
    return nc


def _emit(nc, P, final_waits):
    global LAST_PROG
    LAST_PROG = (P, final_waits)
    with nc.Block() as block:
        @block.tensor
        def _(e):
            P.run("tensor", e)

        @block.vector
        def _(e):
            P.run("vector", e)

        @block.scalar
        def _(e):
            P.run("scalar", e)

        @block.gpsimd
        def _(e):
            P.run("gpsimd", e)

        @block.sync
        def _(e):
            P.run("sync", e, final_waits=final_waits)



def _mk_masks():
    m = np.zeros((128, 4 * 512), dtype=np.float32)
    f = np.arange(512)
    for c in range(4):
        m[:, c * 512:(c + 1) * 512] = (f[None, :] >= (c * 128 + np.arange(128))[:, None])
    return m


def kernel(L0_attn_x, L0_attn_w_qkv, L0_attn_b_qkv, L0_attn_w_out, L0_attn_b_out,
           L0_ffn_router_w, L0_ffn_router_b, L0_ffn_expert_w, L0_ffn_expert_b):
    x = np.ascontiguousarray(np.asarray(L0_attn_x, dtype=np.float32))
    w_qkv = np.asarray(L0_attn_w_qkv, dtype=np.float32)
    b_qkv = np.asarray(L0_attn_b_qkv, dtype=np.float32)
    w_out = np.asarray(L0_attn_w_out, dtype=np.float32)
    b_out = np.asarray(L0_attn_b_out, dtype=np.float32)
    router_w = np.asarray(L0_ffn_router_w, dtype=np.float32)
    router_b = np.asarray(L0_ffn_router_b, dtype=np.float32)
    expert_w = np.asarray(L0_ffn_expert_w, dtype=np.float32)
    expert_b = np.asarray(L0_ffn_expert_b, dtype=np.float32)

    wrT = np.concatenate([router_w.T, router_b[None, :],
                          np.zeros((127, E * D), dtype=np.float32)], axis=0)
    wrT = np.ascontiguousarray(wrT)
    weT = np.ascontiguousarray(expert_w.transpose(0, 2, 1))
    ebp = np.zeros((128, D), dtype=np.float32)
    ebp[0:8] = expert_b
    ebp[64] = b_out
    woT = np.ascontiguousarray(w_out.T)
    masks = _mk_masks()
    ident = np.eye(128, dtype=np.float32)
    consts = np.full((128, 1), -CEXP, dtype=np.float32)

    in_maps = []
    for core in range(8):
        b, r = core // 4, core % 4
        hs0 = r * HPC
        q_rows = w_qkv[hs0 * HD:(hs0 + HPC) * HD]
        k_rows = w_qkv[2048 + hs0 * HD: 2048 + (hs0 + HPC) * HD]
        v_rows = w_qkv[4096 + hs0 * HD: 4096 + (hs0 + HPC) * HD]
        wcat = np.concatenate([q_rows, k_rows, v_rows], axis=0)
        bvpad = np.zeros((128, 1536), dtype=np.float32)
        bvpad[0, 1024:1536] = b_qkv[4096 + hs0 * HD: 4096 + (hs0 + HPC) * HD]
        wqkvT = np.ascontiguousarray(np.concatenate([wcat.T, bvpad], axis=0))
        bq = b_qkv[hs0 * HD:(hs0 + HPC) * HD].reshape(4, 128).T
        bk = b_qkv[2048 + hs0 * HD: 2048 + (hs0 + HPC) * HD].reshape(4, 128).T
        bqkv_fm = np.ascontiguousarray(np.concatenate([bq, bk], axis=1))
        woTs = np.ascontiguousarray(woT[hs0 * HD:(hs0 + HPC) * HD, :])
        xb_c = np.ascontiguousarray(x[b])
        xo_c = np.ascontiguousarray(x[b, r * 1024:(r + 1) * 1024])
        in_maps.append(dict(
            xb=xb_c, xo=xo_c, wqkvT=wqkvT, bqkv_fm=bqkv_fm,
            woTs=woTs, wrT=wrT, weT=weT, ebp=ebp,
            masks=masks, ident=ident, consts=consts,
        ))

    nc = build(debug=DEBUG, phases=PHASES)
    res = run_bass_kernel_spmd(nc, in_maps, core_ids=list(range(8)))

    full = np.empty((2, SEQ, D), dtype=np.float32)
    for core in range(8):
        b, r = core // 4, core % 4
        full[b, r * 1024:(r + 1) * 1024] = res.results[core]["out"]
    kernel.last_results = res
    return full


# version 09
# speedup vs baseline: 1.0429x; 1.0429x over previous
import math
from contextlib import ExitStack

import numpy as np
import ml_dtypes

import concourse.bass as bass
import concourse.mybir as mybir
from concourse.bass_utils import run_bass_kernel_spmd

F32 = mybir.dt.float32
BF16 = mybir.dt.bfloat16
FP16 = mybir.dt.float16
AF = mybir.ActivationFunctionType
ALU = mybir.AluOpType
AX = mybir.AxisListType

D = 2048
SEQ = 4096
HD = 128
HPC = 4
E = 8
NCH = 32
NG = 8
OCH = 8
NCT = 32
CEXP = 12.0
BIG = 60000.0
EPS = 1e-5
ISCALE = 1.0 / math.sqrt(HD)

ENGS = ("tensor", "vector", "scalar", "gpsimd", "sync")

PHASES = 4
LAST_PROG = None
NO_SMALLK = False
DEBUG = 0


class Prog:
    def __init__(self, nc, ctx):
        self.nc = nc
        self.q = {e: [] for e in ENGS}
        self.cnt = {}
        self.sems = {}
        self.ctx = ctx

    def sem(self, name):
        h = self.ctx.enter_context(self.nc.semaphore(name))
        self.sems[name] = h
        self.cnt[name] = 0

    def op(self, eng, fn, waits=(), inc=None, amt=1):
        ws = tuple((s, v) for (s, v) in waits if s is not None and v and v > 0)
        if inc is not None:
            self.cnt[inc] += amt
        self.q[eng].append((ws, fn, inc, amt))
        return self.cnt[inc] if inc is not None else None

    def c(self, sem):
        return self.cnt[sem]

    def opd(self, eng, fn, waits=(), inc=None, amt=1):
        self.op(eng, fn, waits=waits)
        return self.op(eng, lambda e: e.drain(), inc=inc, amt=amt)

    def run(self, eng, e, final_waits=()):
        seen = {}
        for ws, fn, inc, amt in self.q[eng]:
            for s, v in ws:
                if seen.get(s, 0) >= v:
                    continue
                seen[s] = v
                e.wait_ge(self.sems[s], v)
            ins = fn(e)
            if inc is not None:
                ins.then_inc(self.sems[inc], amt)
        for s, v in final_waits:
            if v > 0 and seen.get(s, 0) < v:
                e.wait_ge(self.sems[s], v)


class Slots:
    def __init__(self, n):
        self.n = n
        self.free_at = [None] * n

    def acquire(self, i):
        w = self.free_at[i % self.n]
        return [w] if w is not None else []

    def release(self, i, sem, val):
        self.free_at[i % self.n] = (sem, val)


def build(debug=0, phases=4, no_smallk=None):
    if no_smallk is None:
        no_smallk = NO_SMALLK
    nc = bass.Bass("TRN2", target_bir_lowering=False, debug=False)

    xb = nc.dram_tensor("xb", [SEQ, D], F32, kind="ExternalInput").ap()
    xo = nc.dram_tensor("xo", [1024, D], F32, kind="ExternalInput").ap()
    wqkvT = nc.dram_tensor("wqkvT", [D + 128, 1536], FP16, kind="ExternalInput").ap()
    bqkv_fm = nc.dram_tensor("bqkv_fm", [128, 8], F32, kind="ExternalInput").ap()
    woTs = nc.dram_tensor("woTs", [512, D], FP16, kind="ExternalInput").ap()
    wrT = nc.dram_tensor("wrT", [D + 128, E * D], FP16, kind="ExternalInput").ap()
    weT = nc.dram_tensor("weT", [E, D, D], FP16, kind="ExternalInput").ap()
    ebp_i = nc.dram_tensor("ebp", [128, D], F32, kind="ExternalInput").ap()
    masks_i = nc.dram_tensor("masks", [128, 4 * 512], F32, kind="ExternalInput").ap()
    ident_i = nc.dram_tensor("ident", [128, 128], F32, kind="ExternalInput").ap()
    consts_i = nc.dram_tensor("consts", [128, 1], F32, kind="ExternalInput").ap()

    out = nc.dram_tensor("out", [1024, D], F32, kind="ExternalOutput").ap()

    attn_tm = nc.dram_tensor("attn_tm", [SEQ, HPC * HD], FP16)
    xn_own = nc.dram_tensor("xn_own", [1024, D], FP16)
    rs_in = nc.dram_tensor("rs_in", [SEQ, D], FP16)
    rs_out_d = nc.dram_tensor("rs_out", [1024, D], FP16)

    dbg = {}
    if debug & 1:
        dbg["q"] = nc.dram_tensor("q_dbg", [512, SEQ], F32, kind="ExternalOutput").ap()
        dbg["k"] = nc.dram_tensor("k_dbg", [512, SEQ], F32, kind="ExternalOutput").ap()
        dbg["v"] = nc.dram_tensor("v_dbg", [512, 32 * 129], F32, kind="ExternalOutput").ap()
    if debug & 2:
        dbg["attn"] = nc.dram_tensor("attn_dbg", [SEQ, HPC * HD], F32, kind="ExternalOutput").ap()
        dbg["rs"] = nc.dram_tensor("rs_dbg", [1024, D], F32, kind="ExternalOutput").ap()
    if debug & 16:
        dbg["lg"] = nc.dram_tensor("lg_dbg", [128, 1024], F32, kind="ExternalOutput").ap()
        dbg["t2"] = nc.dram_tensor("t2_dbg", [128, 1536], F32, kind="ExternalOutput").ap()
    if debug & 8:
        dbg["p"] = nc.dram_tensor("p_dbg", [128, 1024], F32, kind="ExternalOutput").ap()
        dbg["l"] = nc.dram_tensor("l_dbg", [128, 8], F32, kind="ExternalOutput").ap()
        dbg["s"] = nc.dram_tensor("s_dbg", [128, 1024], F32, kind="ExternalOutput").ap()
        dbg["av"] = nc.dram_tensor("av_dbg", [128, 1024], F32, kind="ExternalOutput").ap()
    if debug & 4:
        dbg["c"] = nc.dram_tensor("c_dbg", [128, 64], F32, kind="ExternalOutput").ap()
        dbg["acc"] = nc.dram_tensor("acc_dbg", [128, OCH * D], F32, kind="ExternalOutput").ap()

    ctx = ExitStack()
    inner_scopes = []

    def _close_all():
        for sc in reversed(inner_scopes):
            sc.close()

    with ctx:
        P = Prog(nc, ctx)
        SBP = lambda n, s, d: ctx.enter_context(nc.sbuf_tensor(n, s, d))

        identb = SBP("identb", [128, 128], FP16)
        onesM = SBP("onesM", [128, 128], FP16)
        bqv = SBP("bqv", [128, 8], F32)
        negC = SBP("negC", [128, 1], F32)
        c_all = SBP("c_all", [128, 64], F32)
        cbf = SBP("cbf", [128, 64], FP16)
        recs = SBP("recs", [128, 2], F32)
        if debug & 8:
            lsave = SBP("lsave", [128, 8], F32)
            ssave = SBP("ssave", [128, 1024], F32)

        qkv_scope = ExitStack()
        inner_scopes.append(qkv_scope)
        SBq = lambda n, s, d: qkv_scope.enter_context(nc.sbuf_tensor(n, s, d))
        Qh = [SBq(f"Qh{h}", [128, SEQ], FP16) for h in range(HPC)]
        Kh = [SBq(f"Kh{h}", [128, SEQ], FP16) for h in range(HPC)]
        Vh = [SBq(f"Vh{h}", [128, 32 * 129], FP16) for h in range(HPC)]

        p1_scope = ExitStack()
        inner_scopes.append(p1_scope)
        SB1 = lambda n, s, d: p1_scope.enter_context(nc.sbuf_tensor(n, s, d, side="right"))
        x_sl = SB1("x_sl", [128, 2 * D], F32)
        stat1 = SB1("stat1", [128, 2 * 24], F32)
        mv1 = SB1("mv1", [128, 2 * 2], F32)
        sm1 = SB1("sm1", [128, 2 * 4], F32)
        xn_tm = SB1("xn_tm", [128, 2 * D], FP16)
        wqT = SB1("wqT", [128, 17 * 1536], FP16)
        xnfm = SB1("xnfm", [128, 2 * 8192], FP16)

        SAp = ctx.enter_context(nc.psum_tensor("SA", [128, 1024], F32))
        SBb = ctx.enter_context(nc.psum_tensor("SBb", [128, 1024], F32))
        B4 = [ctx.enter_context(nc.psum_tensor(f"B{i}", [128, 512], F32)) for i in range(4)]
        SAb16 = SAp.ap().bitcast(FP16)
        SBb16 = SBb.ap().bitcast(FP16)
        BANKS = [SAp[:, 0:512], SAp[:, 512:1024], SBb[:, 0:512], SBb[:, 512:1024],
                 B4[0][:, :], B4[1][:, :], B4[2][:, :], B4[3][:, :]]

        for s in ("su", "sx0", "sx1", "dxn", "arstd", "trc", "cpc", "mmq", "evq",
                  "vone", "smm", "sexp", "smsk", "avp", "ep", "sto", "su2",
                  "sxo", "dxo", "aro", "xns", "trx",
                  "p25", "wsl", "omm", "oev", "rsst", "cc",
                  "rmm", "rev", "lrel", "du", "asig", "dge2", "pw", "dpw", "dself",
                  "dcb", "tcT", "dcT", "emm", "eev",
                  "fev", "outd", "dbgs", "rsod", "dbgp"):
            P.sem(s)

        P.op("gpsimd", lambda e: e.dma_start(out=identb[:, :], in_=ident_i), inc="su", amt=16)
        P.op("sync", lambda e: e.dma_start(out=bqv[:, :], in_=bqkv_fm), inc="su", amt=16)
        P.op("sync", lambda e: e.dma_start(out=negC[:, :], in_=consts_i), inc="su", amt=16)
        P.op("gpsimd", lambda e: e.dma_start(
            out=wqT.ap().rearrange("p (c n) -> p c n", n=1536),
            in_=wqkvT.rearrange("(c p) n -> p c n", p=128)), inc="su", amt=16)
        SU_ALL = P.c("su")

        for h in range(HPC):
            P.op("vector", lambda e, h=h: e.memset(
                Vh[h].ap().rearrange("p (k c) -> p k c", c=129)[:, :, 128:129], 1.0),
                inc="vone")
        P.op("vector", lambda e: e.memset(c_all[:, :], 0.0), inc="vone")
        P.op("vector", lambda e: e.memset(onesM[:, :], 1.0), inc="vone")
        VONE_ALL = P.c("vone")

        xslot = Slots(2)
        xnslot = Slots(2)
        trslot = Slots(2)
        fmslot = Slots(2)
        qkv_bank = Slots(4)

        for ch in range(NCH):
            g, sub = ch // 4, ch % 4
            sl = ch % 2
            xsem = "sx0" if sl == 0 else "sx1"
            P.op("sync", lambda e, ch=ch, sl=sl: e.dma_start(
                out=x_sl[:, sl * D:(sl + 1) * D], in_=xb[ch * 128:(ch + 1) * 128, :]),
                waits=xslot.acquire(ch), inc=xsem, amt=16)
            xv = P.c(xsem)

            def ln_stats(e, sl=sl):
                for i in range(4):
                    ins = e.bn_stats(stat1[:, sl * 24 + 6 * i: sl * 24 + 6 * i + 6],
                                     x_sl[:, sl * D + 512 * i: sl * D + 512 * i + 512])
                return ins
            P.op("vector", ln_stats, waits=[(xsem, xv)], inc="dself")
            sv = P.c("dself")
            P.op("vector", lambda e, sl=sl: e.bn_aggr(
                mv1[:, sl * 2: sl * 2 + 2], stat1[:, sl * 24: sl * 24 + 24]),
                waits=[("dself", sv)], inc="dself")
            sv = P.c("dself")
            P.op("vector", lambda e, sl=sl: e.tensor_scalar(
                sm1[:, sl * 4: sl * 4 + 1], mv1[:, sl * 2: sl * 2 + 1], -1.0, None, ALU.mult),
                waits=[("dself", sv)])
            P.op("vector", lambda e, sl=sl: e.tensor_scalar(
                sm1[:, sl * 4 + 1: sl * 4 + 2], mv1[:, sl * 2 + 1: sl * 2 + 2], EPS, None, ALU.add),
                inc="dself")
            sv = P.c("dself")
            P.op("scalar", lambda e, sl=sl: e.activation(
                sm1[:, sl * 4 + 2: sl * 4 + 3], sm1[:, sl * 4 + 1: sl * 4 + 2], AF.Ln),
                waits=[("dself", sv)])
            P.op("scalar", lambda e, sl=sl: e.activation(
                sm1[:, sl * 4 + 3: sl * 4 + 4], sm1[:, sl * 4 + 2: sl * 4 + 3], AF.Exp, scale=-0.5),
                inc="arstd")
            av = P.c("arstd")
            P.op("vector", lambda e, sl=sl: e.tensor_scalar(
                xn_tm[:, sl * D:(sl + 1) * D], x_sl[:, sl * D:(sl + 1) * D],
                sm1[:, sl * 4: sl * 4 + 1], sm1[:, sl * 4 + 3: sl * 4 + 4],
                ALU.add, ALU.mult),
                waits=[("arstd", av)] + xnslot.acquire(ch), inc="dxn")
            xslot.release(ch, "dxn", P.c("dxn"))
            dv = P.c("dxn")

            fw = fmslot.acquire(g) if sub == 0 else []
            for dk in range(16):
                gj = ch * 16 + dk
                P.op("tensor", lambda e, sl=sl, dk=dk, gj=gj: e.transpose(
                    SAb16[:, (gj % 2) * 1024:(gj % 2) * 1024 + 128],
                    xn_tm[:, sl * D + dk * 128: sl * D + dk * 128 + 128], identb[:, :]),
                    waits=[("dxn", dv), ("su", SU_ALL)] + trslot.acquire(gj), inc="trc")
                tv = P.c("trc")
                P.op("vector", lambda e, g=g, sub=sub, dk=dk, gj=gj: e.tensor_copy(
                    xnfm[:, (g % 2) * 8192 + dk * 512 + sub * 128:
                         (g % 2) * 8192 + dk * 512 + sub * 128 + 128],
                    SAb16[:, (gj % 2) * 1024:(gj % 2) * 1024 + 128]),
                    waits=[("trc", tv)] + (fw if dk == 0 else []), inc="cpc")
                trslot.release(gj, "cpc", P.c("cpc"))
            xnslot.release(ch, "trc", P.c("trc"))

            if sub == 3:
                cpv = P.c("cpc")
                for part in range(12):
                    bank = B4[part % 4]
                    bw = qkv_bank.acquire(part)
                    kind, p = part // 4, part % 4
                    for dk in range(16):
                        if kind < 2:
                            col0 = kind * 512 + p * 128
                            P.op("tensor", lambda e, bank=bank, dk=dk, col0=col0, g=g: e.matmul(
                                bank[:, :],
                                wqT[:, dk * 1536 + col0: dk * 1536 + col0 + 128],
                                xnfm[:, (g % 2) * 8192 + dk * 512:(g % 2) * 8192 + dk * 512 + 512],
                                start=(dk == 0), stop=(dk == 15)),
                                waits=([("cpc", cpv)] + bw) if dk == 0 else [],
                                inc="mmq" if dk == 15 else None)
                        else:
                            P.op("tensor", lambda e, bank=bank, dk=dk, p=p, g=g: e.matmul(
                                bank[:, :],
                                xnfm[:, (g % 2) * 8192 + dk * 512 + p * 128:
                                     (g % 2) * 8192 + dk * 512 + p * 128 + 128],
                                wqT[:, dk * 1536 + 1024: dk * 1536 + 1536],
                                start=(dk == 0), stop=False),
                                waits=([("cpc", cpv)] + bw) if dk == 0 else [])
                    if kind == 2:
                        P.op("tensor", lambda e, bank=bank: e.matmul(
                            bank[:, :], onesM[:, :],
                            wqT[:, 16 * 1536 + 1024: 16 * 1536 + 1536],
                            start=False, stop=True),
                            waits=[("vone", VONE_ALL)],
                            inc="mmq")
                    mv_ = P.c("mmq")
                    if kind < 2:
                        dst = (Qh if kind == 0 else Kh)[p]
                        bcol = kind * 4 + p
                        P.op("scalar", lambda e, dst=dst, g=g, bank=bank, bcol=bcol: e.activation(
                            dst[:, g * 512:(g + 1) * 512], bank[:, :], AF.Identity,
                            bias=bqv[:, bcol:bcol + 1], scale=1.0),
                            waits=[("mmq", mv_), ("su", SU_ALL)], inc="evq")
                    else:
                        kc = g * 4 + p
                        for h in range(HPC):
                            P.op("vector", lambda e, h=h, kc=kc, bank=bank: e.tensor_copy(
                                Vh[h][:, kc * 129: kc * 129 + 128],
                                bank[:, h * 128:(h + 1) * 128]),
                                waits=[("mmq", mv_)] if h == 0 else [],
                                inc="evq" if h == 3 else None)
                    qkv_bank.release(part, "evq", P.c("evq"))
                fmslot.release(g, "mmq", P.c("mmq"))
        EVQ_ALL = P.c("evq")
        assert EVQ_ALL == 96
        MMQ_ALL = P.c("mmq")
        TRC_ALL = P.c("trc")
        DXN_ALL = P.c("dxn")

        if debug & 1:
            for h in range(HPC):
                P.op("gpsimd", lambda e, h=h: e.dma_start(
                    out=dbg["q"][h * 128:(h + 1) * 128, :], in_=Qh[h][:, :]),
                    waits=[("evq", EVQ_ALL)], inc="dbgs", amt=16)
                P.op("gpsimd", lambda e, h=h: e.dma_start(
                    out=dbg["k"][h * 128:(h + 1) * 128, :], in_=Kh[h][:, :]),
                    inc="dbgs", amt=16)
                P.op("gpsimd", lambda e, h=h: e.dma_start(
                    out=dbg["v"][h * 128:(h + 1) * 128, :], in_=Vh[h][:, :]),
                    waits=[("vone", VONE_ALL)], inc="dbgs", amt=16)

        p1_scope.close()
        r1_scope = ExitStack()
        inner_scopes.append(r1_scope)
        SBr1 = lambda n, s, d: r1_scope.enter_context(nc.sbuf_tensor(n, s, d, side="right"))
        maskb = SBr1("maskb", [128, 4 * 512], FP16)
        acc = SBr1("acc", [128, OCH * D], F32)
        Pb = SBr1("Pb", [128, 4 * 1024], FP16)
        stg = SBr1("stg", [128, 4 * 128], FP16)
        cT = SBr1("cT", [128, 1024], FP16)
        xo_tm = SBr1("xo_tm", [128, 2 * D], FP16)
        stat2 = SBr1("stat2", [128, 2 * 24], F32)
        mv2 = SBr1("mv2", [128, 2 * 2], F32)
        sm2 = SBr1("sm2", [128, 2 * 4], F32)

        R1GATE = [("mmq", MMQ_ALL), ("trc", TRC_ALL), ("dxn", DXN_ALL)]
        P.op("gpsimd", lambda e: e.dma_start(out=maskb[:, :], in_=masks_i),
             waits=R1GATE, inc="su2", amt=16)
        MASK_V = P.c("su2")

        if phases < 2:
            P.op("sync", lambda e: e.dma_start(out=out, in_=xo),
                 waits=[("evq", EVQ_ALL)], inc="outd", amt=16)
            _emit(nc, P, [("outd", P.c("outd")), ("dbgs", P.c("dbgs"))])
            _close_all()
            return nc

        for tc in range(OCH):
            P.op("sync", lambda e, tc=tc: e.dma_start(
                out=acc[:, tc * D:(tc + 1) * D], in_=xo[tc * 128:(tc + 1) * 128, :]),
                waits=R1GATE if tc == 0 else [], inc="sxo", amt=16)

        sbank = Slots(2)
        avbank = Slots(4)
        stslot = Slots(4)
        pbslot = Slots(4)
        xoslot = Slots(2)

        def emit_own_ln(tc):
            sl = tc % 2

            def ln2(e, sl=sl, tc=tc):
                for i in range(4):
                    ins = e.bn_stats(stat2[:, sl * 24 + 6 * i: sl * 24 + 6 * i + 6],
                                     acc[:, tc * D + 512 * i: tc * D + 512 * i + 512])
                return ins
            P.op("vector", ln2, waits=[("sxo", 16 * (tc + 1))], inc="dself")
            sv = P.c("dself")
            P.op("vector", lambda e, sl=sl: e.bn_aggr(
                mv2[:, sl * 2: sl * 2 + 2], stat2[:, sl * 24: sl * 24 + 24]),
                waits=[("dself", sv)], inc="dself")
            sv = P.c("dself")
            P.op("vector", lambda e, sl=sl: e.tensor_scalar(
                sm2[:, sl * 4: sl * 4 + 1], mv2[:, sl * 2: sl * 2 + 1], -1.0, None, ALU.mult),
                waits=[("dself", sv)])
            P.op("vector", lambda e, sl=sl: e.tensor_scalar(
                sm2[:, sl * 4 + 1: sl * 4 + 2], mv2[:, sl * 2 + 1: sl * 2 + 2], EPS, None, ALU.add),
                inc="dself")
            sv = P.c("dself")
            P.op("scalar", lambda e, sl=sl: e.activation(
                sm2[:, sl * 4 + 2: sl * 4 + 3], sm2[:, sl * 4 + 1: sl * 4 + 2], AF.Ln),
                waits=[("dself", sv)])
            P.op("scalar", lambda e, sl=sl: e.activation(
                sm2[:, sl * 4 + 3: sl * 4 + 4], sm2[:, sl * 4 + 2: sl * 4 + 3], AF.Exp, scale=-0.5),
                inc="aro")
            av = P.c("aro")
            P.opd("vector", lambda e, sl=sl, tc=tc: e.tensor_scalar(
                xo_tm[:, sl * D:(sl + 1) * D], acc[:, tc * D:(tc + 1) * D],
                sm2[:, sl * 4: sl * 4 + 1], sm2[:, sl * 4 + 3: sl * 4 + 4],
                ALU.add, ALU.mult),
                waits=[("aro", av)] + xoslot.acquire(tc), inc="dxo")
            dv = P.c("dxo")
            P.op("sync", lambda e, tc=tc, sl=sl: e.dma_start(
                out=xn_own.ap()[tc * 128:(tc + 1) * 128, :],
                in_=xo_tm[:, sl * D:(sl + 1) * D]),
                waits=[("dxo", dv)], inc="xns", amt=16)
            xoslot.release(tc, "xns", P.c("xns"))

        pairs = []
        for h in range(HPC):
            for qt in range(NG):
                nkc = 4 * (qt + 1)
                for kp in range(nkc // 2):
                    pairs.append((h, qt, kp, nkc))

        def emit_av(job):
            h, qt, kp, nkc, pidx, sev, mwaits = job
            kc0, kc1 = 2 * kp, 2 * kp + 1
            par = pidx % 4
            for j, kc in enumerate((kc0, kc1)):
                for qs in range(4):
                    first = (kc == 0)
                    lastmm = (kc == nkc - 1)
                    w = []
                    if qs == 0:
                        w = [("sexp", sev)] + mwaits
                    if first and qs == 0:
                        w = w + [("vone", VONE_ALL)]
                    if first:
                        w = w + avbank.acquire(qs)
                    P.op("tensor", lambda e, qs=qs, par=par, j=j, kc=kc, h=h,
                         first=first, lastmm=lastmm: e.matmul(
                             B4[qs][:, 0:129],
                             Pb[:, par * 1024 + j * 512 + qs * 128:
                                par * 1024 + j * 512 + qs * 128 + 128],
                             Vh[h][:, kc * 129: kc * 129 + 129],
                             start=first, stop=lastmm),
                        waits=w,
                        inc="avp" if (j == 1 and qs == 3) else None)
            pbslot.release(pidx, "avp", P.c("avp"))

        def emit_epilogue(h, qt):
            n = h * NG + qt
            apv = P.c("avp")
            for qs in range(4):
                ssl = (n * 4 + qs) % 2
                sidx = (n * 4 + qs) % 4
                P.op("vector", lambda e, qs=qs, ssl=ssl: e.reciprocal(
                    recs[:, ssl:ssl + 1], B4[qs][:, 128:129]),
                    waits=[("avp", apv)], inc="dself")
                sv2 = P.c("dself")
                stw = stslot.acquire(n * 4 + qs)
                P.opd("vector", lambda e, qs=qs, ssl=ssl, sidx=sidx: e.tensor_scalar(
                    stg[:, sidx * 128:(sidx + 1) * 128], B4[qs][:, 0:128],
                    recs[:, ssl:ssl + 1], None, ALU.mult),
                    waits=[("dself", sv2)] + stw, inc="ep")
                epv = P.c("ep")
                avbank.release(qs, "ep", epv)
                P.op("sync", lambda e, h=h, qt=qt, qs=qs, sidx=sidx: e.dma_start(
                    out=attn_tm.ap()[qt * 512 + qs * 128: qt * 512 + qs * 128 + 128,
                                     h * 128:(h + 1) * 128],
                    in_=stg[:, sidx * 128:(sidx + 1) * 128]),
                    waits=[("ep", epv)], inc="sto", amt=16)
                stslot.release(n * 4 + qs, "sto", P.c("sto"))

        pair_idx = 0
        pending = None
        for h, qt, kp, nkc in pairs:
            kc0, kc1 = 2 * kp, 2 * kp + 1
            par = pair_idx % 4
            Sp = SAp if pair_idx % 2 == 0 else SBb
            bw = sbank.acquire(pair_idx)
            for j, kc in enumerate((kc0, kc1)):
                P.op("tensor", lambda e, Sp=Sp, j=j, kc=kc, h=h, qt=qt: e.matmul(
                    Sp[:, j * 512:(j + 1) * 512],
                    Kh[h][:, kc * 128: kc * 128 + 128],
                    Qh[h][:, qt * 512: qt * 512 + 512],
                    start=True, stop=True),
                    waits=([("evq", EVQ_ALL)] + bw) if j == 0 else [],
                    inc="smm" if j == 1 else None)
            smv = P.c("smm")
            pw_ = pbslot.acquire(pair_idx)
            P.op("scalar", lambda e, Sp=Sp, par=par: e.activation(
                Pb[:, par * 1024:(par + 1) * 1024], Sp[:, :], AF.Exp,
                bias=negC[:, :], scale=ISCALE),
                waits=[("smm", smv), ("su", SU_ALL), ("su2", MASK_V)] + pw_,
                inc="sexp")
            sev = P.c("sexp")
            sbank.release(pair_idx, "sexp", sev)
            mwaits = []
            for j, kc in enumerate((kc0, kc1)):
                c = kc - 4 * qt
                if 0 <= c <= 3:
                    P.op("vector", lambda e, par=par, j=j, c=c: e.tensor_tensor(
                        Pb[:, par * 1024 + j * 512: par * 1024 + j * 512 + 512],
                        Pb[:, par * 1024 + j * 512: par * 1024 + j * 512 + 512],
                        maskb[:, c * 512:(c + 1) * 512], ALU.mult),
                        waits=[("sexp", sev), ("su2", MASK_V)], inc="smsk")
                    mwaits = [("smsk", P.c("smsk"))]
            if pending is not None:
                emit_av(pending)
                if (pending[0], pending[1]) != (h, qt):
                    emit_epilogue(pending[0], pending[1])
            pending = (h, qt, kp, nkc, pair_idx, sev, mwaits)
            pair_idx += 1
        emit_av(pending)
        emit_epilogue(pending[0], pending[1])

        if phases >= 2.2:
            for tc_ in range(OCH):
                emit_own_ln(tc_)
        STO_ALL = P.c("sto")
        assert STO_ALL == 32 * 4 * 16
        XNS_ALL = P.c("xns")
        EP_ALL = P.c("ep")
        if debug & 8:
            P.op("gpsimd", lambda e: e.dma_start(out=dbg["l"], in_=lsave[:, :]),
                 waits=[("dbgp", P.c("dbgp")), ("ep", EP_ALL)], inc="dbgs", amt=16)
        if phases < 2.2:
            P.op("sync", lambda e: e.dma_start(out=out, in_=xo),
                 waits=[("sto", STO_ALL), ("ep", EP_ALL)], inc="outd", amt=16)
            _emit(nc, P, [("outd", P.c("outd")), ("dbgs", P.c("dbgs"))])
            _close_all()
            return nc
        assert XNS_ALL == OCH * 16

        qkv_scope.close()
        r2_scope = ExitStack()
        inner_scopes.append(r2_scope)
        xnfo = r2_scope.enter_context(nc.sbuf_tensor("xnfo", [128, 16 * 1024], FP16))
        p25_scope = ExitStack()
        inner_scopes.append(p25_scope)
        SB25 = lambda n, s, d: p25_scope.enter_context(nc.sbuf_tensor(n, s, d))
        attnfm = SB25("attnfm", [128, 4 * SEQ], FP16)
        woslab = SB25("woslab", [128, 4 * D], FP16)
        rsbuf = SB25("rsbuf", [128, 4 * 512], FP16)

        for dk in range(16):
            P.op("sync", lambda e, dk=dk: e.dma_start_transpose(
                out=xnfo[:, dk * 1024:(dk + 1) * 1024],
                in_=xn_own.ap()[:, dk * 128:(dk + 1) * 128]),
                waits=[("xns", XNS_ALL), ("ep", EP_ALL)], inc="trx", amt=16)
        TRX_ALL = P.c("trx")
        if phases < 2.3:
            P.op("sync", lambda e: e.dma_start(out=out, in_=xo),
                 waits=[("sto", STO_ALL), ("trx", TRX_ALL)], inc="outd", amt=16)
            _emit(nc, P, [("outd", P.c("outd")), ("dbgs", P.c("dbgs"))])
            _close_all()
            return nc

        for fk in range(4):
            P.op("sync", lambda e, fk=fk: e.dma_start_transpose(
                out=attnfm[:, fk * SEQ:(fk + 1) * SEQ],
                in_=attn_tm.ap()[:, fk * 128:(fk + 1) * 128]),
                waits=[("sto", STO_ALL), ("ep", EP_ALL)], inc="p25", amt=16)
        P25_ALL = P.c("p25")
        if debug & 2:
            P.op("gpsimd", lambda e: e.dma_start(out=dbg["attn"], in_=attn_tm.ap()),
                 waits=[("sto", STO_ALL)], inc="dbgs", amt=16)

        P.op("gpsimd", lambda e: e.dma_start(
            out=woslab.ap().rearrange("p (c n) -> p c n", n=D),
            in_=woTs.rearrange("(c p) n -> p c n", p=128)),
            waits=[("ep", EP_ALL)], inc="wsl", amt=16)
        WO_V = P.c("wsl")

        obank = avbank
        oslot = Slots(4)
        for tt in range(NCH):
            for ot in range(4):
                oi = tt * 4 + ot
                bank = B4[oi % 4]
                bw = obank.acquire(oi)
                for fk in range(4):
                    P.op("tensor", lambda e, bank=bank, fk=fk, tt=tt, ot=ot: e.matmul(
                        bank[:, :],
                        attnfm[:, fk * SEQ + tt * 128: fk * SEQ + tt * 128 + 128],
                        woslab[:, fk * D + ot * 512: fk * D + ot * 512 + 512],
                        start=(fk == 0), stop=(fk == 3)),
                        waits=([("p25", P25_ALL), ("wsl", WO_V)] + bw) if fk == 0 else [],
                        inc="omm" if fk == 3 else None)
                omv = P.c("omm")
                sw = oslot.acquire(oi)
                if oi % 2 == 0:
                    P.opd("scalar", lambda e, bank=bank, osl=oi % 4: e.activation(
                        rsbuf[:, osl * 512:(osl + 1) * 512], bank[:, :], AF.Identity),
                        waits=[("omm", omv)] + sw, inc="oev")
                else:
                    P.opd("vector", lambda e, bank=bank, osl=oi % 4: e.tensor_copy(
                        rsbuf[:, osl * 512:(osl + 1) * 512], bank[:, :]),
                        waits=[("omm", omv)] + sw, inc="oev")
                oev = P.c("oev")
                obank.release(oi, "oev", oev)
                P.op("sync", lambda e, tt=tt, ot=ot, osl=oi % 4: e.dma_start(
                    out=rs_in.ap()[tt * 128:(tt + 1) * 128, ot * 512:(ot + 1) * 512],
                    in_=rsbuf[:, osl * 512:(osl + 1) * 512]),
                    waits=[("oev", oev)], inc="rsst", amt=16)
                oslot.release(oi, "rsst", P.c("rsst"))
        RSST_ALL = P.c("rsst")
        OMM_ALL = P.c("omm")
        if phases < 2.4:
            P.op("sync", lambda e: e.dma_start(out=out, in_=xo),
                 waits=[("rsst", RSST_ALL)], inc="outd", amt=16)
            _emit(nc, P, [("outd", P.c("outd")), ("dbgs", P.c("dbgs"))])
            _close_all()
            return nc

        if phases < 3:
            P.op("gpsimd", lambda e: e.collective_compute(
                "ReduceScatter", ALU.add,
                replica_groups=[[0, 1, 2, 3], [4, 5, 6, 7]],
                ins=[rs_in.ap().opt()], outs=[rs_out_d.ap().opt()]),
                waits=[("rsst", RSST_ALL)], inc="cc")
            CC_V = P.c("cc")
        if phases < 3:
            if debug & 2:
                P.op("gpsimd", lambda e: e.dma_start(out=dbg["rs"], in_=rs_out_d.ap()),
                     waits=[("cc", CC_V)], inc="dbgs", amt=16)
            P.op("sync", lambda e: e.dma_start(out=out, in_=xo),
                 waits=[("rsst", RSST_ALL)], inc="outd", amt=16)
            _emit(nc, P, [("outd", P.c("outd")), ("dbgs", P.c("dbgs")), ("cc", CC_V)])
            _close_all()
            return nc

        p25_scope.close()
        p3_scope = ExitStack()
        inner_scopes.append(p3_scope)
        SB3 = lambda n, s, d: p3_scope.enter_context(nc.sbuf_tensor(n, s, d))
        SLABW = 17 * 512
        wsl = SB3("wsl", [128, 2 * SLABW], FP16)
        NT2 = 3
        lb = SB3("lb", [128, 3 * 1024], FP16)
        m1b = SB3("m1b", [128, NT2 * 128], FP16)
        m2b = SB3("m2b", [128, NT2 * 128], FP16)
        s12 = SB3("s12", [128, NT2 * 128], FP16)
        ge1b = SB3("ge1b", [128, NT2 * 1024], FP16)
        lmb = SB3("lmb", [128, NT2 * 1024], FP16)
        ub = SB3("ub", [128, NT2 * 1024], FP16)
        sgb = SB3("sgb", [128, NT2 * 1024], FP16)
        wb_ = SB3("wb", [128, NT2 * 1024], FP16)
        tsc = SB3("tsc", [128, 1024], FP16)
        cw = SB3("cw", [128, 4 * 1024], FP16)
        rtmp = SB3("rtmp", [128, 2 * 16], F32)
        ebpb = SB3("ebpb", [128, D], FP16)

        P3GATE = [("omm", OMM_ALL)]
        P.op("gpsimd", lambda e: e.dma_start(out=ebpb[:, :], in_=ebp_i),
             waits=P3GATE, inc="su2", amt=16)
        SU2_V = P.c("su2")
        P.op("gpsimd", lambda e: e.memset(cw[:, :], 0.0), waits=P3GATE, inc="pw")

        p3bank = Slots(8)
        p3bank.free_at[0] = p3bank.free_at[1] = sbank.free_at[0]
        p3bank.free_at[2] = p3bank.free_at[3] = sbank.free_at[1]
        for i in range(4):
            p3bank.free_at[4 + i] = obank.free_at[i]

        wslslot = Slots(2)
        slab_ready = {}

        def load_router_slab(ct):
            slw = wslslot.acquire(ct) + (P3GATE if ct < 2 else [])
            P.op("gpsimd", lambda e, ct=ct: e.dma_start(
                out=wsl.ap().rearrange("p (s c n) -> p s c n", s=2, n=512)[:, ct % 2],
                in_=wrT[:, ct * 512:(ct + 1) * 512].rearrange("(c p) n -> p c n", p=128)),
                waits=slw, inc="wsl", amt=16)
            slab_ready[ct] = P.c("wsl")

        load_router_slab(0)
        load_router_slab(1)
        CC_V = None

        lbslot = Slots(3)
        t2slot = Slots(NT2)
        tile_idx = 0
        pair_cnt = 0

        for ct in range(NCT):
            for tc in range(OCH):
                bank = BANKS[tc]
                bw = p3bank.acquire(tc)
                slv = slab_ready[ct]
                for dk in range(16):
                    P.op("tensor", lambda e, bank=bank, dk=dk, ct=ct, tc=tc: e.matmul(
                        bank,
                        xnfo[:, dk * 1024 + tc * 128: dk * 1024 + tc * 128 + 128],
                        wsl[:, (ct % 2) * SLABW + dk * 512:(ct % 2) * SLABW + dk * 512 + 512],
                        start=(dk == 0), stop=False),
                        waits=([("trx", TRX_ALL), ("wsl", slv)] + bw) if dk == 0 else [])
                P.op("tensor", lambda e, bank=bank, ct=ct: e.matmul(
                    bank, onesM[:, :],
                    wsl[:, (ct % 2) * SLABW + 16 * 512:(ct % 2) * SLABW + 17 * 512],
                    start=False, stop=True),
                    inc="rmm")
                rmv = P.c("rmm")
                half = tile_idx % 2
                psl = (tile_idx // 2) % 3
                lw = lbslot.acquire(tile_idx // 2) if half == 0 else []
                P.opd("scalar", lambda e, bank=bank, psl=psl, half=half: e.activation(
                    lb[:, psl * 1024 + half * 512: psl * 1024 + half * 512 + 512],
                    bank, AF.Identity),
                    waits=[("rmm", rmv)] + lw, inc="rev")
                p3bank.release(tc, "rev", P.c("rev"))
                tile_idx += 1

                if half == 1:
                    pi = pair_cnt
                    pair_cnt += 1
                    revv = P.c("rev")
                    dbg_lb_hold = None
                    if debug & 16 and pi == 0:
                        P.op("gpsimd", lambda e: e.dma_start(
                            out=dbg["lg"], in_=lb[:, 0:1024]),
                            waits=[("rev", 2)], inc="dbgp", amt=16)
                        dbg_lb_hold = P.c("dbgp")
                    ts = pi % NT2
                    L = lb[:, psl * 1024:(psl + 1) * 1024]
                    Lv = lb.ap().rearrange("p (s g e) -> p s g e", s=3, e=8)[:, psl]
                    M1 = m1b[:, ts * 128: ts * 128 + 128]
                    M2 = m2b[:, ts * 128: ts * 128 + 128]
                    S12 = s12[:, ts * 128: ts * 128 + 128]
                    GE = ge1b[:, ts * 1024: ts * 1024 + 1024]
                    LM = lmb[:, ts * 1024: ts * 1024 + 1024]
                    U = ub[:, ts * 1024: ts * 1024 + 1024]
                    SG = sgb[:, ts * 1024: ts * 1024 + 1024]
                    W = wb_[:, ts * 1024: ts * 1024 + 1024]
                    TSC = tsc[:, 0:1024]
                    tw = t2slot.acquire(pi)

                    P.op("vector", lambda e, Lv=Lv, M1=M1: e.reduce_max(M1, Lv, axis=AX.X),
                         waits=[("rev", revv)] + tw, inc="dself")
                    sv = P.c("dself")
                    P.op("vector", lambda e, L=L, M1=M1, GE=GE: e.tensor_tensor(
                        GE.rearrange("p (g e) -> p g e", e=8),
                        L.rearrange("p (g e) -> p g e", e=8),
                        M1.unsqueeze(2).broadcast_to([128, 128, 8]), ALU.is_ge),
                        waits=[("dself", sv)])
                    P.op("vector", lambda e, GE=GE, TSC=TSC: e.tensor_scalar(
                        TSC, GE, -BIG, None, ALU.mult))
                    P.op("vector", lambda e, L=L, TSC=TSC, LM=LM: e.tensor_tensor(
                        LM, TSC, L, ALU.add), inc="dself")
                    sv = P.c("dself")
                    P.op("vector", lambda e, LM=LM, M2=M2: e.reduce_max(
                        M2, LM.rearrange("p (g e) -> p g e", e=8), axis=AX.X),
                        waits=[("dself", sv)], inc="dself")
                    sv = P.c("dself")
                    P.op("vector", lambda e, M1=M1, M2=M2, S12=S12: e.tensor_tensor(
                        S12, M1, M2, ALU.add),
                        waits=[("dself", sv)], inc="dself")
                    sv = P.c("dself")
                    P.op("vector", lambda e, L=L, TSC=TSC: e.tensor_scalar(
                        TSC, L, 2.0, None, ALU.mult),
                        waits=[("dself", sv)])
                    P.opd("vector", lambda e, TSC=TSC, S12=S12, U=U: e.tensor_tensor(
                        U.rearrange("p (g e) -> p g e", e=8),
                        TSC.rearrange("p (g e) -> p g e", e=8),
                        S12.unsqueeze(2).broadcast_to([128, 128, 8]), ALU.subtract),
                        inc="du")
                    duv = P.c("du")
                    P.opd("vector", lambda e, L=L, M2=M2, GE=GE: e.tensor_tensor(
                        GE.rearrange("p (g e) -> p g e", e=8),
                        L.rearrange("p (g e) -> p g e", e=8),
                        M2.unsqueeze(2).broadcast_to([128, 128, 8]), ALU.is_ge),
                        inc="dge2")
                    gev = P.c("dge2")
                    P.opd("scalar", lambda e, U=U, SG=SG: e.activation(SG, U, AF.Sigmoid),
                          waits=[("du", duv)], inc="asig")
                    sgv = P.c("asig")
                    P.op("gpsimd", lambda e, SG=SG, GE=GE, W=W: e.tensor_tensor(
                        W, SG, GE, ALU.mult),
                        waits=[("asig", sgv), ("dge2", gev)])
                    P.opd("gpsimd", lambda e, W=W, tcp=(tc // 2): e.tensor_tensor(
                        cw[:, tcp * 1024:(tcp + 1) * 1024],
                        cw[:, tcp * 1024:(tcp + 1) * 1024], W, ALU.add),
                        inc="pw", amt=2)
                    pwv = P.c("pw")
                    if debug & 16 and pi == 0:
                        P.op("gpsimd", lambda e, W=W: e.dma_start(
                            out=dbg["t2"][:, 0:1024], in_=W), waits=[("pw", pwv)],
                            inc="dbgp", amt=16)
                        P.op("gpsimd", lambda e, M1=M1: e.dma_start(
                            out=dbg["t2"][:, 1024:1152], in_=M1), inc="dbgp", amt=16)
                        P.op("gpsimd", lambda e, M2=M2: e.dma_start(
                            out=dbg["t2"][:, 1152:1280], in_=M2), inc="dbgp", amt=16)
                        P.op("gpsimd", lambda e, U=U: e.dma_start(
                            out=dbg["t2"][:, 1280:1536], in_=U[:, 0:256]), inc="dbgp", amt=16)
                        t2_hold = P.c("dbgp")
                    t2slot.release(pi, "pw", pwv)
                    lbslot.release(pi, "pw", pwv)
                    if dbg_lb_hold is not None:
                        lbslot.release(pi, "dbgp", dbg_lb_hold)
                    if debug & 16 and pi == 0:
                        t2slot.release(pi, "dbgp", t2_hold)
            wslslot.release(ct, "rmm", P.c("rmm"))
            if ct + 2 < NCT:
                load_router_slab(ct + 2)
            if ct == 3:
                P.op("gpsimd", lambda e: e.collective_compute(
                    "ReduceScatter", ALU.add,
                    replica_groups=[[0, 1, 2, 3], [4, 5, 6, 7]],
                    ins=[rs_in.ap().opt()], outs=[rs_out_d.ap().opt()]),
                    waits=[("rsst", RSST_ALL)], inc="cc")
        RMM_ALL = P.c("rmm")
        assert RMM_ALL == NCT * OCH
        CC_V = P.c("cc")
        PW_ALL = P.c("pw")
        for tcp in range(4):
            for hh in range(2):
                P.op("vector", lambda e, tcp=tcp, hh=hh: e.reduce_sum(
                    rtmp[:, hh * 8: hh * 8 + 8],
                    cw.ap().rearrange("p (q s g e) -> p q s e g", q=4, s=2, e=8)[:, tcp, hh],
                    axis=AX.X),
                    waits=[("pw", PW_ALL)], inc="dself")
                sv = P.c("dself")
                P.op("vector", lambda e, tcp=tcp, hh=hh: e.tensor_tensor(
                    c_all[:, (tcp * 2 + hh) * 8: (tcp * 2 + hh) * 8 + 8],
                    c_all[:, (tcp * 2 + hh) * 8: (tcp * 2 + hh) * 8 + 8],
                    rtmp[:, hh * 8: hh * 8 + 8], ALU.add),
                    waits=[("dself", sv), ("vone", VONE_ALL)], inc="lrel")
        LREL_ALL = P.c("lrel")

        if debug & 4:
            P.op("gpsimd", lambda e: e.dma_start(out=dbg["c"], in_=c_all[:, :]),
                 waits=[("lrel", LREL_ALL)], inc="dbgs", amt=16)

        P.op("vector", lambda e: e.memset(cT[:, :], 0.0), waits=R1GATE)
        P.op("vector", lambda e: e.memset(cT[64:65, :], 1.0))
        P.op("vector", lambda e: e.tensor_copy(cbf[:, :], c_all[:, :]),
             waits=[("lrel", LREL_ALL)], inc="dcb")
        DCB_V = P.c("dcb")
        REV_ALL = P.c("rev")
        ctslot = Slots(2)
        for tc in range(OCH):
            P.op("tensor", lambda e, tc=tc: e.transpose(
                SBb16[0:8, (tc % 2) * 1024:(tc % 2) * 1024 + 128],
                cbf[:, tc * 8: tc * 8 + 8], identb[:, :]),
                waits=[("dcb", DCB_V), ("rev", REV_ALL)] + ctslot.acquire(tc), inc="tcT")
            tv = P.c("tcT")
            P.op("vector", lambda e, tc=tc: e.tensor_copy(
                cT[0:8, tc * 128:(tc + 1) * 128],
                SBb.ap().bitcast(FP16)[0:8, (tc % 2) * 1024:(tc % 2) * 1024 + 128]),
                waits=[("tcT", tv)], inc="dcT")
            ctslot.release(tc, "dcT", P.c("dcT"))
        DCT_ALL = P.c("dcT")

        for ot in range(4):
            for tc in range(OCH):
                bank = BANKS[tc]
                bw = p3bank.acquire(tc)
                P.op("tensor", lambda e, bank=bank, tc=tc, ot=ot: e.matmul(
                    bank, cT[:, tc * 128:(tc + 1) * 128],
                    ebpb[:, ot * 512:(ot + 1) * 512],
                    start=True, stop=True),
                    waits=[("dcT", DCT_ALL), ("su2", SU2_V)] + bw,
                    inc="emm")
                emv = P.c("emm")
                P.op("vector", lambda e, bank=bank, tc=tc, ot=ot: e.tensor_tensor(
                    acc[:, tc * D + ot * 512: tc * D + ot * 512 + 512],
                    acc[:, tc * D + ot * 512: tc * D + ot * 512 + 512],
                    bank, ALU.add),
                    waits=[("emm", emv)], inc="eev")
                p3bank.release(tc, "eev", P.c("eev"))

        def load_expert_slab(s):
            ei, ot = s // 4, s % 4
            G = NCT + s
            slw = wslslot.acquire(G)
            P.op("gpsimd", lambda e, ei=ei, ot=ot, G=G: e.dma_start(
                out=wsl.ap().rearrange("p (s c n) -> p s c n", s=2, n=512)[:, G % 2, 0:16],
                in_=weT[ei, :, ot * 512:(ot + 1) * 512].rearrange("(c p) n -> p c n", p=128)),
                waits=slw, inc="wsl", amt=16)
            slab_ready[G] = P.c("wsl")

        load_expert_slab(0)
        load_expert_slab(1)
        for s in range(32):
            ei, ot = s // 4, s % 4
            G = NCT + s
            slv = slab_ready[G]
            for tc in range(OCH):
                bank = BANKS[tc]
                bw = p3bank.acquire(tc)
                for dk in range(16):
                    P.op("tensor", lambda e, bank=bank, dk=dk, G=G, tc=tc: e.matmul(
                        bank,
                        xnfo[:, dk * 1024 + tc * 128: dk * 1024 + tc * 128 + 128],
                        wsl[:, (G % 2) * SLABW + dk * 512:(G % 2) * SLABW + dk * 512 + 512],
                        start=(dk == 0), stop=(dk == 15)),
                        waits=([("wsl", slv)] + bw) if dk == 0 else [],
                        inc="emm" if dk == 15 else None)
                emv = P.c("emm")
                P.op("vector", lambda e, bank=bank, tc=tc, ot=ot, ei=ei: e.scalar_tensor_tensor(
                    acc[:, tc * D + ot * 512: tc * D + ot * 512 + 512],
                    bank, c_all[:, tc * 8 + ei: tc * 8 + ei + 1],
                    acc[:, tc * D + ot * 512: tc * D + ot * 512 + 512],
                    ALU.mult, ALU.add),
                    waits=[("emm", emv)], inc="eev")
                p3bank.release(tc, "eev", P.c("eev"))
            wslslot.release(G, "emm", P.c("emm"))
            if s + 2 < 32:
                load_expert_slab(s + 2)
        EEV_ALL = P.c("eev")
        assert EEV_ALL == 32 + 256

        if debug & 4:
            P.op("gpsimd", lambda e: e.dma_start(out=dbg["acc"], in_=acc[:, :]),
                 waits=[("eev", EEV_ALL)], inc="dbgs", amt=16)
        if debug & 2:
            P.op("gpsimd", lambda e: e.dma_start(out=dbg["rs"], in_=rs_out_d.ap()),
                 waits=[("cc", CC_V)], inc="dbgs", amt=16)

        if phases < 4:
            P.op("sync", lambda e: e.dma_start(out=out, in_=xo),
                 waits=[("eev", EEV_ALL)], inc="outd", amt=16)
            _emit(nc, P, [("outd", P.c("outd")), ("dbgs", P.c("dbgs")), ("cc", CC_V)])
            _close_all()
            return nc

        p3_scope.close()
        p4_scope = ExitStack()
        inner_scopes.append(p4_scope)
        rso = p4_scope.enter_context(nc.sbuf_tensor("rso", [128, 2 * D], FP16))
        fout = p4_scope.enter_context(nc.sbuf_tensor("fout", [128, 2 * D], F32))
        rsoslot = Slots(2)
        for tc in range(OCH):
            sl = tc % 2
            P.op("sync", lambda e, tc=tc, sl=sl: e.dma_start(
                out=rso[:, sl * D:(sl + 1) * D],
                in_=rs_out_d.ap()[tc * 128:(tc + 1) * 128, :]),
                waits=[("cc", CC_V), ("eev", EEV_ALL)] + rsoslot.acquire(tc),
                inc="rsod", amt=16)
            rv = P.c("rsod")
            P.opd("vector", lambda e, tc=tc, sl=sl: e.tensor_tensor(
                fout[:, sl * D:(sl + 1) * D], rso[:, sl * D:(sl + 1) * D],
                acc[:, tc * D:(tc + 1) * D], ALU.add),
                waits=[("rsod", rv), ("eev", EEV_ALL)], inc="fev")
            fv = P.c("fev")
            P.op("sync", lambda e, tc=tc, sl=sl: e.dma_start(
                out=out[tc * 128:(tc + 1) * 128, :], in_=fout[:, sl * D:(sl + 1) * D]),
                waits=[("fev", fv)], inc="outd", amt=16)
            rsoslot.release(tc, "outd", P.c("outd"))
        OUTD_ALL = P.c("outd")
        assert OUTD_ALL == OCH * 16

        _emit(nc, P, [("outd", OUTD_ALL), ("dbgs", P.c("dbgs")), ("cc", CC_V)])
        _close_all()
    return nc


def _emit(nc, P, final_waits):
    global LAST_PROG
    LAST_PROG = (P, final_waits)
    with nc.Block() as block:
        @block.tensor
        def _(e):
            P.run("tensor", e)

        @block.vector
        def _(e):
            P.run("vector", e)

        @block.scalar
        def _(e):
            P.run("scalar", e)

        @block.gpsimd
        def _(e):
            P.run("gpsimd", e)

        @block.sync
        def _(e):
            P.run("sync", e, final_waits=final_waits)



def _mk_masks():
    m = np.zeros((128, 4 * 512), dtype=np.float32)
    f = np.arange(512)
    for c in range(4):
        m[:, c * 512:(c + 1) * 512] = (f[None, :] >= (c * 128 + np.arange(128))[:, None])
    return m


def kernel(L0_attn_x, L0_attn_w_qkv, L0_attn_b_qkv, L0_attn_w_out, L0_attn_b_out,
           L0_ffn_router_w, L0_ffn_router_b, L0_ffn_expert_w, L0_ffn_expert_b):
    x = np.ascontiguousarray(np.asarray(L0_attn_x, dtype=np.float32))
    w_qkv = np.asarray(L0_attn_w_qkv, dtype=np.float32)
    b_qkv = np.asarray(L0_attn_b_qkv, dtype=np.float32)
    w_out = np.asarray(L0_attn_w_out, dtype=np.float32)
    b_out = np.asarray(L0_attn_b_out, dtype=np.float32)
    router_w = np.asarray(L0_ffn_router_w, dtype=np.float32)
    router_b = np.asarray(L0_ffn_router_b, dtype=np.float32)
    expert_w = np.asarray(L0_ffn_expert_w, dtype=np.float32)
    expert_b = np.asarray(L0_ffn_expert_b, dtype=np.float32)

    wrT = np.concatenate([router_w.T, router_b[None, :],
                          np.zeros((127, E * D), dtype=np.float32)], axis=0)
    wrT = np.ascontiguousarray(wrT.astype(np.float16))
    weT = np.ascontiguousarray(expert_w.transpose(0, 2, 1).astype(np.float16))
    ebp = np.zeros((128, D), dtype=np.float32)
    ebp[0:8] = expert_b
    ebp[64] = b_out
    woT = np.ascontiguousarray(w_out.T)
    masks = _mk_masks()
    ident = np.eye(128, dtype=np.float32)
    consts = np.full((128, 1), -CEXP, dtype=np.float32)

    in_maps = []
    for core in range(8):
        b, r = core // 4, core % 4
        hs0 = r * HPC
        q_rows = w_qkv[hs0 * HD:(hs0 + HPC) * HD]
        k_rows = w_qkv[2048 + hs0 * HD: 2048 + (hs0 + HPC) * HD]
        v_rows = w_qkv[4096 + hs0 * HD: 4096 + (hs0 + HPC) * HD]
        wcat = np.concatenate([q_rows, k_rows, v_rows], axis=0)
        bvpad = np.zeros((128, 1536), dtype=np.float32)
        bvpad[0, 1024:1536] = b_qkv[4096 + hs0 * HD: 4096 + (hs0 + HPC) * HD]
        wqkvT = np.ascontiguousarray(np.concatenate([wcat.T, bvpad], axis=0).astype(np.float16))
        bq = b_qkv[hs0 * HD:(hs0 + HPC) * HD].reshape(4, 128).T
        bk = b_qkv[2048 + hs0 * HD: 2048 + (hs0 + HPC) * HD].reshape(4, 128).T
        bqkv_fm = np.ascontiguousarray(np.concatenate([bq, bk], axis=1))
        woTs = np.ascontiguousarray(woT[hs0 * HD:(hs0 + HPC) * HD, :].astype(np.float16))
        xb_c = np.ascontiguousarray(x[b])
        xo_c = np.ascontiguousarray(x[b, r * 1024:(r + 1) * 1024])
        in_maps.append(dict(
            xb=xb_c, xo=xo_c, wqkvT=wqkvT, bqkv_fm=bqkv_fm,
            woTs=woTs, wrT=wrT, weT=weT, ebp=ebp,
            masks=masks, ident=ident, consts=consts,
        ))

    nc = build(debug=DEBUG, phases=PHASES)
    res = run_bass_kernel_spmd(nc, in_maps, core_ids=list(range(8)))

    full = np.empty((2, SEQ, D), dtype=np.float32)
    for core in range(8):
        b, r = core // 4, core % 4
        full[b, r * 1024:(r + 1) * 1024] = res.results[core]["out"]
    kernel.last_results = res
    return full


# version 10
# speedup vs baseline: 1.0540x; 1.0106x over previous
import math
from contextlib import ExitStack

import numpy as np
import ml_dtypes

import concourse.bass as bass
import concourse.mybir as mybir
from concourse.bass_utils import run_bass_kernel_spmd

F32 = mybir.dt.float32
BF16 = mybir.dt.bfloat16
FP16 = mybir.dt.float16
AF = mybir.ActivationFunctionType
ALU = mybir.AluOpType
AX = mybir.AxisListType

D = 2048
SEQ = 4096
HD = 128
HPC = 4
E = 8
NCH = 32
NG = 8
OCH = 8
NCT = 32
CEXP = 12.0
BIG = 60000.0
EPS = 1e-5
ISCALE = 1.0 / math.sqrt(HD)

ENGS = ("tensor", "vector", "scalar", "gpsimd", "sync")

PHASES = 4
LAST_PROG = None
NO_SMALLK = False
DEBUG = 0


class Prog:
    def __init__(self, nc, ctx):
        self.nc = nc
        self.q = {e: [] for e in ENGS}
        self.cnt = {}
        self.sems = {}
        self.ctx = ctx

    def sem(self, name):
        h = self.ctx.enter_context(self.nc.semaphore(name))
        self.sems[name] = h
        self.cnt[name] = 0

    def op(self, eng, fn, waits=(), inc=None, amt=1):
        ws = tuple((s, v) for (s, v) in waits if s is not None and v and v > 0)
        if inc is not None:
            self.cnt[inc] += amt
        self.q[eng].append((ws, fn, inc, amt))
        return self.cnt[inc] if inc is not None else None

    def c(self, sem):
        return self.cnt[sem]

    def opd(self, eng, fn, waits=(), inc=None, amt=1):
        self.op(eng, fn, waits=waits)
        return self.op(eng, lambda e: e.drain(), inc=inc, amt=amt)

    def run(self, eng, e, final_waits=()):
        seen = {}
        for ws, fn, inc, amt in self.q[eng]:
            for s, v in ws:
                if seen.get(s, 0) >= v:
                    continue
                seen[s] = v
                e.wait_ge(self.sems[s], v)
            ins = fn(e)
            if inc is not None:
                ins.then_inc(self.sems[inc], amt)
        for s, v in final_waits:
            if v > 0 and seen.get(s, 0) < v:
                e.wait_ge(self.sems[s], v)


class Slots:
    def __init__(self, n):
        self.n = n
        self.free_at = [None] * n

    def acquire(self, i):
        w = self.free_at[i % self.n]
        return [w] if w is not None else []

    def release(self, i, sem, val):
        self.free_at[i % self.n] = (sem, val)


def build(debug=0, phases=4, no_smallk=None):
    if no_smallk is None:
        no_smallk = NO_SMALLK
    nc = bass.Bass("TRN2", target_bir_lowering=False, debug=False)

    xb = nc.dram_tensor("xb", [SEQ, D], F32, kind="ExternalInput").ap()
    xo = nc.dram_tensor("xo", [1024, D], F32, kind="ExternalInput").ap()
    wqkvT = nc.dram_tensor("wqkvT", [D + 128, 1536], FP16, kind="ExternalInput").ap()
    bqkv_fm = nc.dram_tensor("bqkv_fm", [128, 8], F32, kind="ExternalInput").ap()
    woTs = nc.dram_tensor("woTs", [512, D], FP16, kind="ExternalInput").ap()
    wrT = nc.dram_tensor("wrT", [D + 128, E * D], FP16, kind="ExternalInput").ap()
    weT = nc.dram_tensor("weT", [E, D, D], FP16, kind="ExternalInput").ap()
    ebp_i = nc.dram_tensor("ebp", [128, D], F32, kind="ExternalInput").ap()
    masks_i = nc.dram_tensor("masks", [128, 4 * 512], F32, kind="ExternalInput").ap()
    ident_i = nc.dram_tensor("ident", [128, 128], F32, kind="ExternalInput").ap()
    consts_i = nc.dram_tensor("consts", [128, 1], F32, kind="ExternalInput").ap()

    out = nc.dram_tensor("out", [1024, D], F32, kind="ExternalOutput").ap()

    attn_tm = nc.dram_tensor("attn_tm", [SEQ, HPC * HD], FP16)
    xn_own = nc.dram_tensor("xn_own", [1024, D], FP16)
    rs_in = nc.dram_tensor("rs_in", [SEQ, D], FP16)
    rs_out_d = nc.dram_tensor("rs_out", [1024, D], FP16)

    dbg = {}
    if debug & 1:
        dbg["q"] = nc.dram_tensor("q_dbg", [512, SEQ], F32, kind="ExternalOutput").ap()
        dbg["k"] = nc.dram_tensor("k_dbg", [512, SEQ], F32, kind="ExternalOutput").ap()
        dbg["v"] = nc.dram_tensor("v_dbg", [512, 32 * 129], F32, kind="ExternalOutput").ap()
    if debug & 2:
        dbg["attn"] = nc.dram_tensor("attn_dbg", [SEQ, HPC * HD], F32, kind="ExternalOutput").ap()
        dbg["rs"] = nc.dram_tensor("rs_dbg", [1024, D], F32, kind="ExternalOutput").ap()
    if debug & 16:
        dbg["lg"] = nc.dram_tensor("lg_dbg", [128, 1024], F32, kind="ExternalOutput").ap()
        dbg["t2"] = nc.dram_tensor("t2_dbg", [128, 1536], F32, kind="ExternalOutput").ap()
    if debug & 8:
        dbg["p"] = nc.dram_tensor("p_dbg", [128, 1024], F32, kind="ExternalOutput").ap()
        dbg["l"] = nc.dram_tensor("l_dbg", [128, 8], F32, kind="ExternalOutput").ap()
        dbg["s"] = nc.dram_tensor("s_dbg", [128, 1024], F32, kind="ExternalOutput").ap()
        dbg["av"] = nc.dram_tensor("av_dbg", [128, 1024], F32, kind="ExternalOutput").ap()
    if debug & 4:
        dbg["c"] = nc.dram_tensor("c_dbg", [128, 64], F32, kind="ExternalOutput").ap()
        dbg["acc"] = nc.dram_tensor("acc_dbg", [128, OCH * D], F32, kind="ExternalOutput").ap()

    ctx = ExitStack()
    inner_scopes = []

    def _close_all():
        for sc in reversed(inner_scopes):
            sc.close()

    with ctx:
        P = Prog(nc, ctx)
        SBP = lambda n, s, d: ctx.enter_context(nc.sbuf_tensor(n, s, d))

        identb = SBP("identb", [128, 128], FP16)
        onesM = SBP("onesM", [128, 128], FP16)
        bqv = SBP("bqv", [128, 8], F32)
        negC = SBP("negC", [128, 1], F32)
        c_all = SBP("c_all", [128, 64], F32)
        cbf = SBP("cbf", [128, 64], FP16)
        recs = SBP("recs", [128, 2], F32)
        if debug & 8:
            lsave = SBP("lsave", [128, 8], F32)
            ssave = SBP("ssave", [128, 1024], F32)

        qkv_scope = ExitStack()
        inner_scopes.append(qkv_scope)
        SBq = lambda n, s, d: qkv_scope.enter_context(nc.sbuf_tensor(n, s, d))
        Qh = [SBq(f"Qh{h}", [128, SEQ], FP16) for h in range(HPC)]
        Kh = [SBq(f"Kh{h}", [128, SEQ], FP16) for h in range(HPC)]
        Vh = [SBq(f"Vh{h}", [128, 32 * 129], FP16) for h in range(HPC)]

        p1_scope = ExitStack()
        inner_scopes.append(p1_scope)
        SB1 = lambda n, s, d: p1_scope.enter_context(nc.sbuf_tensor(n, s, d, side="right"))
        x_sl = SB1("x_sl", [128, 2 * D], F32)
        stat1 = SB1("stat1", [128, 2 * 24], F32)
        mv1 = SB1("mv1", [128, 2 * 2], F32)
        sm1 = SB1("sm1", [128, 2 * 4], F32)
        xn_tm = SB1("xn_tm", [128, 2 * D], FP16)
        wqT = SB1("wqT", [128, 17 * 1536], FP16)
        xnfm = SB1("xnfm", [128, 2 * 8192], FP16)

        SAp = ctx.enter_context(nc.psum_tensor("SA", [128, 1024], F32))
        SBb = ctx.enter_context(nc.psum_tensor("SBb", [128, 1024], F32))
        B4 = [ctx.enter_context(nc.psum_tensor(f"B{i}", [128, 512], F32)) for i in range(4)]
        SAb16 = SAp.ap().bitcast(FP16)
        SBb16 = SBb.ap().bitcast(FP16)
        BANKS = [SAp[:, 0:512], SAp[:, 512:1024], SBb[:, 0:512], SBb[:, 512:1024],
                 B4[0][:, :], B4[1][:, :], B4[2][:, :], B4[3][:, :]]

        for s in ("su", "sx0", "sx1", "dxn", "arstd", "trc", "cpc", "mmq", "evq",
                  "vone", "smm", "sexp", "smsk", "avp", "ep", "sto", "su2",
                  "sxo", "dxo", "aro", "xns", "trx",
                  "p25", "wsl", "omm", "oev", "rsst", "cc",
                  "rmm", "rev", "lrel", "du", "asig", "dge2", "pw", "dpw", "dself",
                  "dcb", "tcT", "dcT", "emm", "eev",
                  "fev", "outd", "dbgs", "rsod", "dbgp"):
            P.sem(s)

        P.op("gpsimd", lambda e: e.dma_start(out=identb[:, :], in_=ident_i), inc="su", amt=16)
        P.op("sync", lambda e: e.dma_start(out=bqv[:, :], in_=bqkv_fm), inc="su", amt=16)
        P.op("sync", lambda e: e.dma_start(out=negC[:, :], in_=consts_i), inc="su", amt=16)
        P.op("gpsimd", lambda e: e.dma_start(
            out=wqT.ap().rearrange("p (c n) -> p c n", n=1536),
            in_=wqkvT.rearrange("(c p) n -> p c n", p=128)), inc="su", amt=16)
        SU_ALL = P.c("su")

        for h in range(HPC):
            P.op("vector", lambda e, h=h: e.memset(
                Vh[h].ap().rearrange("p (k c) -> p k c", c=129)[:, :, 128:129], 1.0),
                inc="vone")
        P.op("vector", lambda e: e.memset(c_all[:, :], 0.0), inc="vone")
        P.op("vector", lambda e: e.memset(onesM[:, :], 1.0), inc="vone")
        VONE_ALL = P.c("vone")

        xslot = Slots(2)
        xnslot = Slots(2)
        trslot = Slots(2)
        fmslot = Slots(2)
        qkv_bank = Slots(4)

        for ch in range(NCH):
            g, sub = ch // 4, ch % 4
            sl = ch % 2
            xsem = "sx0" if sl == 0 else "sx1"
            P.op("sync", lambda e, ch=ch, sl=sl: e.dma_start(
                out=x_sl[:, sl * D:(sl + 1) * D], in_=xb[ch * 128:(ch + 1) * 128, :]),
                waits=xslot.acquire(ch), inc=xsem, amt=16)
            xv = P.c(xsem)

            def ln_stats(e, sl=sl):
                for i in range(4):
                    ins = e.bn_stats(stat1[:, sl * 24 + 6 * i: sl * 24 + 6 * i + 6],
                                     x_sl[:, sl * D + 512 * i: sl * D + 512 * i + 512])
                return ins
            P.op("vector", ln_stats, waits=[(xsem, xv)], inc="dself")
            sv = P.c("dself")
            P.op("vector", lambda e, sl=sl: e.bn_aggr(
                mv1[:, sl * 2: sl * 2 + 2], stat1[:, sl * 24: sl * 24 + 24]),
                waits=[("dself", sv)], inc="dself")
            sv = P.c("dself")
            P.op("vector", lambda e, sl=sl: e.tensor_scalar(
                sm1[:, sl * 4: sl * 4 + 1], mv1[:, sl * 2: sl * 2 + 1], -1.0, None, ALU.mult),
                waits=[("dself", sv)])
            P.op("vector", lambda e, sl=sl: e.tensor_scalar(
                sm1[:, sl * 4 + 1: sl * 4 + 2], mv1[:, sl * 2 + 1: sl * 2 + 2], EPS, None, ALU.add),
                inc="dself")
            sv = P.c("dself")
            P.op("scalar", lambda e, sl=sl: e.activation(
                sm1[:, sl * 4 + 2: sl * 4 + 3], sm1[:, sl * 4 + 1: sl * 4 + 2], AF.Ln),
                waits=[("dself", sv)])
            P.op("scalar", lambda e, sl=sl: e.activation(
                sm1[:, sl * 4 + 3: sl * 4 + 4], sm1[:, sl * 4 + 2: sl * 4 + 3], AF.Exp, scale=-0.5),
                inc="arstd")
            av = P.c("arstd")
            P.op("vector", lambda e, sl=sl: e.tensor_scalar(
                xn_tm[:, sl * D:(sl + 1) * D], x_sl[:, sl * D:(sl + 1) * D],
                sm1[:, sl * 4: sl * 4 + 1], sm1[:, sl * 4 + 3: sl * 4 + 4],
                ALU.add, ALU.mult),
                waits=[("arstd", av)] + xnslot.acquire(ch), inc="dxn")
            xslot.release(ch, "dxn", P.c("dxn"))
            dv = P.c("dxn")

            fw = fmslot.acquire(g) if sub == 0 else []
            for dk in range(16):
                gj = ch * 16 + dk
                P.op("tensor", lambda e, sl=sl, dk=dk, gj=gj: e.transpose(
                    SAb16[:, (gj % 2) * 1024:(gj % 2) * 1024 + 128],
                    xn_tm[:, sl * D + dk * 128: sl * D + dk * 128 + 128], identb[:, :]),
                    waits=[("dxn", dv), ("su", SU_ALL)] + trslot.acquire(gj), inc="trc")
                tv = P.c("trc")
                P.op("vector", lambda e, g=g, sub=sub, dk=dk, gj=gj: e.tensor_copy(
                    xnfm[:, (g % 2) * 8192 + dk * 512 + sub * 128:
                         (g % 2) * 8192 + dk * 512 + sub * 128 + 128],
                    SAb16[:, (gj % 2) * 1024:(gj % 2) * 1024 + 128]),
                    waits=[("trc", tv)] + (fw if dk == 0 else []), inc="cpc")
                trslot.release(gj, "cpc", P.c("cpc"))
            xnslot.release(ch, "trc", P.c("trc"))

            if sub == 3:
                cpv = P.c("cpc")
                for part in range(12):
                    bank = B4[part % 4]
                    bw = qkv_bank.acquire(part)
                    kind, p = part // 4, part % 4
                    for dk in range(16):
                        if kind < 2:
                            col0 = kind * 512 + p * 128
                            P.op("tensor", lambda e, bank=bank, dk=dk, col0=col0, g=g: e.matmul(
                                bank[:, :],
                                wqT[:, dk * 1536 + col0: dk * 1536 + col0 + 128],
                                xnfm[:, (g % 2) * 8192 + dk * 512:(g % 2) * 8192 + dk * 512 + 512],
                                start=(dk == 0), stop=(dk == 15)),
                                waits=([("cpc", cpv)] + bw) if dk == 0 else [],
                                inc="mmq" if dk == 15 else None)
                        else:
                            P.op("tensor", lambda e, bank=bank, dk=dk, p=p, g=g: e.matmul(
                                bank[:, :],
                                xnfm[:, (g % 2) * 8192 + dk * 512 + p * 128:
                                     (g % 2) * 8192 + dk * 512 + p * 128 + 128],
                                wqT[:, dk * 1536 + 1024: dk * 1536 + 1536],
                                start=(dk == 0), stop=False),
                                waits=([("cpc", cpv)] + bw) if dk == 0 else [])
                    if kind == 2:
                        P.op("tensor", lambda e, bank=bank: e.matmul(
                            bank[:, :], onesM[:, :],
                            wqT[:, 16 * 1536 + 1024: 16 * 1536 + 1536],
                            start=False, stop=True),
                            waits=[("vone", VONE_ALL)],
                            inc="mmq")
                    mv_ = P.c("mmq")
                    if kind < 2:
                        dst = (Qh if kind == 0 else Kh)[p]
                        bcol = kind * 4 + p
                        P.op("scalar", lambda e, dst=dst, g=g, bank=bank, bcol=bcol: e.activation(
                            dst[:, g * 512:(g + 1) * 512], bank[:, :], AF.Identity,
                            bias=bqv[:, bcol:bcol + 1], scale=1.0),
                            waits=[("mmq", mv_), ("su", SU_ALL)], inc="evq")
                    else:
                        kc = g * 4 + p
                        for h in range(HPC):
                            P.op("vector", lambda e, h=h, kc=kc, bank=bank: e.tensor_copy(
                                Vh[h][:, kc * 129: kc * 129 + 128],
                                bank[:, h * 128:(h + 1) * 128]),
                                waits=[("mmq", mv_)] if h == 0 else [],
                                inc="evq" if h == 3 else None)
                    qkv_bank.release(part, "evq", P.c("evq"))
                fmslot.release(g, "mmq", P.c("mmq"))
        EVQ_ALL = P.c("evq")
        assert EVQ_ALL == 96
        MMQ_ALL = P.c("mmq")
        TRC_ALL = P.c("trc")
        DXN_ALL = P.c("dxn")

        if debug & 1:
            for h in range(HPC):
                P.op("gpsimd", lambda e, h=h: e.dma_start(
                    out=dbg["q"][h * 128:(h + 1) * 128, :], in_=Qh[h][:, :]),
                    waits=[("evq", EVQ_ALL)], inc="dbgs", amt=16)
                P.op("gpsimd", lambda e, h=h: e.dma_start(
                    out=dbg["k"][h * 128:(h + 1) * 128, :], in_=Kh[h][:, :]),
                    inc="dbgs", amt=16)
                P.op("gpsimd", lambda e, h=h: e.dma_start(
                    out=dbg["v"][h * 128:(h + 1) * 128, :], in_=Vh[h][:, :]),
                    waits=[("vone", VONE_ALL)], inc="dbgs", amt=16)

        p1_scope.close()
        r1_scope = ExitStack()
        inner_scopes.append(r1_scope)
        SBr1 = lambda n, s, d: r1_scope.enter_context(nc.sbuf_tensor(n, s, d, side="right"))
        maskb = SBr1("maskb", [128, 4 * 512], FP16)
        acc = SBr1("acc", [128, OCH * D], F32)
        Pb = SBr1("Pb", [128, 4 * 1024], FP16)
        stg = SBr1("stg", [128, 4 * 128], FP16)
        cT = SBr1("cT", [128, 1024], FP16)
        xo_tm = SBr1("xo_tm", [128, 2 * D], FP16)
        stat2 = SBr1("stat2", [128, 2 * 24], F32)
        mv2 = SBr1("mv2", [128, 2 * 2], F32)
        sm2 = SBr1("sm2", [128, 2 * 4], F32)

        R1GATE = [("mmq", MMQ_ALL), ("trc", TRC_ALL), ("dxn", DXN_ALL)]
        P.op("gpsimd", lambda e: e.dma_start(out=maskb[:, :], in_=masks_i),
             waits=R1GATE, inc="su2", amt=16)
        MASK_V = P.c("su2")

        if phases < 2:
            P.op("sync", lambda e: e.dma_start(out=out, in_=xo),
                 waits=[("evq", EVQ_ALL)], inc="outd", amt=16)
            _emit(nc, P, [("outd", P.c("outd")), ("dbgs", P.c("dbgs"))])
            _close_all()
            return nc

        for tc in range(OCH):
            P.op("sync", lambda e, tc=tc: e.dma_start(
                out=acc[:, tc * D:(tc + 1) * D], in_=xo[tc * 128:(tc + 1) * 128, :]),
                waits=R1GATE if tc == 0 else [], inc="sxo", amt=16)

        sbank = Slots(2)
        avbank = Slots(4)
        stslot = Slots(4)
        pbslot = Slots(4)
        xoslot = Slots(2)

        def emit_own_ln(tc):
            sl = tc % 2

            def ln2(e, sl=sl, tc=tc):
                for i in range(4):
                    ins = e.bn_stats(stat2[:, sl * 24 + 6 * i: sl * 24 + 6 * i + 6],
                                     acc[:, tc * D + 512 * i: tc * D + 512 * i + 512])
                return ins
            P.op("vector", ln2, waits=[("sxo", 16 * (tc + 1))], inc="dself")
            sv = P.c("dself")
            P.op("vector", lambda e, sl=sl: e.bn_aggr(
                mv2[:, sl * 2: sl * 2 + 2], stat2[:, sl * 24: sl * 24 + 24]),
                waits=[("dself", sv)], inc="dself")
            sv = P.c("dself")
            P.op("vector", lambda e, sl=sl: e.tensor_scalar(
                sm2[:, sl * 4: sl * 4 + 1], mv2[:, sl * 2: sl * 2 + 1], -1.0, None, ALU.mult),
                waits=[("dself", sv)])
            P.op("vector", lambda e, sl=sl: e.tensor_scalar(
                sm2[:, sl * 4 + 1: sl * 4 + 2], mv2[:, sl * 2 + 1: sl * 2 + 2], EPS, None, ALU.add),
                inc="dself")
            sv = P.c("dself")
            P.op("scalar", lambda e, sl=sl: e.activation(
                sm2[:, sl * 4 + 2: sl * 4 + 3], sm2[:, sl * 4 + 1: sl * 4 + 2], AF.Ln),
                waits=[("dself", sv)])
            P.op("scalar", lambda e, sl=sl: e.activation(
                sm2[:, sl * 4 + 3: sl * 4 + 4], sm2[:, sl * 4 + 2: sl * 4 + 3], AF.Exp, scale=-0.5),
                inc="aro")
            av = P.c("aro")
            P.opd("vector", lambda e, sl=sl, tc=tc: e.tensor_scalar(
                xo_tm[:, sl * D:(sl + 1) * D], acc[:, tc * D:(tc + 1) * D],
                sm2[:, sl * 4: sl * 4 + 1], sm2[:, sl * 4 + 3: sl * 4 + 4],
                ALU.add, ALU.mult),
                waits=[("aro", av)] + xoslot.acquire(tc), inc="dxo")
            dv = P.c("dxo")
            P.op("sync", lambda e, tc=tc, sl=sl: e.dma_start(
                out=xn_own.ap()[tc * 128:(tc + 1) * 128, :],
                in_=xo_tm[:, sl * D:(sl + 1) * D]),
                waits=[("dxo", dv)], inc="xns", amt=16)
            xoslot.release(tc, "xns", P.c("xns"))

        pairs = []
        for h in range(HPC):
            for qt in range(NG):
                nkc = 4 * (qt + 1)
                for kp in range(nkc // 2):
                    pairs.append((h, qt, kp, nkc))

        def emit_av(job):
            h, qt, kp, nkc, pidx, sev, mwaits = job
            kc0, kc1 = 2 * kp, 2 * kp + 1
            par = pidx % 4
            for j, kc in enumerate((kc0, kc1)):
                for qs in range(4):
                    first = (kc == 0)
                    lastmm = (kc == nkc - 1)
                    w = []
                    if qs == 0:
                        w = [("sexp", sev)] + mwaits
                    if first and qs == 0:
                        w = w + [("vone", VONE_ALL)]
                    if first:
                        w = w + avbank.acquire(qs)
                    P.op("tensor", lambda e, qs=qs, par=par, j=j, kc=kc, h=h,
                         first=first, lastmm=lastmm: e.matmul(
                             B4[qs][:, 0:129],
                             Pb[:, par * 1024 + j * 512 + qs * 128:
                                par * 1024 + j * 512 + qs * 128 + 128],
                             Vh[h][:, kc * 129: kc * 129 + 129],
                             start=first, stop=lastmm),
                        waits=w,
                        inc="avp" if (j == 1 and qs == 3) else None)
            pbslot.release(pidx, "avp", P.c("avp"))

        def emit_epilogue(h, qt):
            n = h * NG + qt
            apv = P.c("avp")
            for qs in range(4):
                ssl = (n * 4 + qs) % 2
                sidx = (n * 4 + qs) % 4
                P.op("vector", lambda e, qs=qs, ssl=ssl: e.reciprocal(
                    recs[:, ssl:ssl + 1], B4[qs][:, 128:129]),
                    waits=[("avp", apv)], inc="dself")
                sv2 = P.c("dself")
                stw = stslot.acquire(n * 4 + qs)
                P.opd("vector", lambda e, qs=qs, ssl=ssl, sidx=sidx: e.tensor_scalar(
                    stg[:, sidx * 128:(sidx + 1) * 128], B4[qs][:, 0:128],
                    recs[:, ssl:ssl + 1], None, ALU.mult),
                    waits=[("dself", sv2)] + stw, inc="ep")
                epv = P.c("ep")
                avbank.release(qs, "ep", epv)
                P.op("sync", lambda e, h=h, qt=qt, qs=qs, sidx=sidx: e.dma_start(
                    out=attn_tm.ap()[qt * 512 + qs * 128: qt * 512 + qs * 128 + 128,
                                     h * 128:(h + 1) * 128],
                    in_=stg[:, sidx * 128:(sidx + 1) * 128]),
                    waits=[("ep", epv)], inc="sto", amt=16)
                stslot.release(n * 4 + qs, "sto", P.c("sto"))

        pair_idx = 0
        pending = None
        for h, qt, kp, nkc in pairs:
            kc0, kc1 = 2 * kp, 2 * kp + 1
            par = pair_idx % 4
            Sp = SAp if pair_idx % 2 == 0 else SBb
            bw = sbank.acquire(pair_idx)
            for j, kc in enumerate((kc0, kc1)):
                P.op("tensor", lambda e, Sp=Sp, j=j, kc=kc, h=h, qt=qt: e.matmul(
                    Sp[:, j * 512:(j + 1) * 512],
                    Kh[h][:, kc * 128: kc * 128 + 128],
                    Qh[h][:, qt * 512: qt * 512 + 512],
                    start=True, stop=True),
                    waits=([("evq", EVQ_ALL)] + bw) if j == 0 else [],
                    inc="smm" if j == 1 else None)
            smv = P.c("smm")
            pw_ = pbslot.acquire(pair_idx)
            P.op("scalar", lambda e, Sp=Sp, par=par: e.activation(
                Pb[:, par * 1024:(par + 1) * 1024], Sp[:, :], AF.Exp,
                bias=negC[:, :], scale=ISCALE),
                waits=[("smm", smv), ("su", SU_ALL), ("su2", MASK_V)] + pw_,
                inc="sexp")
            sev = P.c("sexp")
            sbank.release(pair_idx, "sexp", sev)
            mwaits = []
            for j, kc in enumerate((kc0, kc1)):
                c = kc - 4 * qt
                if 0 <= c <= 3:
                    P.op("vector", lambda e, par=par, j=j, c=c: e.tensor_tensor(
                        Pb[:, par * 1024 + j * 512: par * 1024 + j * 512 + 512],
                        Pb[:, par * 1024 + j * 512: par * 1024 + j * 512 + 512],
                        maskb[:, c * 512:(c + 1) * 512], ALU.mult),
                        waits=[("sexp", sev), ("su2", MASK_V)], inc="smsk")
                    mwaits = [("smsk", P.c("smsk"))]
            if pending is not None:
                emit_av(pending)
                if (pending[0], pending[1]) != (h, qt):
                    emit_epilogue(pending[0], pending[1])
            pending = (h, qt, kp, nkc, pair_idx, sev, mwaits)
            pair_idx += 1
        emit_av(pending)
        emit_epilogue(pending[0], pending[1])

        if phases >= 2.2:
            for tc_ in range(OCH):
                emit_own_ln(tc_)
        STO_ALL = P.c("sto")
        assert STO_ALL == 32 * 4 * 16
        XNS_ALL = P.c("xns")
        EP_ALL = P.c("ep")
        if debug & 8:
            P.op("gpsimd", lambda e: e.dma_start(out=dbg["l"], in_=lsave[:, :]),
                 waits=[("dbgp", P.c("dbgp")), ("ep", EP_ALL)], inc="dbgs", amt=16)
        if phases < 2.2:
            P.op("sync", lambda e: e.dma_start(out=out, in_=xo),
                 waits=[("sto", STO_ALL), ("ep", EP_ALL)], inc="outd", amt=16)
            _emit(nc, P, [("outd", P.c("outd")), ("dbgs", P.c("dbgs"))])
            _close_all()
            return nc
        assert XNS_ALL == OCH * 16

        qkv_scope.close()
        r2_scope = ExitStack()
        inner_scopes.append(r2_scope)
        xnfo = r2_scope.enter_context(nc.sbuf_tensor("xnfo", [128, 16 * 1024], FP16))
        p25_scope = ExitStack()
        inner_scopes.append(p25_scope)
        SB25 = lambda n, s, d: p25_scope.enter_context(nc.sbuf_tensor(n, s, d))
        attnfm = SB25("attnfm", [128, 4 * SEQ], FP16)
        woslab = SB25("woslab", [128, 4 * D], FP16)
        rsbuf = SB25("rsbuf", [128, 4 * 512], FP16)

        for dk in range(16):
            P.op("sync", lambda e, dk=dk: e.dma_start_transpose(
                out=xnfo[:, dk * 1024:(dk + 1) * 1024],
                in_=xn_own.ap()[:, dk * 128:(dk + 1) * 128]),
                waits=[("xns", XNS_ALL), ("ep", EP_ALL)], inc="trx", amt=16)
        TRX_ALL = P.c("trx")
        if phases < 2.3:
            P.op("sync", lambda e: e.dma_start(out=out, in_=xo),
                 waits=[("sto", STO_ALL), ("trx", TRX_ALL)], inc="outd", amt=16)
            _emit(nc, P, [("outd", P.c("outd")), ("dbgs", P.c("dbgs"))])
            _close_all()
            return nc

        for fk in range(4):
            P.op("sync", lambda e, fk=fk: e.dma_start_transpose(
                out=attnfm[:, fk * SEQ:(fk + 1) * SEQ],
                in_=attn_tm.ap()[:, fk * 128:(fk + 1) * 128]),
                waits=[("sto", STO_ALL), ("ep", EP_ALL)], inc="p25", amt=16)
        P25_ALL = P.c("p25")
        if debug & 2:
            P.op("gpsimd", lambda e: e.dma_start(out=dbg["attn"], in_=attn_tm.ap()),
                 waits=[("sto", STO_ALL)], inc="dbgs", amt=16)

        P.op("gpsimd", lambda e: e.dma_start(
            out=woslab.ap().rearrange("p (c n) -> p c n", n=D),
            in_=woTs.rearrange("(c p) n -> p c n", p=128)),
            waits=[("ep", EP_ALL)], inc="wsl", amt=16)
        WO_V = P.c("wsl")

        obank = avbank
        oslot = Slots(4)
        for tt in range(NCH):
            for ot in range(4):
                oi = tt * 4 + ot
                bank = B4[oi % 4]
                bw = obank.acquire(oi)
                for fk in range(4):
                    P.op("tensor", lambda e, bank=bank, fk=fk, tt=tt, ot=ot: e.matmul(
                        bank[:, :],
                        attnfm[:, fk * SEQ + tt * 128: fk * SEQ + tt * 128 + 128],
                        woslab[:, fk * D + ot * 512: fk * D + ot * 512 + 512],
                        start=(fk == 0), stop=(fk == 3)),
                        waits=([("p25", P25_ALL), ("wsl", WO_V)] + bw) if fk == 0 else [],
                        inc="omm" if fk == 3 else None)
                omv = P.c("omm")
                sw = oslot.acquire(oi)
                if oi % 2 == 0:
                    P.opd("scalar", lambda e, bank=bank, osl=oi % 4: e.activation(
                        rsbuf[:, osl * 512:(osl + 1) * 512], bank[:, :], AF.Identity),
                        waits=[("omm", omv)] + sw, inc="oev")
                else:
                    P.opd("vector", lambda e, bank=bank, osl=oi % 4: e.tensor_copy(
                        rsbuf[:, osl * 512:(osl + 1) * 512], bank[:, :]),
                        waits=[("omm", omv)] + sw, inc="oev")
                oev = P.c("oev")
                obank.release(oi, "oev", oev)
                P.op("sync", lambda e, tt=tt, ot=ot, osl=oi % 4: e.dma_start(
                    out=rs_in.ap()[tt * 128:(tt + 1) * 128, ot * 512:(ot + 1) * 512],
                    in_=rsbuf[:, osl * 512:(osl + 1) * 512]),
                    waits=[("oev", oev)], inc="rsst", amt=16)
                oslot.release(oi, "rsst", P.c("rsst"))
        RSST_ALL = P.c("rsst")
        OMM_ALL = P.c("omm")
        if phases < 2.4:
            P.op("sync", lambda e: e.dma_start(out=out, in_=xo),
                 waits=[("rsst", RSST_ALL)], inc="outd", amt=16)
            _emit(nc, P, [("outd", P.c("outd")), ("dbgs", P.c("dbgs"))])
            _close_all()
            return nc

        if phases < 3:
            P.op("gpsimd", lambda e: e.collective_compute(
                "ReduceScatter", ALU.add,
                replica_groups=[[0, 1, 2, 3], [4, 5, 6, 7]],
                ins=[rs_in.ap().opt()], outs=[rs_out_d.ap().opt()]),
                waits=[("rsst", RSST_ALL)], inc="cc")
            CC_V = P.c("cc")
        if phases < 3:
            if debug & 2:
                P.op("gpsimd", lambda e: e.dma_start(out=dbg["rs"], in_=rs_out_d.ap()),
                     waits=[("cc", CC_V)], inc="dbgs", amt=16)
            P.op("sync", lambda e: e.dma_start(out=out, in_=xo),
                 waits=[("rsst", RSST_ALL)], inc="outd", amt=16)
            _emit(nc, P, [("outd", P.c("outd")), ("dbgs", P.c("dbgs")), ("cc", CC_V)])
            _close_all()
            return nc

        p25_scope.close()
        p3_scope = ExitStack()
        inner_scopes.append(p3_scope)
        SB3 = lambda n, s, d: p3_scope.enter_context(nc.sbuf_tensor(n, s, d))
        SLABW = 17 * 512
        wsl = SB3("wsl", [128, 2 * SLABW], FP16)
        NT2 = 4
        lb = SB3("lb", [128, 4 * 1024], FP16)
        m1b = SB3("m1b", [128, NT2 * 128], FP16)
        m2b = SB3("m2b", [128, NT2 * 128], FP16)
        s12 = SB3("s12", [128, NT2 * 128], FP16)
        ge1b = SB3("ge1b", [128, NT2 * 1024], FP16)
        lmb = SB3("lmb", [128, 1024], FP16)
        ub = SB3("ub", [128, NT2 * 1024], FP16)
        sgb = SB3("sgb", [128, NT2 * 1024], FP16)
        wb_ = SB3("wb", [128, 1024], FP16)
        tsc = SB3("tsc", [128, 1024], FP16)
        cw = SB3("cw", [128, 4 * 1024], FP16)
        rtmp = SB3("rtmp", [128, 2 * 16], F32)
        ebpb = SB3("ebpb", [128, D], FP16)

        P3GATE = [("omm", OMM_ALL)]
        P.op("gpsimd", lambda e: e.dma_start(out=ebpb[:, :], in_=ebp_i),
             waits=P3GATE, inc="su2", amt=16)
        SU2_V = P.c("su2")
        P.op("gpsimd", lambda e: e.memset(cw[:, :], 0.0), waits=P3GATE, inc="pw")

        p3bank = Slots(8)
        p3bank.free_at[0] = p3bank.free_at[1] = sbank.free_at[0]
        p3bank.free_at[2] = p3bank.free_at[3] = sbank.free_at[1]
        for i in range(4):
            p3bank.free_at[4 + i] = obank.free_at[i]

        wslslot = Slots(2)
        slab_ready = {}

        def load_router_slab(ct):
            slw = wslslot.acquire(ct) + (P3GATE if ct < 2 else [])
            P.op("gpsimd", lambda e, ct=ct: e.dma_start(
                out=wsl.ap().rearrange("p (s c n) -> p s c n", s=2, n=512)[:, ct % 2],
                in_=wrT[:, ct * 512:(ct + 1) * 512].rearrange("(c p) n -> p c n", p=128)),
                waits=slw, inc="wsl", amt=16)
            slab_ready[ct] = P.c("wsl")

        load_router_slab(0)
        load_router_slab(1)
        CC_V = None

        lbslot = Slots(4)
        t2slot = Slots(NT2)
        tile_idx = 0
        pair_cnt = 0

        for ct in range(NCT):
            for tc in range(OCH):
                bank = BANKS[tc]
                bw = p3bank.acquire(tc)
                slv = slab_ready[ct]
                for dk in range(16):
                    P.op("tensor", lambda e, bank=bank, dk=dk, ct=ct, tc=tc: e.matmul(
                        bank,
                        xnfo[:, dk * 1024 + tc * 128: dk * 1024 + tc * 128 + 128],
                        wsl[:, (ct % 2) * SLABW + dk * 512:(ct % 2) * SLABW + dk * 512 + 512],
                        start=(dk == 0), stop=False),
                        waits=([("trx", TRX_ALL), ("wsl", slv)] + bw) if dk == 0 else [])
                P.op("tensor", lambda e, bank=bank, ct=ct: e.matmul(
                    bank, onesM[:, :],
                    wsl[:, (ct % 2) * SLABW + 16 * 512:(ct % 2) * SLABW + 17 * 512],
                    start=False, stop=True),
                    inc="rmm")
                rmv = P.c("rmm")
                half = tile_idx % 2
                psl = (tile_idx // 2) % 4
                lw = lbslot.acquire(tile_idx // 2) if half == 0 else []
                P.opd("scalar", lambda e, bank=bank, psl=psl, half=half: e.activation(
                    lb[:, psl * 1024 + half * 512: psl * 1024 + half * 512 + 512],
                    bank, AF.Identity),
                    waits=[("rmm", rmv)] + lw, inc="rev")
                p3bank.release(tc, "rev", P.c("rev"))
                tile_idx += 1

                if half == 1:
                    pi = pair_cnt
                    pair_cnt += 1
                    revv = P.c("rev")
                    dbg_lb_hold = None
                    if debug & 16 and pi == 0:
                        P.op("gpsimd", lambda e: e.dma_start(
                            out=dbg["lg"], in_=lb[:, 0:1024]),
                            waits=[("rev", 2)], inc="dbgp", amt=16)
                        dbg_lb_hold = P.c("dbgp")
                    ts = pi % NT2
                    L = lb[:, psl * 1024:(psl + 1) * 1024]
                    Lv = lb.ap().rearrange("p (s g e) -> p s g e", s=4, e=8)[:, psl]
                    M1 = m1b[:, ts * 128: ts * 128 + 128]
                    M2 = m2b[:, ts * 128: ts * 128 + 128]
                    S12 = s12[:, ts * 128: ts * 128 + 128]
                    GE = ge1b[:, ts * 1024: ts * 1024 + 1024]
                    LM = lmb[:, 0:1024]
                    U = ub[:, ts * 1024: ts * 1024 + 1024]
                    SG = sgb[:, ts * 1024: ts * 1024 + 1024]
                    W = wb_[:, 0:1024]
                    TSC = tsc[:, 0:1024]
                    tw = t2slot.acquire(pi)

                    P.op("vector", lambda e, Lv=Lv, M1=M1: e.reduce_max(M1, Lv, axis=AX.X),
                         waits=[("rev", revv)] + tw, inc="dself")
                    sv = P.c("dself")
                    P.op("vector", lambda e, L=L, M1=M1, GE=GE: e.tensor_tensor(
                        GE.rearrange("p (g e) -> p g e", e=8),
                        L.rearrange("p (g e) -> p g e", e=8),
                        M1.unsqueeze(2).broadcast_to([128, 128, 8]), ALU.is_ge),
                        waits=[("dself", sv)])
                    P.op("vector", lambda e, GE=GE, TSC=TSC: e.tensor_scalar(
                        TSC, GE, -BIG, None, ALU.mult))
                    P.op("vector", lambda e, L=L, TSC=TSC, LM=LM: e.tensor_tensor(
                        LM, TSC, L, ALU.add), inc="dself")
                    sv = P.c("dself")
                    P.op("vector", lambda e, LM=LM, M2=M2: e.reduce_max(
                        M2, LM.rearrange("p (g e) -> p g e", e=8), axis=AX.X),
                        waits=[("dself", sv)], inc="dself")
                    sv = P.c("dself")
                    P.op("vector", lambda e, M1=M1, M2=M2, S12=S12: e.tensor_tensor(
                        S12, M1, M2, ALU.add),
                        waits=[("dself", sv)], inc="dself")
                    sv = P.c("dself")
                    P.op("vector", lambda e, L=L, TSC=TSC: e.tensor_scalar(
                        TSC, L, 2.0, None, ALU.mult),
                        waits=[("dself", sv)])
                    P.opd("vector", lambda e, TSC=TSC, S12=S12, U=U: e.tensor_tensor(
                        U.rearrange("p (g e) -> p g e", e=8),
                        TSC.rearrange("p (g e) -> p g e", e=8),
                        S12.unsqueeze(2).broadcast_to([128, 128, 8]), ALU.subtract),
                        inc="du")
                    duv = P.c("du")
                    P.opd("vector", lambda e, L=L, M2=M2, GE=GE: e.tensor_tensor(
                        GE.rearrange("p (g e) -> p g e", e=8),
                        L.rearrange("p (g e) -> p g e", e=8),
                        M2.unsqueeze(2).broadcast_to([128, 128, 8]), ALU.is_ge),
                        inc="dge2")
                    gev = P.c("dge2")
                    P.opd("scalar", lambda e, U=U, SG=SG: e.activation(SG, U, AF.Sigmoid),
                          waits=[("du", duv)], inc="asig")
                    sgv = P.c("asig")
                    P.op("gpsimd", lambda e, SG=SG, GE=GE, W=W: e.tensor_tensor(
                        W, SG, GE, ALU.mult),
                        waits=[("asig", sgv), ("dge2", gev)])
                    P.opd("gpsimd", lambda e, W=W, tcp=(tc // 2): e.tensor_tensor(
                        cw[:, tcp * 1024:(tcp + 1) * 1024],
                        cw[:, tcp * 1024:(tcp + 1) * 1024], W, ALU.add),
                        inc="pw", amt=2)
                    pwv = P.c("pw")
                    if debug & 16 and pi == 0:
                        P.op("gpsimd", lambda e, W=W: e.dma_start(
                            out=dbg["t2"][:, 0:1024], in_=W), waits=[("pw", pwv)],
                            inc="dbgp", amt=16)
                        P.op("gpsimd", lambda e, M1=M1: e.dma_start(
                            out=dbg["t2"][:, 1024:1152], in_=M1), inc="dbgp", amt=16)
                        P.op("gpsimd", lambda e, M2=M2: e.dma_start(
                            out=dbg["t2"][:, 1152:1280], in_=M2), inc="dbgp", amt=16)
                        P.op("gpsimd", lambda e, U=U: e.dma_start(
                            out=dbg["t2"][:, 1280:1536], in_=U[:, 0:256]), inc="dbgp", amt=16)
                        t2_hold = P.c("dbgp")
                    t2slot.release(pi, "pw", pwv)
                    lbslot.release(pi, "pw", pwv)
                    if dbg_lb_hold is not None:
                        lbslot.release(pi, "dbgp", dbg_lb_hold)
                    if debug & 16 and pi == 0:
                        t2slot.release(pi, "dbgp", t2_hold)
            wslslot.release(ct, "rmm", P.c("rmm"))
            if ct + 2 < NCT:
                load_router_slab(ct + 2)
            if ct == 3:
                P.op("gpsimd", lambda e: e.collective_compute(
                    "ReduceScatter", ALU.add,
                    replica_groups=[[0, 1, 2, 3], [4, 5, 6, 7]],
                    ins=[rs_in.ap().opt()], outs=[rs_out_d.ap().opt()]),
                    waits=[("rsst", RSST_ALL)], inc="cc")
        RMM_ALL = P.c("rmm")
        assert RMM_ALL == NCT * OCH
        CC_V = P.c("cc")
        PW_ALL = P.c("pw")
        for tcp in range(4):
            for hh in range(2):
                P.op("vector", lambda e, tcp=tcp, hh=hh: e.reduce_sum(
                    rtmp[:, hh * 8: hh * 8 + 8],
                    cw.ap().rearrange("p (q s g e) -> p q s e g", q=4, s=2, e=8)[:, tcp, hh],
                    axis=AX.X),
                    waits=[("pw", PW_ALL)], inc="dself")
                sv = P.c("dself")
                P.op("vector", lambda e, tcp=tcp, hh=hh: e.tensor_tensor(
                    c_all[:, (tcp * 2 + hh) * 8: (tcp * 2 + hh) * 8 + 8],
                    c_all[:, (tcp * 2 + hh) * 8: (tcp * 2 + hh) * 8 + 8],
                    rtmp[:, hh * 8: hh * 8 + 8], ALU.add),
                    waits=[("dself", sv), ("vone", VONE_ALL)], inc="lrel")
        LREL_ALL = P.c("lrel")

        if debug & 4:
            P.op("gpsimd", lambda e: e.dma_start(out=dbg["c"], in_=c_all[:, :]),
                 waits=[("lrel", LREL_ALL)], inc="dbgs", amt=16)

        P.op("vector", lambda e: e.memset(cT[:, :], 0.0), waits=R1GATE)
        P.op("vector", lambda e: e.memset(cT[64:65, :], 1.0))
        P.op("vector", lambda e: e.tensor_copy(cbf[:, :], c_all[:, :]),
             waits=[("lrel", LREL_ALL)], inc="dcb")
        DCB_V = P.c("dcb")
        REV_ALL = P.c("rev")
        ctslot = Slots(2)
        for tc in range(OCH):
            P.op("tensor", lambda e, tc=tc: e.transpose(
                SBb16[0:8, (tc % 2) * 1024:(tc % 2) * 1024 + 128],
                cbf[:, tc * 8: tc * 8 + 8], identb[:, :]),
                waits=[("dcb", DCB_V), ("rev", REV_ALL)] + ctslot.acquire(tc), inc="tcT")
            tv = P.c("tcT")
            P.op("vector", lambda e, tc=tc: e.tensor_copy(
                cT[0:8, tc * 128:(tc + 1) * 128],
                SBb.ap().bitcast(FP16)[0:8, (tc % 2) * 1024:(tc % 2) * 1024 + 128]),
                waits=[("tcT", tv)], inc="dcT")
            ctslot.release(tc, "dcT", P.c("dcT"))
        DCT_ALL = P.c("dcT")

        for ot in range(4):
            for tc in range(OCH):
                bank = BANKS[tc]
                bw = p3bank.acquire(tc)
                P.op("tensor", lambda e, bank=bank, tc=tc, ot=ot: e.matmul(
                    bank, cT[:, tc * 128:(tc + 1) * 128],
                    ebpb[:, ot * 512:(ot + 1) * 512],
                    start=True, stop=True),
                    waits=[("dcT", DCT_ALL), ("su2", SU2_V)] + bw,
                    inc="emm")
                emv = P.c("emm")
                P.op("vector", lambda e, bank=bank, tc=tc, ot=ot: e.tensor_tensor(
                    acc[:, tc * D + ot * 512: tc * D + ot * 512 + 512],
                    acc[:, tc * D + ot * 512: tc * D + ot * 512 + 512],
                    bank, ALU.add),
                    waits=[("emm", emv)], inc="eev")
                p3bank.release(tc, "eev", P.c("eev"))

        def load_expert_slab(s):
            ei, ot = s // 4, s % 4
            G = NCT + s
            slw = wslslot.acquire(G)
            P.op("gpsimd", lambda e, ei=ei, ot=ot, G=G: e.dma_start(
                out=wsl.ap().rearrange("p (s c n) -> p s c n", s=2, n=512)[:, G % 2, 0:16],
                in_=weT[ei, :, ot * 512:(ot + 1) * 512].rearrange("(c p) n -> p c n", p=128)),
                waits=slw, inc="wsl", amt=16)
            slab_ready[G] = P.c("wsl")

        load_expert_slab(0)
        load_expert_slab(1)
        for s in range(32):
            ei, ot = s // 4, s % 4
            G = NCT + s
            slv = slab_ready[G]
            for tc in range(OCH):
                bank = BANKS[tc]
                bw = p3bank.acquire(tc)
                for dk in range(16):
                    P.op("tensor", lambda e, bank=bank, dk=dk, G=G, tc=tc: e.matmul(
                        bank,
                        xnfo[:, dk * 1024 + tc * 128: dk * 1024 + tc * 128 + 128],
                        wsl[:, (G % 2) * SLABW + dk * 512:(G % 2) * SLABW + dk * 512 + 512],
                        start=(dk == 0), stop=(dk == 15)),
                        waits=([("wsl", slv)] + bw) if dk == 0 else [],
                        inc="emm" if dk == 15 else None)
                emv = P.c("emm")
                P.op("vector", lambda e, bank=bank, tc=tc, ot=ot, ei=ei: e.scalar_tensor_tensor(
                    acc[:, tc * D + ot * 512: tc * D + ot * 512 + 512],
                    bank, c_all[:, tc * 8 + ei: tc * 8 + ei + 1],
                    acc[:, tc * D + ot * 512: tc * D + ot * 512 + 512],
                    ALU.mult, ALU.add),
                    waits=[("emm", emv)], inc="eev")
                p3bank.release(tc, "eev", P.c("eev"))
            wslslot.release(G, "emm", P.c("emm"))
            if s + 2 < 32:
                load_expert_slab(s + 2)
        EEV_ALL = P.c("eev")
        assert EEV_ALL == 32 + 256

        if debug & 4:
            P.op("gpsimd", lambda e: e.dma_start(out=dbg["acc"], in_=acc[:, :]),
                 waits=[("eev", EEV_ALL)], inc="dbgs", amt=16)
        if debug & 2:
            P.op("gpsimd", lambda e: e.dma_start(out=dbg["rs"], in_=rs_out_d.ap()),
                 waits=[("cc", CC_V)], inc="dbgs", amt=16)

        if phases < 4:
            P.op("sync", lambda e: e.dma_start(out=out, in_=xo),
                 waits=[("eev", EEV_ALL)], inc="outd", amt=16)
            _emit(nc, P, [("outd", P.c("outd")), ("dbgs", P.c("dbgs")), ("cc", CC_V)])
            _close_all()
            return nc

        p3_scope.close()
        p4_scope = ExitStack()
        inner_scopes.append(p4_scope)
        rso = p4_scope.enter_context(nc.sbuf_tensor("rso", [128, 2 * D], FP16))
        fout = p4_scope.enter_context(nc.sbuf_tensor("fout", [128, 2 * D], F32))
        rsoslot = Slots(2)
        for tc in range(OCH):
            sl = tc % 2
            P.op("sync", lambda e, tc=tc, sl=sl: e.dma_start(
                out=rso[:, sl * D:(sl + 1) * D],
                in_=rs_out_d.ap()[tc * 128:(tc + 1) * 128, :]),
                waits=[("cc", CC_V), ("eev", EEV_ALL)] + rsoslot.acquire(tc),
                inc="rsod", amt=16)
            rv = P.c("rsod")
            P.opd("vector", lambda e, tc=tc, sl=sl: e.tensor_tensor(
                fout[:, sl * D:(sl + 1) * D], rso[:, sl * D:(sl + 1) * D],
                acc[:, tc * D:(tc + 1) * D], ALU.add),
                waits=[("rsod", rv), ("eev", EEV_ALL)], inc="fev")
            fv = P.c("fev")
            P.op("sync", lambda e, tc=tc, sl=sl: e.dma_start(
                out=out[tc * 128:(tc + 1) * 128, :], in_=fout[:, sl * D:(sl + 1) * D]),
                waits=[("fev", fv)], inc="outd", amt=16)
            rsoslot.release(tc, "outd", P.c("outd"))
        OUTD_ALL = P.c("outd")
        assert OUTD_ALL == OCH * 16

        _emit(nc, P, [("outd", OUTD_ALL), ("dbgs", P.c("dbgs")), ("cc", CC_V)])
        _close_all()
    return nc


def _emit(nc, P, final_waits):
    global LAST_PROG
    LAST_PROG = (P, final_waits)
    with nc.Block() as block:
        @block.tensor
        def _(e):
            P.run("tensor", e)

        @block.vector
        def _(e):
            P.run("vector", e)

        @block.scalar
        def _(e):
            P.run("scalar", e)

        @block.gpsimd
        def _(e):
            P.run("gpsimd", e)

        @block.sync
        def _(e):
            P.run("sync", e, final_waits=final_waits)



def _mk_masks():
    m = np.zeros((128, 4 * 512), dtype=np.float32)
    f = np.arange(512)
    for c in range(4):
        m[:, c * 512:(c + 1) * 512] = (f[None, :] >= (c * 128 + np.arange(128))[:, None])
    return m


def kernel(L0_attn_x, L0_attn_w_qkv, L0_attn_b_qkv, L0_attn_w_out, L0_attn_b_out,
           L0_ffn_router_w, L0_ffn_router_b, L0_ffn_expert_w, L0_ffn_expert_b):
    x = np.ascontiguousarray(np.asarray(L0_attn_x, dtype=np.float32))
    w_qkv = np.asarray(L0_attn_w_qkv, dtype=np.float32)
    b_qkv = np.asarray(L0_attn_b_qkv, dtype=np.float32)
    w_out = np.asarray(L0_attn_w_out, dtype=np.float32)
    b_out = np.asarray(L0_attn_b_out, dtype=np.float32)
    router_w = np.asarray(L0_ffn_router_w, dtype=np.float32)
    router_b = np.asarray(L0_ffn_router_b, dtype=np.float32)
    expert_w = np.asarray(L0_ffn_expert_w, dtype=np.float32)
    expert_b = np.asarray(L0_ffn_expert_b, dtype=np.float32)

    wrT = np.concatenate([router_w.T, router_b[None, :],
                          np.zeros((127, E * D), dtype=np.float32)], axis=0)
    wrT = np.ascontiguousarray(wrT.astype(np.float16))
    weT = np.ascontiguousarray(expert_w.transpose(0, 2, 1).astype(np.float16))
    ebp = np.zeros((128, D), dtype=np.float32)
    ebp[0:8] = expert_b
    ebp[64] = b_out
    woT = np.ascontiguousarray(w_out.T)
    masks = _mk_masks()
    ident = np.eye(128, dtype=np.float32)
    consts = np.full((128, 1), -CEXP, dtype=np.float32)

    in_maps = []
    for core in range(8):
        b, r = core // 4, core % 4
        hs0 = r * HPC
        q_rows = w_qkv[hs0 * HD:(hs0 + HPC) * HD]
        k_rows = w_qkv[2048 + hs0 * HD: 2048 + (hs0 + HPC) * HD]
        v_rows = w_qkv[4096 + hs0 * HD: 4096 + (hs0 + HPC) * HD]
        wcat = np.concatenate([q_rows, k_rows, v_rows], axis=0)
        bvpad = np.zeros((128, 1536), dtype=np.float32)
        bvpad[0, 1024:1536] = b_qkv[4096 + hs0 * HD: 4096 + (hs0 + HPC) * HD]
        wqkvT = np.ascontiguousarray(np.concatenate([wcat.T, bvpad], axis=0).astype(np.float16))
        bq = b_qkv[hs0 * HD:(hs0 + HPC) * HD].reshape(4, 128).T
        bk = b_qkv[2048 + hs0 * HD: 2048 + (hs0 + HPC) * HD].reshape(4, 128).T
        bqkv_fm = np.ascontiguousarray(np.concatenate([bq, bk], axis=1))
        woTs = np.ascontiguousarray(woT[hs0 * HD:(hs0 + HPC) * HD, :].astype(np.float16))
        xb_c = np.ascontiguousarray(x[b])
        xo_c = np.ascontiguousarray(x[b, r * 1024:(r + 1) * 1024])
        in_maps.append(dict(
            xb=xb_c, xo=xo_c, wqkvT=wqkvT, bqkv_fm=bqkv_fm,
            woTs=woTs, wrT=wrT, weT=weT, ebp=ebp,
            masks=masks, ident=ident, consts=consts,
        ))

    nc = build(debug=DEBUG, phases=PHASES)
    res = run_bass_kernel_spmd(nc, in_maps, core_ids=list(range(8)))

    full = np.empty((2, SEQ, D), dtype=np.float32)
    for core in range(8):
        b, r = core // 4, core % 4
        full[b, r * 1024:(r + 1) * 1024] = res.results[core]["out"]
    kernel.last_results = res
    return full


# version 11
# speedup vs baseline: 1.0671x; 1.0125x over previous
import math
from contextlib import ExitStack

import numpy as np
import ml_dtypes

import concourse.bass as bass
import concourse.mybir as mybir
from concourse.bass_utils import run_bass_kernel_spmd

F32 = mybir.dt.float32
BF16 = mybir.dt.bfloat16
FP16 = mybir.dt.float16
AF = mybir.ActivationFunctionType
ALU = mybir.AluOpType
AX = mybir.AxisListType

D = 2048
SEQ = 4096
HD = 128
HPC = 4
E = 8
NCH = 32
NG = 8
OCH = 8
NCT = 32
CEXP = 12.0
BIG = 60000.0
EPS = 1e-5
ISCALE = 1.0 / math.sqrt(HD)

ENGS = ("tensor", "vector", "scalar", "gpsimd", "sync")

PHASES = 4
LAST_PROG = None
NO_SMALLK = False
DEBUG = 0


class Prog:
    def __init__(self, nc, ctx):
        self.nc = nc
        self.q = {e: [] for e in ENGS}
        self.cnt = {}
        self.sems = {}
        self.ctx = ctx

    def sem(self, name):
        h = self.ctx.enter_context(self.nc.semaphore(name))
        self.sems[name] = h
        self.cnt[name] = 0

    def op(self, eng, fn, waits=(), inc=None, amt=1):
        ws = tuple((s, v) for (s, v) in waits if s is not None and v and v > 0)
        if inc is not None:
            self.cnt[inc] += amt
        self.q[eng].append((ws, fn, inc, amt))
        return self.cnt[inc] if inc is not None else None

    def c(self, sem):
        return self.cnt[sem]

    def opd(self, eng, fn, waits=(), inc=None, amt=1):
        self.op(eng, fn, waits=waits)
        return self.op(eng, lambda e: e.drain(), inc=inc, amt=amt)

    def run(self, eng, e, final_waits=()):
        seen = {}
        for ws, fn, inc, amt in self.q[eng]:
            for s, v in ws:
                if seen.get(s, 0) >= v:
                    continue
                seen[s] = v
                e.wait_ge(self.sems[s], v)
            ins = fn(e)
            if inc is not None:
                ins.then_inc(self.sems[inc], amt)
        for s, v in final_waits:
            if v > 0 and seen.get(s, 0) < v:
                e.wait_ge(self.sems[s], v)


class Slots:
    def __init__(self, n):
        self.n = n
        self.free_at = [None] * n

    def acquire(self, i):
        w = self.free_at[i % self.n]
        return [w] if w is not None else []

    def release(self, i, sem, val):
        self.free_at[i % self.n] = (sem, val)


def build(debug=0, phases=4, no_smallk=None):
    if no_smallk is None:
        no_smallk = NO_SMALLK
    nc = bass.Bass("TRN2", target_bir_lowering=False, debug=False)

    xb = nc.dram_tensor("xb", [SEQ, D], F32, kind="ExternalInput").ap()
    xo = nc.dram_tensor("xo", [1024, D], F32, kind="ExternalInput").ap()
    wqkvT = nc.dram_tensor("wqkvT", [D + 128, 1536], FP16, kind="ExternalInput").ap()
    bqkv_fm = nc.dram_tensor("bqkv_fm", [128, 8], F32, kind="ExternalInput").ap()
    woTs = nc.dram_tensor("woTs", [512, D], FP16, kind="ExternalInput").ap()
    wrT = nc.dram_tensor("wrT", [D + 128, E * D], FP16, kind="ExternalInput").ap()
    weT = nc.dram_tensor("weT", [E, D, D], FP16, kind="ExternalInput").ap()
    ebp_i = nc.dram_tensor("ebp", [128, D], F32, kind="ExternalInput").ap()
    masks_i = nc.dram_tensor("masks", [128, 4 * 512], F32, kind="ExternalInput").ap()
    ident_i = nc.dram_tensor("ident", [128, 128], F32, kind="ExternalInput").ap()
    consts_i = nc.dram_tensor("consts", [128, 1], F32, kind="ExternalInput").ap()

    out = nc.dram_tensor("out", [1024, D], F32, kind="ExternalOutput").ap()

    attn_tm = nc.dram_tensor("attn_tm", [SEQ, HPC * HD], FP16)
    xn_own = nc.dram_tensor("xn_own", [1024, D], FP16)
    rs_in = nc.dram_tensor("rs_in", [SEQ, D], FP16)
    rs_out_d = nc.dram_tensor("rs_out", [1024, D], FP16)

    dbg = {}
    if debug & 1:
        dbg["q"] = nc.dram_tensor("q_dbg", [512, SEQ], F32, kind="ExternalOutput").ap()
        dbg["k"] = nc.dram_tensor("k_dbg", [512, SEQ], F32, kind="ExternalOutput").ap()
        dbg["v"] = nc.dram_tensor("v_dbg", [512, 32 * 129], F32, kind="ExternalOutput").ap()
    if debug & 2:
        dbg["attn"] = nc.dram_tensor("attn_dbg", [SEQ, HPC * HD], F32, kind="ExternalOutput").ap()
        dbg["rs"] = nc.dram_tensor("rs_dbg", [1024, D], F32, kind="ExternalOutput").ap()
    if debug & 16:
        dbg["lg"] = nc.dram_tensor("lg_dbg", [128, 1024], F32, kind="ExternalOutput").ap()
        dbg["t2"] = nc.dram_tensor("t2_dbg", [128, 1536], F32, kind="ExternalOutput").ap()
    if debug & 8:
        dbg["p"] = nc.dram_tensor("p_dbg", [128, 1024], F32, kind="ExternalOutput").ap()
        dbg["l"] = nc.dram_tensor("l_dbg", [128, 8], F32, kind="ExternalOutput").ap()
        dbg["s"] = nc.dram_tensor("s_dbg", [128, 1024], F32, kind="ExternalOutput").ap()
        dbg["av"] = nc.dram_tensor("av_dbg", [128, 1024], F32, kind="ExternalOutput").ap()
    if debug & 4:
        dbg["c"] = nc.dram_tensor("c_dbg", [128, 64], F32, kind="ExternalOutput").ap()
        dbg["acc"] = nc.dram_tensor("acc_dbg", [128, OCH * D], F32, kind="ExternalOutput").ap()

    ctx = ExitStack()
    inner_scopes = []

    def _close_all():
        for sc in reversed(inner_scopes):
            sc.close()

    with ctx:
        P = Prog(nc, ctx)
        SBP = lambda n, s, d: ctx.enter_context(nc.sbuf_tensor(n, s, d))

        identb = SBP("identb", [128, 128], FP16)
        onesM = SBP("onesM", [128, 128], FP16)
        bqv = SBP("bqv", [128, 8], F32)
        negC = SBP("negC", [128, 1], F32)
        c_all = SBP("c_all", [128, 64], F32)
        cbf = SBP("cbf", [128, 64], FP16)
        recs = SBP("recs", [128, 2], F32)
        if debug & 8:
            lsave = SBP("lsave", [128, 8], F32)
            ssave = SBP("ssave", [128, 1024], F32)

        qkv_scope = ExitStack()
        inner_scopes.append(qkv_scope)
        SBq = lambda n, s, d: qkv_scope.enter_context(nc.sbuf_tensor(n, s, d))
        Qh = [SBq(f"Qh{h}", [128, SEQ], FP16) for h in range(HPC)]
        Kh = [SBq(f"Kh{h}", [128, SEQ], FP16) for h in range(HPC)]
        Vh = [SBq(f"Vh{h}", [128, 32 * 129], FP16) for h in range(HPC)]

        p1_scope = ExitStack()
        inner_scopes.append(p1_scope)
        SB1 = lambda n, s, d: p1_scope.enter_context(nc.sbuf_tensor(n, s, d, side="right"))
        x_sl = SB1("x_sl", [128, 2 * D], F32)
        stat1 = SB1("stat1", [128, 2 * 24], F32)
        mv1 = SB1("mv1", [128, 2 * 2], F32)
        sm1 = SB1("sm1", [128, 2 * 6], F32)
        xn_tm = SB1("xn_tm", [128, 2 * D], FP16)
        wqT = SB1("wqT", [128, 17 * 1536], FP16)
        xnfm = SB1("xnfm", [128, 2 * 8192], FP16)

        SAp = ctx.enter_context(nc.psum_tensor("SA", [128, 1024], F32))
        SBb = ctx.enter_context(nc.psum_tensor("SBb", [128, 1024], F32))
        B4 = [ctx.enter_context(nc.psum_tensor(f"B{i}", [128, 512], F32)) for i in range(4)]
        SAb16 = SAp.ap().bitcast(FP16)
        SBb16 = SBb.ap().bitcast(FP16)
        BANKS = [SAp[:, 0:512], SAp[:, 512:1024], SBb[:, 0:512], SBb[:, 512:1024],
                 B4[0][:, :], B4[1][:, :], B4[2][:, :], B4[3][:, :]]

        for s in ("su", "sx0", "sx1", "dxn", "arstd", "trc", "cpc", "mmq", "evq",
                  "vone", "smm", "sexp", "smsk", "avp", "ep", "sto", "su2",
                  "sxo", "dxo", "aro", "xns", "trx",
                  "p25", "wsl", "omm", "oev", "rsst", "cc",
                  "rmm", "rev", "lrel", "du", "asig", "dge2", "pw", "dpw", "dself",
                  "dcb", "tcT", "dcT", "emm", "eev",
                  "fev", "outd", "dbgs", "rsod", "dbgp"):
            P.sem(s)

        P.op("gpsimd", lambda e: e.dma_start(out=identb[:, :], in_=ident_i), inc="su", amt=16)
        P.op("sync", lambda e: e.dma_start(out=bqv[:, :], in_=bqkv_fm), inc="su", amt=16)
        P.op("sync", lambda e: e.dma_start(out=negC[:, :], in_=consts_i), inc="su", amt=16)
        P.op("gpsimd", lambda e: e.dma_start(
            out=wqT.ap().rearrange("p (c n) -> p c n", n=1536),
            in_=wqkvT.rearrange("(c p) n -> p c n", p=128)), inc="su", amt=16)
        SU_ALL = P.c("su")

        for h in range(HPC):
            P.op("vector", lambda e, h=h: e.memset(
                Vh[h].ap().rearrange("p (k c) -> p k c", c=129)[:, :, 128:129], 1.0),
                inc="vone")
        P.op("vector", lambda e: e.memset(c_all[:, :], 0.0), inc="vone")
        P.op("vector", lambda e: e.memset(onesM[:, :], 1.0), inc="vone")
        VONE_ALL = P.c("vone")

        xslot = Slots(2)
        xnslot = Slots(2)
        trslot = Slots(4)
        fmslot = Slots(2)
        qkv_bank = Slots(4)

        TRB = [SAb16, SAb16, SBb16, SBb16]
        xv_of, av_of, dxn_of, cp_done = {}, {}, {}, {}

        def st_load(ch):
            sl = ch % 2
            xsem = "sx0" if sl == 0 else "sx1"
            P.op("sync", lambda e, ch=ch, sl=sl: e.dma_start(
                out=x_sl[:, sl * D:(sl + 1) * D], in_=xb[ch * 128:(ch + 1) * 128, :]),
                waits=xslot.acquire(ch), inc=xsem, amt=16)
            xv_of[ch] = (xsem, P.c(xsem))

        def st_stats(ch):
            sl = ch % 2
            xsem, xv = xv_of[ch]

            def ln_stats(e, sl=sl):
                for i in range(4):
                    ins = e.bn_stats(stat1[:, sl * 24 + 6 * i: sl * 24 + 6 * i + 6],
                                     x_sl[:, sl * D + 512 * i: sl * D + 512 * i + 512])
                return ins
            P.op("vector", ln_stats, waits=[(xsem, xv)], inc="dself")
            sv = P.c("dself")
            P.op("vector", lambda e, sl=sl: e.bn_aggr(
                mv1[:, sl * 2: sl * 2 + 2], stat1[:, sl * 24: sl * 24 + 24]),
                waits=[("dself", sv)], inc="dself")
            sv = P.c("dself")
            P.op("vector", lambda e, sl=sl: e.tensor_scalar(
                sm1[:, sl * 6: sl * 6 + 1], mv1[:, sl * 2: sl * 2 + 1], -1.0, None, ALU.mult),
                waits=[("dself", sv)])
            P.op("vector", lambda e, sl=sl: e.tensor_scalar(
                sm1[:, sl * 6 + 1: sl * 6 + 2], mv1[:, sl * 2 + 1: sl * 2 + 2], EPS, None,
                ALU.add), inc="dself")
            sv = P.c("dself")
            P.op("scalar", lambda e, sl=sl: e.activation(
                sm1[:, sl * 6 + 2: sl * 6 + 3], sm1[:, sl * 6 + 1: sl * 6 + 2], AF.Ln),
                waits=[("dself", sv)])
            P.op("scalar", lambda e, sl=sl: e.activation(
                sm1[:, sl * 6 + 3: sl * 6 + 4], sm1[:, sl * 6 + 2: sl * 6 + 3], AF.Exp,
                scale=-0.5))
            P.op("scalar", lambda e, sl=sl: e.activation(
                sm1[:, sl * 6 + 4: sl * 6 + 5], sm1[:, sl * 6: sl * 6 + 1], AF.Identity,
                scale=sm1[:, sl * 6 + 3: sl * 6 + 4]), inc="arstd")
            av_of[ch] = P.c("arstd")

        def st_norm(ch):
            sl = ch % 2
            P.op("scalar", lambda e, sl=sl: e.activation(
                xn_tm[:, sl * D:(sl + 1) * D], x_sl[:, sl * D:(sl + 1) * D], AF.Identity,
                bias=sm1[:, sl * 6 + 4: sl * 6 + 5], scale=sm1[:, sl * 6 + 3: sl * 6 + 4]),
                waits=[("arstd", av_of[ch])] + xnslot.acquire(ch), inc="dxn")
            xslot.release(ch, "dxn", P.c("dxn"))
            dxn_of[ch] = P.c("dxn")

        def st_tr(ch):
            g, sub = ch // 4, ch % 4
            sl = ch % 2
            dv = dxn_of[ch]
            fw = fmslot.acquire(g) if sub == 0 else []
            for dk in range(16):
                gj = ch * 16 + dk
                j = gj % 4
                buf = TRB[j]
                off = (j % 2) * 1024
                P.op("tensor", lambda e, sl=sl, dk=dk, buf=buf, off=off: e.transpose(
                    buf[:, off:off + 128],
                    xn_tm[:, sl * D + dk * 128: sl * D + dk * 128 + 128], identb[:, :]),
                    waits=[("dxn", dv), ("su", SU_ALL)] + trslot.acquire(gj), inc="trc")
                tv = P.c("trc")
                dst = xnfm[:, (g % 2) * 8192 + dk * 512 + sub * 128:
                           (g % 2) * 8192 + dk * 512 + sub * 128 + 128]
                cw_ = [("trc", tv)] + (fw if dk in (0, 1) else [])
                if gj % 2 == 0:
                    P.op("vector", lambda e, dst=dst, buf=buf, off=off: e.tensor_copy(
                        dst, buf[:, off:off + 128]), waits=cw_, inc="cpc")
                else:
                    P.op("scalar", lambda e, dst=dst, buf=buf, off=off: e.activation(
                        dst, buf[:, off:off + 128], AF.Identity), waits=cw_, inc="cpc")
                trslot.release(gj, "cpc", P.c("cpc"))
            xnslot.release(ch, "trc", P.c("trc"))
            if sub == 3:
                cp_done[g] = P.c("cpc")

        def st_mm(g, parts):
            cpv = cp_done[g]
            for part in parts:
                bank = B4[part % 4]
                bw = qkv_bank.acquire(part)
                kind, p = part // 4, part % 4
                for dk in range(16):
                    if kind < 2:
                        col0 = kind * 512 + p * 128
                        P.op("tensor", lambda e, bank=bank, dk=dk, col0=col0, g=g: e.matmul(
                            bank[:, :],
                            wqT[:, dk * 1536 + col0: dk * 1536 + col0 + 128],
                            xnfm[:, (g % 2) * 8192 + dk * 512:(g % 2) * 8192 + dk * 512 + 512],
                            start=(dk == 0), stop=(dk == 15)),
                            waits=([("cpc", cpv)] + bw) if dk == 0 else [],
                            inc="mmq" if dk == 15 else None)
                    else:
                        P.op("tensor", lambda e, bank=bank, dk=dk, p=p, g=g: e.matmul(
                            bank[:, :],
                            xnfm[:, (g % 2) * 8192 + dk * 512 + p * 128:
                                 (g % 2) * 8192 + dk * 512 + p * 128 + 128],
                            wqT[:, dk * 1536 + 1024: dk * 1536 + 1536],
                            start=(dk == 0), stop=False),
                            waits=([("cpc", cpv)] + bw) if dk == 0 else [])
                if kind == 2:
                    P.op("tensor", lambda e, bank=bank: e.matmul(
                        bank[:, :], onesM[:, :],
                        wqT[:, 16 * 1536 + 1024: 16 * 1536 + 1536],
                        start=False, stop=True),
                        waits=[("vone", VONE_ALL)],
                        inc="mmq")
                mv_ = P.c("mmq")
                if kind < 2:
                    dst = (Qh if kind == 0 else Kh)[p]
                    bcol = kind * 4 + p
                    P.op("scalar", lambda e, dst=dst, g=g, bank=bank, bcol=bcol: e.activation(
                        dst[:, g * 512:(g + 1) * 512], bank[:, :], AF.Identity,
                        bias=bqv[:, bcol:bcol + 1], scale=1.0),
                        waits=[("mmq", mv_), ("su", SU_ALL)], inc="evq")
                else:
                    kc = g * 4 + p
                    for h in range(HPC):
                        P.op("vector", lambda e, h=h, kc=kc, bank=bank: e.tensor_copy(
                            Vh[h][:, kc * 129: kc * 129 + 128],
                            bank[:, h * 128:(h + 1) * 128]),
                            waits=[("mmq", mv_)] if h == 0 else [],
                            inc="evq" if h == 3 else None)
                qkv_bank.release(part, "evq", P.c("evq"))
            if parts[-1] == 11:
                fmslot.release(g, "mmq", P.c("mmq"))

        st_load(0)
        st_load(1)
        st_stats(0)
        for ch in range(NCH):
            g, sub = ch // 4, ch % 4
            if ch + 1 < NCH:
                st_stats(ch + 1)
            st_norm(ch)
            if ch + 2 < NCH:
                st_load(ch + 2)
            st_tr(ch)
            if g > 0:
                st_mm(g - 1, list(range(sub * 3, sub * 3 + 3)))
        st_mm(NCH // 4 - 1, list(range(12)))
        EVQ_ALL = P.c("evq")
        assert EVQ_ALL == 96
        MMQ_ALL = P.c("mmq")
        TRC_ALL = P.c("trc")
        DXN_ALL = P.c("dxn")

        if debug & 1:
            for h in range(HPC):
                P.op("gpsimd", lambda e, h=h: e.dma_start(
                    out=dbg["q"][h * 128:(h + 1) * 128, :], in_=Qh[h][:, :]),
                    waits=[("evq", EVQ_ALL)], inc="dbgs", amt=16)
                P.op("gpsimd", lambda e, h=h: e.dma_start(
                    out=dbg["k"][h * 128:(h + 1) * 128, :], in_=Kh[h][:, :]),
                    inc="dbgs", amt=16)
                P.op("gpsimd", lambda e, h=h: e.dma_start(
                    out=dbg["v"][h * 128:(h + 1) * 128, :], in_=Vh[h][:, :]),
                    waits=[("vone", VONE_ALL)], inc="dbgs", amt=16)

        p1_scope.close()
        r1_scope = ExitStack()
        inner_scopes.append(r1_scope)
        SBr1 = lambda n, s, d: r1_scope.enter_context(nc.sbuf_tensor(n, s, d, side="right"))
        maskb = SBr1("maskb", [128, 4 * 512], FP16)
        acc = SBr1("acc", [128, OCH * D], F32)
        Pb = SBr1("Pb", [128, 4 * 1024], FP16)
        stg = SBr1("stg", [128, 4 * 128], FP16)
        cT = SBr1("cT", [128, 1024], FP16)
        xo_tm = SBr1("xo_tm", [128, 2 * D], FP16)
        stat2 = SBr1("stat2", [128, 2 * 24], F32)
        mv2 = SBr1("mv2", [128, 2 * 2], F32)
        sm2 = SBr1("sm2", [128, 2 * 4], F32)

        R1GATE = [("mmq", MMQ_ALL), ("trc", TRC_ALL), ("dxn", DXN_ALL)]
        P.op("gpsimd", lambda e: e.dma_start(out=maskb[:, :], in_=masks_i),
             waits=R1GATE, inc="su2", amt=16)
        MASK_V = P.c("su2")

        if phases < 2:
            P.op("sync", lambda e: e.dma_start(out=out, in_=xo),
                 waits=[("evq", EVQ_ALL)], inc="outd", amt=16)
            _emit(nc, P, [("outd", P.c("outd")), ("dbgs", P.c("dbgs"))])
            _close_all()
            return nc

        for tc in range(OCH):
            P.op("sync", lambda e, tc=tc: e.dma_start(
                out=acc[:, tc * D:(tc + 1) * D], in_=xo[tc * 128:(tc + 1) * 128, :]),
                waits=R1GATE if tc == 0 else [], inc="sxo", amt=16)

        sbank = Slots(2)
        avbank = Slots(4)
        stslot = Slots(4)
        pbslot = Slots(4)
        xoslot = Slots(2)

        def emit_own_ln(tc):
            sl = tc % 2

            def ln2(e, sl=sl, tc=tc):
                for i in range(4):
                    ins = e.bn_stats(stat2[:, sl * 24 + 6 * i: sl * 24 + 6 * i + 6],
                                     acc[:, tc * D + 512 * i: tc * D + 512 * i + 512])
                return ins
            P.op("vector", ln2, waits=[("sxo", 16 * (tc + 1))], inc="dself")
            sv = P.c("dself")
            P.op("vector", lambda e, sl=sl: e.bn_aggr(
                mv2[:, sl * 2: sl * 2 + 2], stat2[:, sl * 24: sl * 24 + 24]),
                waits=[("dself", sv)], inc="dself")
            sv = P.c("dself")
            P.op("vector", lambda e, sl=sl: e.tensor_scalar(
                sm2[:, sl * 4: sl * 4 + 1], mv2[:, sl * 2: sl * 2 + 1], -1.0, None, ALU.mult),
                waits=[("dself", sv)])
            P.op("vector", lambda e, sl=sl: e.tensor_scalar(
                sm2[:, sl * 4 + 1: sl * 4 + 2], mv2[:, sl * 2 + 1: sl * 2 + 2], EPS, None, ALU.add),
                inc="dself")
            sv = P.c("dself")
            P.op("scalar", lambda e, sl=sl: e.activation(
                sm2[:, sl * 4 + 2: sl * 4 + 3], sm2[:, sl * 4 + 1: sl * 4 + 2], AF.Ln),
                waits=[("dself", sv)])
            P.op("scalar", lambda e, sl=sl: e.activation(
                sm2[:, sl * 4 + 3: sl * 4 + 4], sm2[:, sl * 4 + 2: sl * 4 + 3], AF.Exp, scale=-0.5),
                inc="aro")
            av = P.c("aro")
            P.opd("vector", lambda e, sl=sl, tc=tc: e.tensor_scalar(
                xo_tm[:, sl * D:(sl + 1) * D], acc[:, tc * D:(tc + 1) * D],
                sm2[:, sl * 4: sl * 4 + 1], sm2[:, sl * 4 + 3: sl * 4 + 4],
                ALU.add, ALU.mult),
                waits=[("aro", av)] + xoslot.acquire(tc), inc="dxo")
            dv = P.c("dxo")
            P.op("sync", lambda e, tc=tc, sl=sl: e.dma_start(
                out=xn_own.ap()[tc * 128:(tc + 1) * 128, :],
                in_=xo_tm[:, sl * D:(sl + 1) * D]),
                waits=[("dxo", dv)], inc="xns", amt=16)
            xoslot.release(tc, "xns", P.c("xns"))

        pairs = []
        for h in range(HPC):
            for qt in range(NG):
                nkc = 4 * (qt + 1)
                for kp in range(nkc // 2):
                    pairs.append((h, qt, kp, nkc))

        def emit_av(job):
            h, qt, kp, nkc, pidx, sev, mwaits = job
            kc0, kc1 = 2 * kp, 2 * kp + 1
            par = pidx % 4
            for j, kc in enumerate((kc0, kc1)):
                for qs in range(4):
                    first = (kc == 0)
                    lastmm = (kc == nkc - 1)
                    w = []
                    if qs == 0:
                        w = [("sexp", sev)] + mwaits
                    if first and qs == 0:
                        w = w + [("vone", VONE_ALL)]
                    if first:
                        w = w + avbank.acquire(qs)
                    P.op("tensor", lambda e, qs=qs, par=par, j=j, kc=kc, h=h,
                         first=first, lastmm=lastmm: e.matmul(
                             B4[qs][:, 0:129],
                             Pb[:, par * 1024 + j * 512 + qs * 128:
                                par * 1024 + j * 512 + qs * 128 + 128],
                             Vh[h][:, kc * 129: kc * 129 + 129],
                             start=first, stop=lastmm),
                        waits=w,
                        inc="avp" if (j == 1 and qs == 3) else None)
            pbslot.release(pidx, "avp", P.c("avp"))

        def emit_epilogue(h, qt):
            n = h * NG + qt
            apv = P.c("avp")
            for qs in range(4):
                ssl = (n * 4 + qs) % 2
                sidx = (n * 4 + qs) % 4
                P.op("vector", lambda e, qs=qs, ssl=ssl: e.reciprocal(
                    recs[:, ssl:ssl + 1], B4[qs][:, 128:129]),
                    waits=[("avp", apv)], inc="dself")
                sv2 = P.c("dself")
                stw = stslot.acquire(n * 4 + qs)
                P.opd("vector", lambda e, qs=qs, ssl=ssl, sidx=sidx: e.tensor_scalar(
                    stg[:, sidx * 128:(sidx + 1) * 128], B4[qs][:, 0:128],
                    recs[:, ssl:ssl + 1], None, ALU.mult),
                    waits=[("dself", sv2)] + stw, inc="ep")
                epv = P.c("ep")
                avbank.release(qs, "ep", epv)
                P.op("sync", lambda e, h=h, qt=qt, qs=qs, sidx=sidx: e.dma_start(
                    out=attn_tm.ap()[qt * 512 + qs * 128: qt * 512 + qs * 128 + 128,
                                     h * 128:(h + 1) * 128],
                    in_=stg[:, sidx * 128:(sidx + 1) * 128]),
                    waits=[("ep", epv)], inc="sto", amt=16)
                stslot.release(n * 4 + qs, "sto", P.c("sto"))

        pair_idx = 0
        pending = None
        for h, qt, kp, nkc in pairs:
            kc0, kc1 = 2 * kp, 2 * kp + 1
            par = pair_idx % 4
            Sp = SAp if pair_idx % 2 == 0 else SBb
            bw = sbank.acquire(pair_idx)
            for j, kc in enumerate((kc0, kc1)):
                P.op("tensor", lambda e, Sp=Sp, j=j, kc=kc, h=h, qt=qt: e.matmul(
                    Sp[:, j * 512:(j + 1) * 512],
                    Kh[h][:, kc * 128: kc * 128 + 128],
                    Qh[h][:, qt * 512: qt * 512 + 512],
                    start=True, stop=True),
                    waits=([("evq", EVQ_ALL)] + bw) if j == 0 else [],
                    inc="smm" if j == 1 else None)
            smv = P.c("smm")
            pw_ = pbslot.acquire(pair_idx)
            P.op("scalar", lambda e, Sp=Sp, par=par: e.activation(
                Pb[:, par * 1024:(par + 1) * 1024], Sp[:, :], AF.Exp,
                bias=negC[:, :], scale=ISCALE),
                waits=[("smm", smv), ("su", SU_ALL), ("su2", MASK_V)] + pw_,
                inc="sexp")
            sev = P.c("sexp")
            sbank.release(pair_idx, "sexp", sev)
            mwaits = []
            for j, kc in enumerate((kc0, kc1)):
                c = kc - 4 * qt
                if 0 <= c <= 3:
                    P.op("vector", lambda e, par=par, j=j, c=c: e.tensor_tensor(
                        Pb[:, par * 1024 + j * 512: par * 1024 + j * 512 + 512],
                        Pb[:, par * 1024 + j * 512: par * 1024 + j * 512 + 512],
                        maskb[:, c * 512:(c + 1) * 512], ALU.mult),
                        waits=[("sexp", sev), ("su2", MASK_V)], inc="smsk")
                    mwaits = [("smsk", P.c("smsk"))]
            if pending is not None:
                emit_av(pending)
                if (pending[0], pending[1]) != (h, qt):
                    emit_epilogue(pending[0], pending[1])
            pending = (h, qt, kp, nkc, pair_idx, sev, mwaits)
            pair_idx += 1
        emit_av(pending)
        emit_epilogue(pending[0], pending[1])

        if phases >= 2.2:
            for tc_ in range(OCH):
                emit_own_ln(tc_)
        STO_ALL = P.c("sto")
        assert STO_ALL == 32 * 4 * 16
        XNS_ALL = P.c("xns")
        EP_ALL = P.c("ep")
        if debug & 8:
            P.op("gpsimd", lambda e: e.dma_start(out=dbg["l"], in_=lsave[:, :]),
                 waits=[("dbgp", P.c("dbgp")), ("ep", EP_ALL)], inc="dbgs", amt=16)
        if phases < 2.2:
            P.op("sync", lambda e: e.dma_start(out=out, in_=xo),
                 waits=[("sto", STO_ALL), ("ep", EP_ALL)], inc="outd", amt=16)
            _emit(nc, P, [("outd", P.c("outd")), ("dbgs", P.c("dbgs"))])
            _close_all()
            return nc
        assert XNS_ALL == OCH * 16

        qkv_scope.close()
        r2_scope = ExitStack()
        inner_scopes.append(r2_scope)
        xnfo = r2_scope.enter_context(nc.sbuf_tensor("xnfo", [128, 16 * 1024], FP16))
        p25_scope = ExitStack()
        inner_scopes.append(p25_scope)
        SB25 = lambda n, s, d: p25_scope.enter_context(nc.sbuf_tensor(n, s, d))
        attnfm = SB25("attnfm", [128, 4 * SEQ], FP16)
        woslab = SB25("woslab", [128, 4 * D], FP16)
        rsbuf = SB25("rsbuf", [128, 4 * 512], FP16)

        for dk in range(16):
            P.op("sync", lambda e, dk=dk: e.dma_start_transpose(
                out=xnfo[:, dk * 1024:(dk + 1) * 1024],
                in_=xn_own.ap()[:, dk * 128:(dk + 1) * 128]),
                waits=[("xns", XNS_ALL), ("ep", EP_ALL)], inc="trx", amt=16)
        TRX_ALL = P.c("trx")
        if phases < 2.3:
            P.op("sync", lambda e: e.dma_start(out=out, in_=xo),
                 waits=[("sto", STO_ALL), ("trx", TRX_ALL)], inc="outd", amt=16)
            _emit(nc, P, [("outd", P.c("outd")), ("dbgs", P.c("dbgs"))])
            _close_all()
            return nc

        for fk in range(4):
            P.op("sync", lambda e, fk=fk: e.dma_start_transpose(
                out=attnfm[:, fk * SEQ:(fk + 1) * SEQ],
                in_=attn_tm.ap()[:, fk * 128:(fk + 1) * 128]),
                waits=[("sto", STO_ALL), ("ep", EP_ALL)], inc="p25", amt=16)
        P25_ALL = P.c("p25")
        if debug & 2:
            P.op("gpsimd", lambda e: e.dma_start(out=dbg["attn"], in_=attn_tm.ap()),
                 waits=[("sto", STO_ALL)], inc="dbgs", amt=16)

        P.op("gpsimd", lambda e: e.dma_start(
            out=woslab.ap().rearrange("p (c n) -> p c n", n=D),
            in_=woTs.rearrange("(c p) n -> p c n", p=128)),
            waits=[("ep", EP_ALL)], inc="wsl", amt=16)
        WO_V = P.c("wsl")

        obank = avbank
        oslot = Slots(4)
        for tt in range(NCH):
            for ot in range(4):
                oi = tt * 4 + ot
                bank = B4[oi % 4]
                bw = obank.acquire(oi)
                for fk in range(4):
                    P.op("tensor", lambda e, bank=bank, fk=fk, tt=tt, ot=ot: e.matmul(
                        bank[:, :],
                        attnfm[:, fk * SEQ + tt * 128: fk * SEQ + tt * 128 + 128],
                        woslab[:, fk * D + ot * 512: fk * D + ot * 512 + 512],
                        start=(fk == 0), stop=(fk == 3)),
                        waits=([("p25", P25_ALL), ("wsl", WO_V)] + bw) if fk == 0 else [],
                        inc="omm" if fk == 3 else None)
                omv = P.c("omm")
                sw = oslot.acquire(oi)
                if oi % 2 == 0:
                    P.opd("scalar", lambda e, bank=bank, osl=oi % 4: e.activation(
                        rsbuf[:, osl * 512:(osl + 1) * 512], bank[:, :], AF.Identity),
                        waits=[("omm", omv)] + sw, inc="oev")
                else:
                    P.opd("vector", lambda e, bank=bank, osl=oi % 4: e.tensor_copy(
                        rsbuf[:, osl * 512:(osl + 1) * 512], bank[:, :]),
                        waits=[("omm", omv)] + sw, inc="oev")
                oev = P.c("oev")
                obank.release(oi, "oev", oev)
                P.op("sync", lambda e, tt=tt, ot=ot, osl=oi % 4: e.dma_start(
                    out=rs_in.ap()[tt * 128:(tt + 1) * 128, ot * 512:(ot + 1) * 512],
                    in_=rsbuf[:, osl * 512:(osl + 1) * 512]),
                    waits=[("oev", oev)], inc="rsst", amt=16)
                oslot.release(oi, "rsst", P.c("rsst"))
        RSST_ALL = P.c("rsst")
        OMM_ALL = P.c("omm")
        if phases < 2.4:
            P.op("sync", lambda e: e.dma_start(out=out, in_=xo),
                 waits=[("rsst", RSST_ALL)], inc="outd", amt=16)
            _emit(nc, P, [("outd", P.c("outd")), ("dbgs", P.c("dbgs"))])
            _close_all()
            return nc

        if phases < 3:
            P.op("gpsimd", lambda e: e.collective_compute(
                "ReduceScatter", ALU.add,
                replica_groups=[[0, 1, 2, 3], [4, 5, 6, 7]],
                ins=[rs_in.ap().opt()], outs=[rs_out_d.ap().opt()]),
                waits=[("rsst", RSST_ALL)], inc="cc")
            CC_V = P.c("cc")
        if phases < 3:
            if debug & 2:
                P.op("gpsimd", lambda e: e.dma_start(out=dbg["rs"], in_=rs_out_d.ap()),
                     waits=[("cc", CC_V)], inc="dbgs", amt=16)
            P.op("sync", lambda e: e.dma_start(out=out, in_=xo),
                 waits=[("rsst", RSST_ALL)], inc="outd", amt=16)
            _emit(nc, P, [("outd", P.c("outd")), ("dbgs", P.c("dbgs")), ("cc", CC_V)])
            _close_all()
            return nc

        p25_scope.close()
        p3_scope = ExitStack()
        inner_scopes.append(p3_scope)
        SB3 = lambda n, s, d: p3_scope.enter_context(nc.sbuf_tensor(n, s, d))
        SLABW = 17 * 512
        wsl = SB3("wsl", [128, 2 * SLABW], FP16)
        NT2 = 4
        lb = SB3("lb", [128, 4 * 1024], FP16)
        m1b = SB3("m1b", [128, NT2 * 128], FP16)
        m2b = SB3("m2b", [128, NT2 * 128], FP16)
        s12 = SB3("s12", [128, NT2 * 128], FP16)
        ge1b = SB3("ge1b", [128, NT2 * 1024], FP16)
        lmb = SB3("lmb", [128, 1024], FP16)
        ub = SB3("ub", [128, NT2 * 1024], FP16)
        sgb = SB3("sgb", [128, NT2 * 1024], FP16)
        wb_ = SB3("wb", [128, 1024], FP16)
        tsc = SB3("tsc", [128, 1024], FP16)
        cw = SB3("cw", [128, 4 * 1024], FP16)
        rtmp = SB3("rtmp", [128, 2 * 16], F32)
        ebpb = SB3("ebpb", [128, D], FP16)

        P3GATE = [("omm", OMM_ALL)]
        P.op("gpsimd", lambda e: e.dma_start(out=ebpb[:, :], in_=ebp_i),
             waits=P3GATE, inc="su2", amt=16)
        SU2_V = P.c("su2")
        P.op("gpsimd", lambda e: e.memset(cw[:, :], 0.0), waits=P3GATE, inc="pw")

        p3bank = Slots(8)
        p3bank.free_at[0] = p3bank.free_at[1] = sbank.free_at[0]
        p3bank.free_at[2] = p3bank.free_at[3] = sbank.free_at[1]
        for i in range(4):
            p3bank.free_at[4 + i] = obank.free_at[i]

        wslslot = Slots(2)
        slab_ready = {}

        def load_router_slab(ct):
            slw = wslslot.acquire(ct) + (P3GATE if ct < 2 else [])
            P.op("gpsimd", lambda e, ct=ct: e.dma_start(
                out=wsl.ap().rearrange("p (s c n) -> p s c n", s=2, n=512)[:, ct % 2],
                in_=wrT[:, ct * 512:(ct + 1) * 512].rearrange("(c p) n -> p c n", p=128)),
                waits=slw, inc="wsl", amt=16)
            slab_ready[ct] = P.c("wsl")

        load_router_slab(0)
        load_router_slab(1)
        CC_V = None

        lbslot = Slots(4)
        t2slot = Slots(NT2)
        tile_idx = 0
        pair_cnt = 0

        for ct in range(NCT):
            for tc in range(OCH):
                bank = BANKS[tc]
                bw = p3bank.acquire(tc)
                slv = slab_ready[ct]
                for dk in range(16):
                    P.op("tensor", lambda e, bank=bank, dk=dk, ct=ct, tc=tc: e.matmul(
                        bank,
                        xnfo[:, dk * 1024 + tc * 128: dk * 1024 + tc * 128 + 128],
                        wsl[:, (ct % 2) * SLABW + dk * 512:(ct % 2) * SLABW + dk * 512 + 512],
                        start=(dk == 0), stop=False),
                        waits=([("trx", TRX_ALL), ("wsl", slv)] + bw) if dk == 0 else [])
                P.op("tensor", lambda e, bank=bank, ct=ct: e.matmul(
                    bank, onesM[:, :],
                    wsl[:, (ct % 2) * SLABW + 16 * 512:(ct % 2) * SLABW + 17 * 512],
                    start=False, stop=True),
                    inc="rmm")
                rmv = P.c("rmm")
                half = tile_idx % 2
                psl = (tile_idx // 2) % 4
                lw = lbslot.acquire(tile_idx // 2) if half == 0 else []
                P.opd("scalar", lambda e, bank=bank, psl=psl, half=half: e.activation(
                    lb[:, psl * 1024 + half * 512: psl * 1024 + half * 512 + 512],
                    bank, AF.Identity),
                    waits=[("rmm", rmv)] + lw, inc="rev")
                p3bank.release(tc, "rev", P.c("rev"))
                tile_idx += 1

                if half == 1:
                    pi = pair_cnt
                    pair_cnt += 1
                    revv = P.c("rev")
                    dbg_lb_hold = None
                    if debug & 16 and pi == 0:
                        P.op("gpsimd", lambda e: e.dma_start(
                            out=dbg["lg"], in_=lb[:, 0:1024]),
                            waits=[("rev", 2)], inc="dbgp", amt=16)
                        dbg_lb_hold = P.c("dbgp")
                    ts = pi % NT2
                    L = lb[:, psl * 1024:(psl + 1) * 1024]
                    Lv = lb.ap().rearrange("p (s g e) -> p s g e", s=4, e=8)[:, psl]
                    M1 = m1b[:, ts * 128: ts * 128 + 128]
                    M2 = m2b[:, ts * 128: ts * 128 + 128]
                    S12 = s12[:, ts * 128: ts * 128 + 128]
                    GE = ge1b[:, ts * 1024: ts * 1024 + 1024]
                    LM = lmb[:, 0:1024]
                    U = ub[:, ts * 1024: ts * 1024 + 1024]
                    SG = sgb[:, ts * 1024: ts * 1024 + 1024]
                    W = wb_[:, 0:1024]
                    TSC = tsc[:, 0:1024]
                    tw = t2slot.acquire(pi)

                    P.op("vector", lambda e, Lv=Lv, M1=M1: e.reduce_max(M1, Lv, axis=AX.X),
                         waits=[("rev", revv)] + tw, inc="dself")
                    sv = P.c("dself")
                    P.op("vector", lambda e, L=L, M1=M1, GE=GE: e.tensor_tensor(
                        GE.rearrange("p (g e) -> p g e", e=8),
                        L.rearrange("p (g e) -> p g e", e=8),
                        M1.unsqueeze(2).broadcast_to([128, 128, 8]), ALU.is_ge),
                        waits=[("dself", sv)])
                    P.op("vector", lambda e, GE=GE, TSC=TSC: e.tensor_scalar(
                        TSC, GE, -BIG, None, ALU.mult))
                    P.op("vector", lambda e, L=L, TSC=TSC, LM=LM: e.tensor_tensor(
                        LM, TSC, L, ALU.add), inc="dself")
                    sv = P.c("dself")
                    P.op("vector", lambda e, LM=LM, M2=M2: e.reduce_max(
                        M2, LM.rearrange("p (g e) -> p g e", e=8), axis=AX.X),
                        waits=[("dself", sv)], inc="dself")
                    sv = P.c("dself")
                    P.op("vector", lambda e, M1=M1, M2=M2, S12=S12: e.tensor_tensor(
                        S12, M1, M2, ALU.add),
                        waits=[("dself", sv)], inc="dself")
                    sv = P.c("dself")
                    P.op("vector", lambda e, L=L, TSC=TSC: e.tensor_scalar(
                        TSC, L, 2.0, None, ALU.mult),
                        waits=[("dself", sv)])
                    P.opd("vector", lambda e, TSC=TSC, S12=S12, U=U: e.tensor_tensor(
                        U.rearrange("p (g e) -> p g e", e=8),
                        TSC.rearrange("p (g e) -> p g e", e=8),
                        S12.unsqueeze(2).broadcast_to([128, 128, 8]), ALU.subtract),
                        inc="du")
                    duv = P.c("du")
                    P.opd("vector", lambda e, L=L, M2=M2, GE=GE: e.tensor_tensor(
                        GE.rearrange("p (g e) -> p g e", e=8),
                        L.rearrange("p (g e) -> p g e", e=8),
                        M2.unsqueeze(2).broadcast_to([128, 128, 8]), ALU.is_ge),
                        inc="dge2")
                    gev = P.c("dge2")
                    P.opd("scalar", lambda e, U=U, SG=SG: e.activation(SG, U, AF.Sigmoid),
                          waits=[("du", duv)], inc="asig")
                    sgv = P.c("asig")
                    P.op("gpsimd", lambda e, SG=SG, GE=GE, W=W: e.tensor_tensor(
                        W, SG, GE, ALU.mult),
                        waits=[("asig", sgv), ("dge2", gev)])
                    P.opd("gpsimd", lambda e, W=W, tcp=(tc // 2): e.tensor_tensor(
                        cw[:, tcp * 1024:(tcp + 1) * 1024],
                        cw[:, tcp * 1024:(tcp + 1) * 1024], W, ALU.add),
                        inc="pw", amt=2)
                    pwv = P.c("pw")
                    if debug & 16 and pi == 0:
                        P.op("gpsimd", lambda e, W=W: e.dma_start(
                            out=dbg["t2"][:, 0:1024], in_=W), waits=[("pw", pwv)],
                            inc="dbgp", amt=16)
                        P.op("gpsimd", lambda e, M1=M1: e.dma_start(
                            out=dbg["t2"][:, 1024:1152], in_=M1), inc="dbgp", amt=16)
                        P.op("gpsimd", lambda e, M2=M2: e.dma_start(
                            out=dbg["t2"][:, 1152:1280], in_=M2), inc="dbgp", amt=16)
                        P.op("gpsimd", lambda e, U=U: e.dma_start(
                            out=dbg["t2"][:, 1280:1536], in_=U[:, 0:256]), inc="dbgp", amt=16)
                        t2_hold = P.c("dbgp")
                    t2slot.release(pi, "pw", pwv)
                    lbslot.release(pi, "pw", pwv)
                    if dbg_lb_hold is not None:
                        lbslot.release(pi, "dbgp", dbg_lb_hold)
                    if debug & 16 and pi == 0:
                        t2slot.release(pi, "dbgp", t2_hold)
            wslslot.release(ct, "rmm", P.c("rmm"))
            if ct + 2 < NCT:
                load_router_slab(ct + 2)
            if ct == 3:
                P.op("gpsimd", lambda e: e.collective_compute(
                    "ReduceScatter", ALU.add,
                    replica_groups=[[0, 1, 2, 3], [4, 5, 6, 7]],
                    ins=[rs_in.ap().opt()], outs=[rs_out_d.ap().opt()]),
                    waits=[("rsst", RSST_ALL)], inc="cc")
        RMM_ALL = P.c("rmm")
        assert RMM_ALL == NCT * OCH
        CC_V = P.c("cc")
        PW_ALL = P.c("pw")
        for tcp in range(4):
            for hh in range(2):
                P.op("vector", lambda e, tcp=tcp, hh=hh: e.reduce_sum(
                    rtmp[:, hh * 8: hh * 8 + 8],
                    cw.ap().rearrange("p (q s g e) -> p q s e g", q=4, s=2, e=8)[:, tcp, hh],
                    axis=AX.X),
                    waits=[("pw", PW_ALL)], inc="dself")
                sv = P.c("dself")
                P.op("vector", lambda e, tcp=tcp, hh=hh: e.tensor_tensor(
                    c_all[:, (tcp * 2 + hh) * 8: (tcp * 2 + hh) * 8 + 8],
                    c_all[:, (tcp * 2 + hh) * 8: (tcp * 2 + hh) * 8 + 8],
                    rtmp[:, hh * 8: hh * 8 + 8], ALU.add),
                    waits=[("dself", sv), ("vone", VONE_ALL)], inc="lrel")
        LREL_ALL = P.c("lrel")

        if debug & 4:
            P.op("gpsimd", lambda e: e.dma_start(out=dbg["c"], in_=c_all[:, :]),
                 waits=[("lrel", LREL_ALL)], inc="dbgs", amt=16)

        P.op("vector", lambda e: e.memset(cT[:, :], 0.0), waits=R1GATE)
        P.op("vector", lambda e: e.memset(cT[64:65, :], 1.0))
        P.op("vector", lambda e: e.tensor_copy(cbf[:, :], c_all[:, :]),
             waits=[("lrel", LREL_ALL)], inc="dcb")
        DCB_V = P.c("dcb")
        REV_ALL = P.c("rev")
        ctslot = Slots(2)
        for tc in range(OCH):
            P.op("tensor", lambda e, tc=tc: e.transpose(
                SBb16[0:8, (tc % 2) * 1024:(tc % 2) * 1024 + 128],
                cbf[:, tc * 8: tc * 8 + 8], identb[:, :]),
                waits=[("dcb", DCB_V), ("rev", REV_ALL)] + ctslot.acquire(tc), inc="tcT")
            tv = P.c("tcT")
            P.op("vector", lambda e, tc=tc: e.tensor_copy(
                cT[0:8, tc * 128:(tc + 1) * 128],
                SBb.ap().bitcast(FP16)[0:8, (tc % 2) * 1024:(tc % 2) * 1024 + 128]),
                waits=[("tcT", tv)], inc="dcT")
            ctslot.release(tc, "dcT", P.c("dcT"))
        DCT_ALL = P.c("dcT")

        for ot in range(4):
            for tc in range(OCH):
                bank = BANKS[tc]
                bw = p3bank.acquire(tc)
                P.op("tensor", lambda e, bank=bank, tc=tc, ot=ot: e.matmul(
                    bank, cT[:, tc * 128:(tc + 1) * 128],
                    ebpb[:, ot * 512:(ot + 1) * 512],
                    start=True, stop=True),
                    waits=[("dcT", DCT_ALL), ("su2", SU2_V)] + bw,
                    inc="emm")
                emv = P.c("emm")
                P.op("vector", lambda e, bank=bank, tc=tc, ot=ot: e.tensor_tensor(
                    acc[:, tc * D + ot * 512: tc * D + ot * 512 + 512],
                    acc[:, tc * D + ot * 512: tc * D + ot * 512 + 512],
                    bank, ALU.add),
                    waits=[("emm", emv)], inc="eev")
                p3bank.release(tc, "eev", P.c("eev"))

        def load_expert_slab(s):
            ei, ot = s // 4, s % 4
            G = NCT + s
            slw = wslslot.acquire(G)
            P.op("gpsimd", lambda e, ei=ei, ot=ot, G=G: e.dma_start(
                out=wsl.ap().rearrange("p (s c n) -> p s c n", s=2, n=512)[:, G % 2, 0:16],
                in_=weT[ei, :, ot * 512:(ot + 1) * 512].rearrange("(c p) n -> p c n", p=128)),
                waits=slw, inc="wsl", amt=16)
            slab_ready[G] = P.c("wsl")

        load_expert_slab(0)
        load_expert_slab(1)
        for s in range(32):
            ei, ot = s // 4, s % 4
            G = NCT + s
            slv = slab_ready[G]
            for tc in range(OCH):
                bank = BANKS[tc]
                bw = p3bank.acquire(tc)
                for dk in range(16):
                    P.op("tensor", lambda e, bank=bank, dk=dk, G=G, tc=tc: e.matmul(
                        bank,
                        xnfo[:, dk * 1024 + tc * 128: dk * 1024 + tc * 128 + 128],
                        wsl[:, (G % 2) * SLABW + dk * 512:(G % 2) * SLABW + dk * 512 + 512],
                        start=(dk == 0), stop=(dk == 15)),
                        waits=([("wsl", slv)] + bw) if dk == 0 else [],
                        inc="emm" if dk == 15 else None)
                emv = P.c("emm")
                P.op("vector", lambda e, bank=bank, tc=tc, ot=ot, ei=ei: e.scalar_tensor_tensor(
                    acc[:, tc * D + ot * 512: tc * D + ot * 512 + 512],
                    bank, c_all[:, tc * 8 + ei: tc * 8 + ei + 1],
                    acc[:, tc * D + ot * 512: tc * D + ot * 512 + 512],
                    ALU.mult, ALU.add),
                    waits=[("emm", emv)], inc="eev")
                p3bank.release(tc, "eev", P.c("eev"))
            wslslot.release(G, "emm", P.c("emm"))
            if s + 2 < 32:
                load_expert_slab(s + 2)
        EEV_ALL = P.c("eev")
        assert EEV_ALL == 32 + 256

        if debug & 4:
            P.op("gpsimd", lambda e: e.dma_start(out=dbg["acc"], in_=acc[:, :]),
                 waits=[("eev", EEV_ALL)], inc="dbgs", amt=16)
        if debug & 2:
            P.op("gpsimd", lambda e: e.dma_start(out=dbg["rs"], in_=rs_out_d.ap()),
                 waits=[("cc", CC_V)], inc="dbgs", amt=16)

        if phases < 4:
            P.op("sync", lambda e: e.dma_start(out=out, in_=xo),
                 waits=[("eev", EEV_ALL)], inc="outd", amt=16)
            _emit(nc, P, [("outd", P.c("outd")), ("dbgs", P.c("dbgs")), ("cc", CC_V)])
            _close_all()
            return nc

        p3_scope.close()
        p4_scope = ExitStack()
        inner_scopes.append(p4_scope)
        rso = p4_scope.enter_context(nc.sbuf_tensor("rso", [128, 2 * D], FP16))
        fout = p4_scope.enter_context(nc.sbuf_tensor("fout", [128, 2 * D], F32))
        rsoslot = Slots(2)
        for tc in range(OCH):
            sl = tc % 2
            P.op("sync", lambda e, tc=tc, sl=sl: e.dma_start(
                out=rso[:, sl * D:(sl + 1) * D],
                in_=rs_out_d.ap()[tc * 128:(tc + 1) * 128, :]),
                waits=[("cc", CC_V), ("eev", EEV_ALL)] + rsoslot.acquire(tc),
                inc="rsod", amt=16)
            rv = P.c("rsod")
            P.opd("vector", lambda e, tc=tc, sl=sl: e.tensor_tensor(
                fout[:, sl * D:(sl + 1) * D], rso[:, sl * D:(sl + 1) * D],
                acc[:, tc * D:(tc + 1) * D], ALU.add),
                waits=[("rsod", rv), ("eev", EEV_ALL)], inc="fev")
            fv = P.c("fev")
            P.op("sync", lambda e, tc=tc, sl=sl: e.dma_start(
                out=out[tc * 128:(tc + 1) * 128, :], in_=fout[:, sl * D:(sl + 1) * D]),
                waits=[("fev", fv)], inc="outd", amt=16)
            rsoslot.release(tc, "outd", P.c("outd"))
        OUTD_ALL = P.c("outd")
        assert OUTD_ALL == OCH * 16

        _emit(nc, P, [("outd", OUTD_ALL), ("dbgs", P.c("dbgs")), ("cc", CC_V)])
        _close_all()
    return nc


def _emit(nc, P, final_waits):
    global LAST_PROG
    LAST_PROG = (P, final_waits)
    with nc.Block() as block:
        @block.tensor
        def _(e):
            P.run("tensor", e)

        @block.vector
        def _(e):
            P.run("vector", e)

        @block.scalar
        def _(e):
            P.run("scalar", e)

        @block.gpsimd
        def _(e):
            P.run("gpsimd", e)

        @block.sync
        def _(e):
            P.run("sync", e, final_waits=final_waits)



def _mk_masks():
    m = np.zeros((128, 4 * 512), dtype=np.float32)
    f = np.arange(512)
    for c in range(4):
        m[:, c * 512:(c + 1) * 512] = (f[None, :] >= (c * 128 + np.arange(128))[:, None])
    return m


def kernel(L0_attn_x, L0_attn_w_qkv, L0_attn_b_qkv, L0_attn_w_out, L0_attn_b_out,
           L0_ffn_router_w, L0_ffn_router_b, L0_ffn_expert_w, L0_ffn_expert_b):
    x = np.ascontiguousarray(np.asarray(L0_attn_x, dtype=np.float32))
    w_qkv = np.asarray(L0_attn_w_qkv, dtype=np.float32)
    b_qkv = np.asarray(L0_attn_b_qkv, dtype=np.float32)
    w_out = np.asarray(L0_attn_w_out, dtype=np.float32)
    b_out = np.asarray(L0_attn_b_out, dtype=np.float32)
    router_w = np.asarray(L0_ffn_router_w, dtype=np.float32)
    router_b = np.asarray(L0_ffn_router_b, dtype=np.float32)
    expert_w = np.asarray(L0_ffn_expert_w, dtype=np.float32)
    expert_b = np.asarray(L0_ffn_expert_b, dtype=np.float32)

    wrT = np.concatenate([router_w.T, router_b[None, :],
                          np.zeros((127, E * D), dtype=np.float32)], axis=0)
    wrT = np.ascontiguousarray(wrT.astype(np.float16))
    weT = np.ascontiguousarray(expert_w.transpose(0, 2, 1).astype(np.float16))
    ebp = np.zeros((128, D), dtype=np.float32)
    ebp[0:8] = expert_b
    ebp[64] = b_out
    woT = np.ascontiguousarray(w_out.T)
    masks = _mk_masks()
    ident = np.eye(128, dtype=np.float32)
    consts = np.full((128, 1), -CEXP, dtype=np.float32)

    in_maps = []
    for core in range(8):
        b, r = core // 4, core % 4
        hs0 = r * HPC
        q_rows = w_qkv[hs0 * HD:(hs0 + HPC) * HD]
        k_rows = w_qkv[2048 + hs0 * HD: 2048 + (hs0 + HPC) * HD]
        v_rows = w_qkv[4096 + hs0 * HD: 4096 + (hs0 + HPC) * HD]
        wcat = np.concatenate([q_rows, k_rows, v_rows], axis=0)
        bvpad = np.zeros((128, 1536), dtype=np.float32)
        bvpad[0, 1024:1536] = b_qkv[4096 + hs0 * HD: 4096 + (hs0 + HPC) * HD]
        wqkvT = np.ascontiguousarray(np.concatenate([wcat.T, bvpad], axis=0).astype(np.float16))
        bq = b_qkv[hs0 * HD:(hs0 + HPC) * HD].reshape(4, 128).T
        bk = b_qkv[2048 + hs0 * HD: 2048 + (hs0 + HPC) * HD].reshape(4, 128).T
        bqkv_fm = np.ascontiguousarray(np.concatenate([bq, bk], axis=1))
        woTs = np.ascontiguousarray(woT[hs0 * HD:(hs0 + HPC) * HD, :].astype(np.float16))
        xb_c = np.ascontiguousarray(x[b])
        xo_c = np.ascontiguousarray(x[b, r * 1024:(r + 1) * 1024])
        in_maps.append(dict(
            xb=xb_c, xo=xo_c, wqkvT=wqkvT, bqkv_fm=bqkv_fm,
            woTs=woTs, wrT=wrT, weT=weT, ebp=ebp,
            masks=masks, ident=ident, consts=consts,
        ))

    nc = build(debug=DEBUG, phases=PHASES)
    res = run_bass_kernel_spmd(nc, in_maps, core_ids=list(range(8)))

    full = np.empty((2, SEQ, D), dtype=np.float32)
    for core in range(8):
        b, r = core // 4, core % 4
        full[b, r * 1024:(r + 1) * 1024] = res.results[core]["out"]
    kernel.last_results = res
    return full


# version 12
# speedup vs baseline: 1.0697x; 1.0024x over previous
import math
from contextlib import ExitStack

import numpy as np
import ml_dtypes

import concourse.bass as bass
import concourse.mybir as mybir
from concourse.bass_utils import run_bass_kernel_spmd

F32 = mybir.dt.float32
BF16 = mybir.dt.bfloat16
FP16 = mybir.dt.float16
AF = mybir.ActivationFunctionType
ALU = mybir.AluOpType
AX = mybir.AxisListType

D = 2048
SEQ = 4096
HD = 128
HPC = 4
E = 8
NCH = 32
NG = 8
OCH = 8
NCT = 32
CEXP = 12.0
BIG = 60000.0
EPS = 1e-5
ISCALE = 1.0 / math.sqrt(HD)

ENGS = ("tensor", "vector", "scalar", "gpsimd", "sync")

PHASES = 4
LAST_PROG = None
NO_SMALLK = False
DEBUG = 0


class Prog:
    def __init__(self, nc, ctx):
        self.nc = nc
        self.q = {e: [] for e in ENGS}
        self.cnt = {}
        self.sems = {}
        self.ctx = ctx

    def sem(self, name):
        h = self.ctx.enter_context(self.nc.semaphore(name))
        self.sems[name] = h
        self.cnt[name] = 0

    def op(self, eng, fn, waits=(), inc=None, amt=1):
        ws = tuple((s, v) for (s, v) in waits if s is not None and v and v > 0)
        if inc is not None:
            self.cnt[inc] += amt
        self.q[eng].append((ws, fn, inc, amt))
        return self.cnt[inc] if inc is not None else None

    def c(self, sem):
        return self.cnt[sem]

    def opd(self, eng, fn, waits=(), inc=None, amt=1):
        self.op(eng, fn, waits=waits)
        return self.op(eng, lambda e: e.drain(), inc=inc, amt=amt)

    def run(self, eng, e, final_waits=()):
        seen = {}
        for ws, fn, inc, amt in self.q[eng]:
            for s, v in ws:
                if seen.get(s, 0) >= v:
                    continue
                seen[s] = v
                e.wait_ge(self.sems[s], v)
            ins = fn(e)
            if inc is not None:
                ins.then_inc(self.sems[inc], amt)
        for s, v in final_waits:
            if v > 0 and seen.get(s, 0) < v:
                e.wait_ge(self.sems[s], v)


class Slots:
    def __init__(self, n):
        self.n = n
        self.free_at = [None] * n

    def acquire(self, i):
        w = self.free_at[i % self.n]
        return [w] if w is not None else []

    def release(self, i, sem, val):
        self.free_at[i % self.n] = (sem, val)


def build(debug=0, phases=4, no_smallk=None):
    if no_smallk is None:
        no_smallk = NO_SMALLK
    nc = bass.Bass("TRN2", target_bir_lowering=False, debug=False)

    xb = nc.dram_tensor("xb", [SEQ, D], F32, kind="ExternalInput").ap()
    xo = nc.dram_tensor("xo", [1024, D], F32, kind="ExternalInput").ap()
    wqkvT = nc.dram_tensor("wqkvT", [D + 128, 1536], FP16, kind="ExternalInput").ap()
    bqkv_fm = nc.dram_tensor("bqkv_fm", [128, 8], F32, kind="ExternalInput").ap()
    woTs = nc.dram_tensor("woTs", [512, D], FP16, kind="ExternalInput").ap()
    wrT = nc.dram_tensor("wrT", [D + 128, E * D], FP16, kind="ExternalInput").ap()
    weT = nc.dram_tensor("weT", [E, D, D], FP16, kind="ExternalInput").ap()
    ebp_i = nc.dram_tensor("ebp", [128, D], F32, kind="ExternalInput").ap()
    masks_i = nc.dram_tensor("masks", [128, 4 * 512], F32, kind="ExternalInput").ap()
    ident_i = nc.dram_tensor("ident", [128, 128], F32, kind="ExternalInput").ap()
    consts_i = nc.dram_tensor("consts", [128, 1], F32, kind="ExternalInput").ap()

    out = nc.dram_tensor("out", [1024, D], F32, kind="ExternalOutput").ap()

    attn_tm = nc.dram_tensor("attn_tm", [SEQ, HPC * HD], FP16)
    xn_own = nc.dram_tensor("xn_own", [1024, D], FP16)
    rs_in = nc.dram_tensor("rs_in", [SEQ, D], FP16)
    rs_out_d = nc.dram_tensor("rs_out", [1024, D], FP16)

    dbg = {}
    if debug & 1:
        dbg["q"] = nc.dram_tensor("q_dbg", [512, SEQ], F32, kind="ExternalOutput").ap()
        dbg["k"] = nc.dram_tensor("k_dbg", [512, SEQ], F32, kind="ExternalOutput").ap()
        dbg["v"] = nc.dram_tensor("v_dbg", [512, 32 * 129], F32, kind="ExternalOutput").ap()
    if debug & 2:
        dbg["attn"] = nc.dram_tensor("attn_dbg", [SEQ, HPC * HD], F32, kind="ExternalOutput").ap()
        dbg["rs"] = nc.dram_tensor("rs_dbg", [1024, D], F32, kind="ExternalOutput").ap()
    if debug & 16:
        dbg["lg"] = nc.dram_tensor("lg_dbg", [128, 1024], F32, kind="ExternalOutput").ap()
        dbg["t2"] = nc.dram_tensor("t2_dbg", [128, 1536], F32, kind="ExternalOutput").ap()
    if debug & 8:
        dbg["p"] = nc.dram_tensor("p_dbg", [128, 1024], F32, kind="ExternalOutput").ap()
        dbg["l"] = nc.dram_tensor("l_dbg", [128, 8], F32, kind="ExternalOutput").ap()
        dbg["s"] = nc.dram_tensor("s_dbg", [128, 1024], F32, kind="ExternalOutput").ap()
        dbg["av"] = nc.dram_tensor("av_dbg", [128, 1024], F32, kind="ExternalOutput").ap()
    if debug & 4:
        dbg["c"] = nc.dram_tensor("c_dbg", [128, 64], F32, kind="ExternalOutput").ap()
        dbg["acc"] = nc.dram_tensor("acc_dbg", [128, OCH * D], F32, kind="ExternalOutput").ap()

    ctx = ExitStack()
    inner_scopes = []

    def _close_all():
        for sc in reversed(inner_scopes):
            sc.close()

    with ctx:
        P = Prog(nc, ctx)
        SBP = lambda n, s, d: ctx.enter_context(nc.sbuf_tensor(n, s, d))

        identb = SBP("identb", [128, 128], FP16)
        onesM = SBP("onesM", [128, 128], FP16)
        bqv = SBP("bqv", [128, 8], F32)
        negC = SBP("negC", [128, 1], F32)
        c_all = SBP("c_all", [128, 64], F32)
        cbf = SBP("cbf", [128, 64], FP16)
        recs = SBP("recs", [128, 2], F32)
        if debug & 8:
            lsave = SBP("lsave", [128, 8], F32)
            ssave = SBP("ssave", [128, 1024], F32)

        qkv_scope = ExitStack()
        inner_scopes.append(qkv_scope)
        SBq = lambda n, s, d: qkv_scope.enter_context(nc.sbuf_tensor(n, s, d))
        Qh = [SBq(f"Qh{h}", [128, SEQ], FP16) for h in range(HPC)]
        Kh = [SBq(f"Kh{h}", [128, SEQ], FP16) for h in range(HPC)]
        Vh = [SBq(f"Vh{h}", [128, 32 * 129], FP16) for h in range(HPC)]

        p1_scope = ExitStack()
        inner_scopes.append(p1_scope)
        SB1 = lambda n, s, d: p1_scope.enter_context(nc.sbuf_tensor(n, s, d, side="right"))
        x_sl = SB1("x_sl", [128, 2 * D], F32)
        stat1 = SB1("stat1", [128, 2 * 24], F32)
        mv1 = SB1("mv1", [128, 2 * 2], F32)
        sm1 = SB1("sm1", [128, 2 * 6], F32)
        xn_tm = SB1("xn_tm", [128, 3 * D], FP16)
        wqT = SB1("wqT", [128, 16 * 1536], FP16)
        wvb = SB1("wvb", [128, 512], FP16)
        xnfm = SB1("xnfm", [128, 2 * 8192], FP16)

        SAp = ctx.enter_context(nc.psum_tensor("SA", [128, 1024], F32))
        SBb = ctx.enter_context(nc.psum_tensor("SBb", [128, 1024], F32))
        B4 = [ctx.enter_context(nc.psum_tensor(f"B{i}", [128, 512], F32)) for i in range(4)]
        SAb16 = SAp.ap().bitcast(FP16)
        SBb16 = SBb.ap().bitcast(FP16)
        BANKS = [SAp[:, 0:512], SAp[:, 512:1024], SBb[:, 0:512], SBb[:, 512:1024],
                 B4[0][:, :], B4[1][:, :], B4[2][:, :], B4[3][:, :]]

        for s in ("su", "sx0", "sx1", "dxn", "arstd", "trc", "cpc", "mmq", "evq",
                  "vone", "smm", "sexp", "smsk", "avp", "ep", "sto", "su2",
                  "sxo", "dxo", "aro", "xns", "trx",
                  "p25", "wsl", "omm", "oev", "rsst", "cc",
                  "rmm", "rev", "lrel", "du", "asig", "dge2", "pw", "dpw", "dself",
                  "dcb", "tcT", "dcT", "emm", "eev",
                  "fev", "outd", "dbgs", "rsod", "dbgp"):
            P.sem(s)

        P.op("gpsimd", lambda e: e.dma_start(out=identb[:, :], in_=ident_i), inc="su", amt=16)
        P.op("sync", lambda e: e.dma_start(out=bqv[:, :], in_=bqkv_fm), inc="su", amt=16)
        P.op("sync", lambda e: e.dma_start(out=negC[:, :], in_=consts_i), inc="su", amt=16)
        P.op("gpsimd", lambda e: e.dma_start(
            out=wqT.ap().rearrange("p (c n) -> p c n", n=1536),
            in_=wqkvT[0:2048, :].rearrange("(c p) n -> p c n", p=128)), inc="su", amt=16)
        P.op("gpsimd", lambda e: e.dma_start(
            out=wvb[:, :], in_=wqkvT[2048:2176, 1024:1536]), inc="su", amt=16)
        SU_ALL = P.c("su")

        for h in range(HPC):
            P.op("vector", lambda e, h=h: e.memset(
                Vh[h].ap().rearrange("p (k c) -> p k c", c=129)[:, :, 128:129], 1.0),
                inc="vone")
        P.op("vector", lambda e: e.memset(c_all[:, :], 0.0), inc="vone")
        P.op("vector", lambda e: e.memset(onesM[:, :], 1.0), inc="vone")
        VONE_ALL = P.c("vone")

        xslot = Slots(2)
        xnslot = Slots(3)
        trslot = Slots(4)
        fmslot = Slots(2)
        qkv_bank = Slots(4)

        TRB = [SAb16, SAb16, SBb16, SBb16]
        xv_of, av_of, dxn_of, cp_done = {}, {}, {}, {}

        def st_load(ch):
            sl = ch % 2
            xsem = "sx0" if sl == 0 else "sx1"
            P.op("sync", lambda e, ch=ch, sl=sl: e.dma_start(
                out=x_sl[:, sl * D:(sl + 1) * D], in_=xb[ch * 128:(ch + 1) * 128, :]),
                waits=xslot.acquire(ch), inc=xsem, amt=16)
            xv_of[ch] = (xsem, P.c(xsem))

        def st_stats(ch):
            sl = ch % 2
            xsem, xv = xv_of[ch]

            def ln_stats(e, sl=sl):
                for i in range(4):
                    ins = e.bn_stats(stat1[:, sl * 24 + 6 * i: sl * 24 + 6 * i + 6],
                                     x_sl[:, sl * D + 512 * i: sl * D + 512 * i + 512])
                return ins
            P.op("vector", ln_stats, waits=[(xsem, xv)], inc="dself")
            sv = P.c("dself")
            P.op("vector", lambda e, sl=sl: e.bn_aggr(
                mv1[:, sl * 2: sl * 2 + 2], stat1[:, sl * 24: sl * 24 + 24]),
                waits=[("dself", sv)], inc="dself")
            sv = P.c("dself")
            P.op("vector", lambda e, sl=sl: e.tensor_scalar(
                sm1[:, sl * 6: sl * 6 + 1], mv1[:, sl * 2: sl * 2 + 1], -1.0, None, ALU.mult),
                waits=[("dself", sv)])
            P.op("vector", lambda e, sl=sl: e.tensor_scalar(
                sm1[:, sl * 6 + 1: sl * 6 + 2], mv1[:, sl * 2 + 1: sl * 2 + 2], EPS, None,
                ALU.add), inc="dself")
            sv = P.c("dself")
            P.op("scalar", lambda e, sl=sl: e.activation(
                sm1[:, sl * 6 + 2: sl * 6 + 3], sm1[:, sl * 6 + 1: sl * 6 + 2], AF.Ln),
                waits=[("dself", sv)])
            P.op("scalar", lambda e, sl=sl: e.activation(
                sm1[:, sl * 6 + 3: sl * 6 + 4], sm1[:, sl * 6 + 2: sl * 6 + 3], AF.Exp,
                scale=-0.5))
            P.op("scalar", lambda e, sl=sl: e.activation(
                sm1[:, sl * 6 + 4: sl * 6 + 5], sm1[:, sl * 6: sl * 6 + 1], AF.Identity,
                scale=sm1[:, sl * 6 + 3: sl * 6 + 4]), inc="arstd")
            av_of[ch] = P.c("arstd")

        def st_norm(ch):
            sl = ch % 2
            nsl = ch % 3
            P.op("scalar", lambda e, sl=sl, nsl=nsl: e.activation(
                xn_tm[:, nsl * D:(nsl + 1) * D], x_sl[:, sl * D:(sl + 1) * D], AF.Identity,
                bias=sm1[:, sl * 6 + 4: sl * 6 + 5], scale=sm1[:, sl * 6 + 3: sl * 6 + 4]),
                waits=[("arstd", av_of[ch])] + xnslot.acquire(ch), inc="dxn")
            xslot.release(ch, "dxn", P.c("dxn"))
            dxn_of[ch] = P.c("dxn")

        def st_tr(ch):
            g, sub = ch // 4, ch % 4
            sl = ch % 3
            dv = dxn_of[ch]
            fw = fmslot.acquire(g) if sub == 0 else []
            for dk in range(16):
                gj = ch * 16 + dk
                j = gj % 4
                buf = TRB[j]
                off = (j % 2) * 1024
                P.op("tensor", lambda e, sl=sl, dk=dk, buf=buf, off=off: e.transpose(
                    buf[:, off:off + 128],
                    xn_tm[:, sl * D + dk * 128: sl * D + dk * 128 + 128], identb[:, :]),
                    waits=[("dxn", dv), ("su", SU_ALL)] + trslot.acquire(gj), inc="trc")
                tv = P.c("trc")
                dst = xnfm[:, (g % 2) * 8192 + dk * 512 + sub * 128:
                           (g % 2) * 8192 + dk * 512 + sub * 128 + 128]
                cw_ = [("trc", tv)] + (fw if dk in (0, 1) else [])
                if gj % 2 == 0:
                    P.op("vector", lambda e, dst=dst, buf=buf, off=off: e.tensor_copy(
                        dst, buf[:, off:off + 128]), waits=cw_, inc="cpc")
                else:
                    P.op("scalar", lambda e, dst=dst, buf=buf, off=off: e.activation(
                        dst, buf[:, off:off + 128], AF.Identity), waits=cw_, inc="cpc")
                trslot.release(gj, "cpc", P.c("cpc"))
            xnslot.release(ch, "trc", P.c("trc"))
            if sub == 3:
                cp_done[g] = P.c("cpc")

        def st_mm(g, parts):
            cpv = cp_done[g]
            for part in parts:
                bank = B4[part % 4]
                bw = qkv_bank.acquire(part)
                kind, p = part // 4, part % 4
                for dk in range(16):
                    if kind < 2:
                        col0 = kind * 512 + p * 128
                        P.op("tensor", lambda e, bank=bank, dk=dk, col0=col0, g=g: e.matmul(
                            bank[:, :],
                            wqT[:, dk * 1536 + col0: dk * 1536 + col0 + 128],
                            xnfm[:, (g % 2) * 8192 + dk * 512:(g % 2) * 8192 + dk * 512 + 512],
                            start=(dk == 0), stop=(dk == 15)),
                            waits=([("cpc", cpv)] + bw) if dk == 0 else [],
                            inc="mmq" if dk == 15 else None)
                    else:
                        P.op("tensor", lambda e, bank=bank, dk=dk, p=p, g=g: e.matmul(
                            bank[:, :],
                            xnfm[:, (g % 2) * 8192 + dk * 512 + p * 128:
                                 (g % 2) * 8192 + dk * 512 + p * 128 + 128],
                            wqT[:, dk * 1536 + 1024: dk * 1536 + 1536],
                            start=(dk == 0), stop=False),
                            waits=([("cpc", cpv)] + bw) if dk == 0 else [])
                if kind == 2:
                    P.op("tensor", lambda e, bank=bank: e.matmul(
                        bank[:, :], onesM[:, :],
                        wvb[:, :],
                        start=False, stop=True),
                        waits=[("vone", VONE_ALL)],
                        inc="mmq")
                mv_ = P.c("mmq")
                if kind < 2:
                    dst = (Qh if kind == 0 else Kh)[p]
                    bcol = kind * 4 + p
                    P.op("scalar", lambda e, dst=dst, g=g, bank=bank, bcol=bcol: e.activation(
                        dst[:, g * 512:(g + 1) * 512], bank[:, :], AF.Identity,
                        bias=bqv[:, bcol:bcol + 1], scale=1.0),
                        waits=[("mmq", mv_), ("su", SU_ALL)], inc="evq")
                else:
                    kc = g * 4 + p
                    for h in range(HPC):
                        P.op("vector", lambda e, h=h, kc=kc, bank=bank: e.tensor_copy(
                            Vh[h][:, kc * 129: kc * 129 + 128],
                            bank[:, h * 128:(h + 1) * 128]),
                            waits=[("mmq", mv_)] if h == 0 else [],
                            inc="evq" if h == 3 else None)
                qkv_bank.release(part, "evq", P.c("evq"))
            if parts[-1] == 11:
                fmslot.release(g, "mmq", P.c("mmq"))

        st_load(0)
        st_load(1)
        st_stats(0)
        st_stats(1)
        st_norm(0)
        for ch in range(NCH):
            g, sub = ch // 4, ch % 4
            if ch + 2 < NCH:
                st_load(ch + 2)
                st_stats(ch + 2)
            if ch + 1 < NCH:
                st_norm(ch + 1)
            st_tr(ch)
            if g > 0:
                st_mm(g - 1, list(range(sub * 3, sub * 3 + 3)))
        st_mm(NCH // 4 - 1, list(range(12)))
        EVQ_ALL = P.c("evq")
        assert EVQ_ALL == 96
        MMQ_ALL = P.c("mmq")
        TRC_ALL = P.c("trc")
        DXN_ALL = P.c("dxn")

        if debug & 1:
            for h in range(HPC):
                P.op("gpsimd", lambda e, h=h: e.dma_start(
                    out=dbg["q"][h * 128:(h + 1) * 128, :], in_=Qh[h][:, :]),
                    waits=[("evq", EVQ_ALL)], inc="dbgs", amt=16)
                P.op("gpsimd", lambda e, h=h: e.dma_start(
                    out=dbg["k"][h * 128:(h + 1) * 128, :], in_=Kh[h][:, :]),
                    inc="dbgs", amt=16)
                P.op("gpsimd", lambda e, h=h: e.dma_start(
                    out=dbg["v"][h * 128:(h + 1) * 128, :], in_=Vh[h][:, :]),
                    waits=[("vone", VONE_ALL)], inc="dbgs", amt=16)

        p1_scope.close()
        r1_scope = ExitStack()
        inner_scopes.append(r1_scope)
        SBr1 = lambda n, s, d: r1_scope.enter_context(nc.sbuf_tensor(n, s, d, side="right"))
        maskb = SBr1("maskb", [128, 4 * 512], FP16)
        acc = SBr1("acc", [128, OCH * D], F32)
        Pb = SBr1("Pb", [128, 4 * 1024], FP16)
        stg = SBr1("stg", [128, 4 * 128], FP16)
        cT = SBr1("cT", [128, 1024], FP16)
        xo_tm = SBr1("xo_tm", [128, 2 * D], FP16)
        stat2 = SBr1("stat2", [128, 2 * 24], F32)
        mv2 = SBr1("mv2", [128, 2 * 2], F32)
        sm2 = SBr1("sm2", [128, 2 * 4], F32)

        R1GATE = [("mmq", MMQ_ALL), ("trc", TRC_ALL), ("dxn", DXN_ALL)]
        P.op("gpsimd", lambda e: e.dma_start(out=maskb[:, :], in_=masks_i),
             waits=R1GATE, inc="su2", amt=16)
        MASK_V = P.c("su2")

        if phases < 2:
            P.op("sync", lambda e: e.dma_start(out=out, in_=xo),
                 waits=[("evq", EVQ_ALL)], inc="outd", amt=16)
            _emit(nc, P, [("outd", P.c("outd")), ("dbgs", P.c("dbgs"))])
            _close_all()
            return nc

        for tc in range(OCH):
            P.op("sync", lambda e, tc=tc: e.dma_start(
                out=acc[:, tc * D:(tc + 1) * D], in_=xo[tc * 128:(tc + 1) * 128, :]),
                waits=R1GATE if tc == 0 else [], inc="sxo", amt=16)

        sbank = Slots(2)
        avbank = Slots(4)
        stslot = Slots(4)
        pbslot = Slots(4)
        xoslot = Slots(2)

        def emit_own_ln(tc):
            sl = tc % 2

            def ln2(e, sl=sl, tc=tc):
                for i in range(4):
                    ins = e.bn_stats(stat2[:, sl * 24 + 6 * i: sl * 24 + 6 * i + 6],
                                     acc[:, tc * D + 512 * i: tc * D + 512 * i + 512])
                return ins
            P.op("vector", ln2, waits=[("sxo", 16 * (tc + 1))], inc="dself")
            sv = P.c("dself")
            P.op("vector", lambda e, sl=sl: e.bn_aggr(
                mv2[:, sl * 2: sl * 2 + 2], stat2[:, sl * 24: sl * 24 + 24]),
                waits=[("dself", sv)], inc="dself")
            sv = P.c("dself")
            P.op("vector", lambda e, sl=sl: e.tensor_scalar(
                sm2[:, sl * 4: sl * 4 + 1], mv2[:, sl * 2: sl * 2 + 1], -1.0, None, ALU.mult),
                waits=[("dself", sv)])
            P.op("vector", lambda e, sl=sl: e.tensor_scalar(
                sm2[:, sl * 4 + 1: sl * 4 + 2], mv2[:, sl * 2 + 1: sl * 2 + 2], EPS, None, ALU.add),
                inc="dself")
            sv = P.c("dself")
            P.op("scalar", lambda e, sl=sl: e.activation(
                sm2[:, sl * 4 + 2: sl * 4 + 3], sm2[:, sl * 4 + 1: sl * 4 + 2], AF.Ln),
                waits=[("dself", sv)])
            P.op("scalar", lambda e, sl=sl: e.activation(
                sm2[:, sl * 4 + 3: sl * 4 + 4], sm2[:, sl * 4 + 2: sl * 4 + 3], AF.Exp, scale=-0.5),
                inc="aro")
            av = P.c("aro")
            P.opd("vector", lambda e, sl=sl, tc=tc: e.tensor_scalar(
                xo_tm[:, sl * D:(sl + 1) * D], acc[:, tc * D:(tc + 1) * D],
                sm2[:, sl * 4: sl * 4 + 1], sm2[:, sl * 4 + 3: sl * 4 + 4],
                ALU.add, ALU.mult),
                waits=[("aro", av)] + xoslot.acquire(tc), inc="dxo")
            dv = P.c("dxo")
            P.op("sync", lambda e, tc=tc, sl=sl: e.dma_start(
                out=xn_own.ap()[tc * 128:(tc + 1) * 128, :],
                in_=xo_tm[:, sl * D:(sl + 1) * D]),
                waits=[("dxo", dv)], inc="xns", amt=16)
            xoslot.release(tc, "xns", P.c("xns"))

        pairs = []
        for h in range(HPC):
            for qt in range(NG):
                nkc = 4 * (qt + 1)
                for kp in range(nkc // 2):
                    pairs.append((h, qt, kp, nkc))

        def emit_av(job):
            h, qt, kp, nkc, pidx, sev, mwaits = job
            kc0, kc1 = 2 * kp, 2 * kp + 1
            par = pidx % 4
            for j, kc in enumerate((kc0, kc1)):
                for qs in range(4):
                    first = (kc == 0)
                    lastmm = (kc == nkc - 1)
                    w = []
                    if qs == 0:
                        w = [("sexp", sev)] + mwaits
                    if first and qs == 0:
                        w = w + [("vone", VONE_ALL)]
                    if first:
                        w = w + avbank.acquire(qs)
                    P.op("tensor", lambda e, qs=qs, par=par, j=j, kc=kc, h=h,
                         first=first, lastmm=lastmm: e.matmul(
                             B4[qs][:, 0:129],
                             Pb[:, par * 1024 + j * 512 + qs * 128:
                                par * 1024 + j * 512 + qs * 128 + 128],
                             Vh[h][:, kc * 129: kc * 129 + 129],
                             start=first, stop=lastmm),
                        waits=w,
                        inc="avp" if (j == 1 and qs == 3) else None)
            pbslot.release(pidx, "avp", P.c("avp"))

        def emit_epilogue(h, qt):
            n = h * NG + qt
            apv = P.c("avp")
            for qs in range(4):
                ssl = (n * 4 + qs) % 2
                sidx = (n * 4 + qs) % 4
                P.op("vector", lambda e, qs=qs, ssl=ssl: e.reciprocal(
                    recs[:, ssl:ssl + 1], B4[qs][:, 128:129]),
                    waits=[("avp", apv)], inc="dself")
                sv2 = P.c("dself")
                stw = stslot.acquire(n * 4 + qs)
                P.opd("vector", lambda e, qs=qs, ssl=ssl, sidx=sidx: e.tensor_scalar(
                    stg[:, sidx * 128:(sidx + 1) * 128], B4[qs][:, 0:128],
                    recs[:, ssl:ssl + 1], None, ALU.mult),
                    waits=[("dself", sv2)] + stw, inc="ep")
                epv = P.c("ep")
                avbank.release(qs, "ep", epv)
                P.op("sync", lambda e, h=h, qt=qt, qs=qs, sidx=sidx: e.dma_start(
                    out=attn_tm.ap()[qt * 512 + qs * 128: qt * 512 + qs * 128 + 128,
                                     h * 128:(h + 1) * 128],
                    in_=stg[:, sidx * 128:(sidx + 1) * 128]),
                    waits=[("ep", epv)], inc="sto", amt=16)
                stslot.release(n * 4 + qs, "sto", P.c("sto"))

        pair_idx = 0
        pending = None
        for h, qt, kp, nkc in pairs:
            kc0, kc1 = 2 * kp, 2 * kp + 1
            par = pair_idx % 4
            Sp = SAp if pair_idx % 2 == 0 else SBb
            bw = sbank.acquire(pair_idx)
            for j, kc in enumerate((kc0, kc1)):
                P.op("tensor", lambda e, Sp=Sp, j=j, kc=kc, h=h, qt=qt: e.matmul(
                    Sp[:, j * 512:(j + 1) * 512],
                    Kh[h][:, kc * 128: kc * 128 + 128],
                    Qh[h][:, qt * 512: qt * 512 + 512],
                    start=True, stop=True),
                    waits=([("evq", EVQ_ALL)] + bw) if j == 0 else [],
                    inc="smm" if j == 1 else None)
            smv = P.c("smm")
            pw_ = pbslot.acquire(pair_idx)
            P.op("scalar", lambda e, Sp=Sp, par=par: e.activation(
                Pb[:, par * 1024:(par + 1) * 1024], Sp[:, :], AF.Exp,
                bias=negC[:, :], scale=ISCALE),
                waits=[("smm", smv), ("su", SU_ALL), ("su2", MASK_V)] + pw_,
                inc="sexp")
            sev = P.c("sexp")
            sbank.release(pair_idx, "sexp", sev)
            mwaits = []
            for j, kc in enumerate((kc0, kc1)):
                c = kc - 4 * qt
                if 0 <= c <= 3:
                    P.op("vector", lambda e, par=par, j=j, c=c: e.tensor_tensor(
                        Pb[:, par * 1024 + j * 512: par * 1024 + j * 512 + 512],
                        Pb[:, par * 1024 + j * 512: par * 1024 + j * 512 + 512],
                        maskb[:, c * 512:(c + 1) * 512], ALU.mult),
                        waits=[("sexp", sev), ("su2", MASK_V)], inc="smsk")
                    mwaits = [("smsk", P.c("smsk"))]
            if pending is not None:
                emit_av(pending)
                if (pending[0], pending[1]) != (h, qt):
                    emit_epilogue(pending[0], pending[1])
            pending = (h, qt, kp, nkc, pair_idx, sev, mwaits)
            pair_idx += 1
        emit_av(pending)
        emit_epilogue(pending[0], pending[1])

        if phases >= 2.2:
            for tc_ in range(OCH):
                emit_own_ln(tc_)
        STO_ALL = P.c("sto")
        assert STO_ALL == 32 * 4 * 16
        XNS_ALL = P.c("xns")
        EP_ALL = P.c("ep")
        if debug & 8:
            P.op("gpsimd", lambda e: e.dma_start(out=dbg["l"], in_=lsave[:, :]),
                 waits=[("dbgp", P.c("dbgp")), ("ep", EP_ALL)], inc="dbgs", amt=16)
        if phases < 2.2:
            P.op("sync", lambda e: e.dma_start(out=out, in_=xo),
                 waits=[("sto", STO_ALL), ("ep", EP_ALL)], inc="outd", amt=16)
            _emit(nc, P, [("outd", P.c("outd")), ("dbgs", P.c("dbgs"))])
            _close_all()
            return nc
        assert XNS_ALL == OCH * 16

        qkv_scope.close()
        r2_scope = ExitStack()
        inner_scopes.append(r2_scope)
        xnfo = r2_scope.enter_context(nc.sbuf_tensor("xnfo", [128, 16 * 1024], FP16))
        p25_scope = ExitStack()
        inner_scopes.append(p25_scope)
        SB25 = lambda n, s, d: p25_scope.enter_context(nc.sbuf_tensor(n, s, d))
        attnfm = SB25("attnfm", [128, 4 * SEQ], FP16)
        woslab = SB25("woslab", [128, 4 * D], FP16)
        rsbuf = SB25("rsbuf", [128, 4 * 512], FP16)

        for dk in range(16):
            P.op("sync", lambda e, dk=dk: e.dma_start_transpose(
                out=xnfo[:, dk * 1024:(dk + 1) * 1024],
                in_=xn_own.ap()[:, dk * 128:(dk + 1) * 128]),
                waits=[("xns", XNS_ALL), ("ep", EP_ALL)], inc="trx", amt=16)
        TRX_ALL = P.c("trx")
        if phases < 2.3:
            P.op("sync", lambda e: e.dma_start(out=out, in_=xo),
                 waits=[("sto", STO_ALL), ("trx", TRX_ALL)], inc="outd", amt=16)
            _emit(nc, P, [("outd", P.c("outd")), ("dbgs", P.c("dbgs"))])
            _close_all()
            return nc

        for fk in range(4):
            P.op("sync", lambda e, fk=fk: e.dma_start_transpose(
                out=attnfm[:, fk * SEQ:(fk + 1) * SEQ],
                in_=attn_tm.ap()[:, fk * 128:(fk + 1) * 128]),
                waits=[("sto", STO_ALL), ("ep", EP_ALL)], inc="p25", amt=16)
        P25_ALL = P.c("p25")
        if debug & 2:
            P.op("gpsimd", lambda e: e.dma_start(out=dbg["attn"], in_=attn_tm.ap()),
                 waits=[("sto", STO_ALL)], inc="dbgs", amt=16)

        P.op("gpsimd", lambda e: e.dma_start(
            out=woslab.ap().rearrange("p (c n) -> p c n", n=D),
            in_=woTs.rearrange("(c p) n -> p c n", p=128)),
            waits=[("ep", EP_ALL)], inc="wsl", amt=16)
        WO_V = P.c("wsl")

        obank = avbank
        oslot = Slots(4)
        for tt in range(NCH):
            for ot in range(4):
                oi = tt * 4 + ot
                bank = B4[oi % 4]
                bw = obank.acquire(oi)
                for fk in range(4):
                    P.op("tensor", lambda e, bank=bank, fk=fk, tt=tt, ot=ot: e.matmul(
                        bank[:, :],
                        attnfm[:, fk * SEQ + tt * 128: fk * SEQ + tt * 128 + 128],
                        woslab[:, fk * D + ot * 512: fk * D + ot * 512 + 512],
                        start=(fk == 0), stop=(fk == 3)),
                        waits=([("p25", P25_ALL), ("wsl", WO_V)] + bw) if fk == 0 else [],
                        inc="omm" if fk == 3 else None)
                omv = P.c("omm")
                sw = oslot.acquire(oi)
                if oi % 2 == 0:
                    P.opd("scalar", lambda e, bank=bank, osl=oi % 4: e.activation(
                        rsbuf[:, osl * 512:(osl + 1) * 512], bank[:, :], AF.Identity),
                        waits=[("omm", omv)] + sw, inc="oev")
                else:
                    P.opd("vector", lambda e, bank=bank, osl=oi % 4: e.tensor_copy(
                        rsbuf[:, osl * 512:(osl + 1) * 512], bank[:, :]),
                        waits=[("omm", omv)] + sw, inc="oev")
                oev = P.c("oev")
                obank.release(oi, "oev", oev)
                P.op("sync", lambda e, tt=tt, ot=ot, osl=oi % 4: e.dma_start(
                    out=rs_in.ap()[tt * 128:(tt + 1) * 128, ot * 512:(ot + 1) * 512],
                    in_=rsbuf[:, osl * 512:(osl + 1) * 512]),
                    waits=[("oev", oev)], inc="rsst", amt=16)
                oslot.release(oi, "rsst", P.c("rsst"))
        RSST_ALL = P.c("rsst")
        OMM_ALL = P.c("omm")
        if phases < 2.4:
            P.op("sync", lambda e: e.dma_start(out=out, in_=xo),
                 waits=[("rsst", RSST_ALL)], inc="outd", amt=16)
            _emit(nc, P, [("outd", P.c("outd")), ("dbgs", P.c("dbgs"))])
            _close_all()
            return nc

        if phases < 3:
            P.op("gpsimd", lambda e: e.collective_compute(
                "ReduceScatter", ALU.add,
                replica_groups=[[0, 1, 2, 3], [4, 5, 6, 7]],
                ins=[rs_in.ap().opt()], outs=[rs_out_d.ap().opt()]),
                waits=[("rsst", RSST_ALL)], inc="cc")
            CC_V = P.c("cc")
        if phases < 3:
            if debug & 2:
                P.op("gpsimd", lambda e: e.dma_start(out=dbg["rs"], in_=rs_out_d.ap()),
                     waits=[("cc", CC_V)], inc="dbgs", amt=16)
            P.op("sync", lambda e: e.dma_start(out=out, in_=xo),
                 waits=[("rsst", RSST_ALL)], inc="outd", amt=16)
            _emit(nc, P, [("outd", P.c("outd")), ("dbgs", P.c("dbgs")), ("cc", CC_V)])
            _close_all()
            return nc

        p25_scope.close()
        p3_scope = ExitStack()
        inner_scopes.append(p3_scope)
        SB3 = lambda n, s, d: p3_scope.enter_context(nc.sbuf_tensor(n, s, d))
        SLABW = 17 * 512
        wsl = SB3("wsl", [128, 2 * SLABW], FP16)
        NT2 = 4
        lb = SB3("lb", [128, 4 * 1024], FP16)
        m1b = SB3("m1b", [128, NT2 * 128], FP16)
        m2b = SB3("m2b", [128, NT2 * 128], FP16)
        s12 = SB3("s12", [128, NT2 * 128], FP16)
        ge1b = SB3("ge1b", [128, NT2 * 1024], FP16)
        lmb = SB3("lmb", [128, 1024], FP16)
        ub = SB3("ub", [128, NT2 * 1024], FP16)
        sgb = SB3("sgb", [128, NT2 * 1024], FP16)
        wb_ = SB3("wb", [128, 1024], FP16)
        tsc = SB3("tsc", [128, 1024], FP16)
        cw = SB3("cw", [128, 4 * 1024], FP16)
        rtmp = SB3("rtmp", [128, 2 * 16], F32)
        ebpb = SB3("ebpb", [128, D], FP16)

        P3GATE = [("omm", OMM_ALL)]
        P.op("gpsimd", lambda e: e.dma_start(out=ebpb[:, :], in_=ebp_i),
             waits=P3GATE, inc="su2", amt=16)
        SU2_V = P.c("su2")
        P.op("gpsimd", lambda e: e.memset(cw[:, :], 0.0), waits=P3GATE, inc="pw")

        p3bank = Slots(8)
        p3bank.free_at[0] = p3bank.free_at[1] = sbank.free_at[0]
        p3bank.free_at[2] = p3bank.free_at[3] = sbank.free_at[1]
        for i in range(4):
            p3bank.free_at[4 + i] = obank.free_at[i]

        wslslot = Slots(2)
        slab_ready = {}

        def load_router_slab(ct):
            slw = wslslot.acquire(ct) + (P3GATE if ct < 2 else [])
            P.op("gpsimd", lambda e, ct=ct: e.dma_start(
                out=wsl.ap().rearrange("p (s c n) -> p s c n", s=2, n=512)[:, ct % 2],
                in_=wrT[:, ct * 512:(ct + 1) * 512].rearrange("(c p) n -> p c n", p=128)),
                waits=slw, inc="wsl", amt=16)
            slab_ready[ct] = P.c("wsl")

        load_router_slab(0)
        load_router_slab(1)
        CC_V = None

        lbslot = Slots(4)
        t2slot = Slots(NT2)
        tile_idx = 0
        pair_cnt = 0

        for ct in range(NCT):
            for tc in range(OCH):
                bank = BANKS[tc]
                bw = p3bank.acquire(tc)
                slv = slab_ready[ct]
                for dk in range(16):
                    P.op("tensor", lambda e, bank=bank, dk=dk, ct=ct, tc=tc: e.matmul(
                        bank,
                        xnfo[:, dk * 1024 + tc * 128: dk * 1024 + tc * 128 + 128],
                        wsl[:, (ct % 2) * SLABW + dk * 512:(ct % 2) * SLABW + dk * 512 + 512],
                        start=(dk == 0), stop=False),
                        waits=([("trx", TRX_ALL), ("wsl", slv)] + bw) if dk == 0 else [])
                P.op("tensor", lambda e, bank=bank, ct=ct: e.matmul(
                    bank, onesM[:, :],
                    wsl[:, (ct % 2) * SLABW + 16 * 512:(ct % 2) * SLABW + 17 * 512],
                    start=False, stop=True),
                    inc="rmm")
                rmv = P.c("rmm")
                half = tile_idx % 2
                psl = (tile_idx // 2) % 4
                lw = lbslot.acquire(tile_idx // 2) if half == 0 else []
                P.opd("scalar", lambda e, bank=bank, psl=psl, half=half: e.activation(
                    lb[:, psl * 1024 + half * 512: psl * 1024 + half * 512 + 512],
                    bank, AF.Identity),
                    waits=[("rmm", rmv)] + lw, inc="rev")
                p3bank.release(tc, "rev", P.c("rev"))
                tile_idx += 1

                if half == 1:
                    pi = pair_cnt
                    pair_cnt += 1
                    revv = P.c("rev")
                    dbg_lb_hold = None
                    if debug & 16 and pi == 0:
                        P.op("gpsimd", lambda e: e.dma_start(
                            out=dbg["lg"], in_=lb[:, 0:1024]),
                            waits=[("rev", 2)], inc="dbgp", amt=16)
                        dbg_lb_hold = P.c("dbgp")
                    ts = pi % NT2
                    L = lb[:, psl * 1024:(psl + 1) * 1024]
                    Lv = lb.ap().rearrange("p (s g e) -> p s g e", s=4, e=8)[:, psl]
                    M1 = m1b[:, ts * 128: ts * 128 + 128]
                    M2 = m2b[:, ts * 128: ts * 128 + 128]
                    S12 = s12[:, ts * 128: ts * 128 + 128]
                    GE = ge1b[:, ts * 1024: ts * 1024 + 1024]
                    LM = lmb[:, 0:1024]
                    U = ub[:, ts * 1024: ts * 1024 + 1024]
                    SG = sgb[:, ts * 1024: ts * 1024 + 1024]
                    W = wb_[:, 0:1024]
                    TSC = tsc[:, 0:1024]
                    tw = t2slot.acquire(pi)

                    P.op("vector", lambda e, Lv=Lv, M1=M1: e.reduce_max(M1, Lv, axis=AX.X),
                         waits=[("rev", revv)] + tw, inc="dself")
                    sv = P.c("dself")
                    P.op("vector", lambda e, L=L, M1=M1, GE=GE: e.tensor_tensor(
                        GE.rearrange("p (g e) -> p g e", e=8),
                        L.rearrange("p (g e) -> p g e", e=8),
                        M1.unsqueeze(2).broadcast_to([128, 128, 8]), ALU.is_ge),
                        waits=[("dself", sv)])
                    P.op("vector", lambda e, GE=GE, TSC=TSC: e.tensor_scalar(
                        TSC, GE, -BIG, None, ALU.mult))
                    P.op("vector", lambda e, L=L, TSC=TSC, LM=LM: e.tensor_tensor(
                        LM, TSC, L, ALU.add), inc="dself")
                    sv = P.c("dself")
                    P.op("vector", lambda e, LM=LM, M2=M2: e.reduce_max(
                        M2, LM.rearrange("p (g e) -> p g e", e=8), axis=AX.X),
                        waits=[("dself", sv)], inc="dself")
                    sv = P.c("dself")
                    P.op("vector", lambda e, M1=M1, M2=M2, S12=S12: e.tensor_tensor(
                        S12, M1, M2, ALU.add),
                        waits=[("dself", sv)], inc="dself")
                    sv = P.c("dself")
                    P.op("vector", lambda e, L=L, TSC=TSC: e.tensor_scalar(
                        TSC, L, 2.0, None, ALU.mult),
                        waits=[("dself", sv)])
                    P.opd("vector", lambda e, TSC=TSC, S12=S12, U=U: e.tensor_tensor(
                        U.rearrange("p (g e) -> p g e", e=8),
                        TSC.rearrange("p (g e) -> p g e", e=8),
                        S12.unsqueeze(2).broadcast_to([128, 128, 8]), ALU.subtract),
                        inc="du")
                    duv = P.c("du")
                    P.opd("vector", lambda e, L=L, M2=M2, GE=GE: e.tensor_tensor(
                        GE.rearrange("p (g e) -> p g e", e=8),
                        L.rearrange("p (g e) -> p g e", e=8),
                        M2.unsqueeze(2).broadcast_to([128, 128, 8]), ALU.is_ge),
                        inc="dge2")
                    gev = P.c("dge2")
                    P.opd("scalar", lambda e, U=U, SG=SG: e.activation(SG, U, AF.Sigmoid),
                          waits=[("du", duv)], inc="asig")
                    sgv = P.c("asig")
                    P.op("gpsimd", lambda e, SG=SG, GE=GE, W=W: e.tensor_tensor(
                        W, SG, GE, ALU.mult),
                        waits=[("asig", sgv), ("dge2", gev)])
                    P.opd("gpsimd", lambda e, W=W, tcp=(tc // 2): e.tensor_tensor(
                        cw[:, tcp * 1024:(tcp + 1) * 1024],
                        cw[:, tcp * 1024:(tcp + 1) * 1024], W, ALU.add),
                        inc="pw", amt=2)
                    pwv = P.c("pw")
                    if debug & 16 and pi == 0:
                        P.op("gpsimd", lambda e, W=W: e.dma_start(
                            out=dbg["t2"][:, 0:1024], in_=W), waits=[("pw", pwv)],
                            inc="dbgp", amt=16)
                        P.op("gpsimd", lambda e, M1=M1: e.dma_start(
                            out=dbg["t2"][:, 1024:1152], in_=M1), inc="dbgp", amt=16)
                        P.op("gpsimd", lambda e, M2=M2: e.dma_start(
                            out=dbg["t2"][:, 1152:1280], in_=M2), inc="dbgp", amt=16)
                        P.op("gpsimd", lambda e, U=U: e.dma_start(
                            out=dbg["t2"][:, 1280:1536], in_=U[:, 0:256]), inc="dbgp", amt=16)
                        t2_hold = P.c("dbgp")
                    t2slot.release(pi, "pw", pwv)
                    lbslot.release(pi, "pw", pwv)
                    if dbg_lb_hold is not None:
                        lbslot.release(pi, "dbgp", dbg_lb_hold)
                    if debug & 16 and pi == 0:
                        t2slot.release(pi, "dbgp", t2_hold)
            wslslot.release(ct, "rmm", P.c("rmm"))
            if ct + 2 < NCT:
                load_router_slab(ct + 2)
            if ct == 3:
                P.op("gpsimd", lambda e: e.collective_compute(
                    "ReduceScatter", ALU.add,
                    replica_groups=[[0, 1, 2, 3], [4, 5, 6, 7]],
                    ins=[rs_in.ap().opt()], outs=[rs_out_d.ap().opt()]),
                    waits=[("rsst", RSST_ALL)], inc="cc")
        RMM_ALL = P.c("rmm")
        assert RMM_ALL == NCT * OCH
        CC_V = P.c("cc")
        PW_ALL = P.c("pw")
        for tcp in range(4):
            for hh in range(2):
                P.op("vector", lambda e, tcp=tcp, hh=hh: e.reduce_sum(
                    rtmp[:, hh * 8: hh * 8 + 8],
                    cw.ap().rearrange("p (q s g e) -> p q s e g", q=4, s=2, e=8)[:, tcp, hh],
                    axis=AX.X),
                    waits=[("pw", PW_ALL)], inc="dself")
                sv = P.c("dself")
                P.op("vector", lambda e, tcp=tcp, hh=hh: e.tensor_tensor(
                    c_all[:, (tcp * 2 + hh) * 8: (tcp * 2 + hh) * 8 + 8],
                    c_all[:, (tcp * 2 + hh) * 8: (tcp * 2 + hh) * 8 + 8],
                    rtmp[:, hh * 8: hh * 8 + 8], ALU.add),
                    waits=[("dself", sv), ("vone", VONE_ALL)], inc="lrel")
        LREL_ALL = P.c("lrel")

        if debug & 4:
            P.op("gpsimd", lambda e: e.dma_start(out=dbg["c"], in_=c_all[:, :]),
                 waits=[("lrel", LREL_ALL)], inc="dbgs", amt=16)

        P.op("vector", lambda e: e.memset(cT[:, :], 0.0), waits=R1GATE)
        P.op("vector", lambda e: e.memset(cT[64:65, :], 1.0))
        P.op("vector", lambda e: e.tensor_copy(cbf[:, :], c_all[:, :]),
             waits=[("lrel", LREL_ALL)], inc="dcb")
        DCB_V = P.c("dcb")
        REV_ALL = P.c("rev")
        ctslot = Slots(2)
        for tc in range(OCH):
            P.op("tensor", lambda e, tc=tc: e.transpose(
                SBb16[0:8, (tc % 2) * 1024:(tc % 2) * 1024 + 128],
                cbf[:, tc * 8: tc * 8 + 8], identb[:, :]),
                waits=[("dcb", DCB_V), ("rev", REV_ALL)] + ctslot.acquire(tc), inc="tcT")
            tv = P.c("tcT")
            P.op("vector", lambda e, tc=tc: e.tensor_copy(
                cT[0:8, tc * 128:(tc + 1) * 128],
                SBb.ap().bitcast(FP16)[0:8, (tc % 2) * 1024:(tc % 2) * 1024 + 128]),
                waits=[("tcT", tv)], inc="dcT")
            ctslot.release(tc, "dcT", P.c("dcT"))
        DCT_ALL = P.c("dcT")

        for ot in range(4):
            for tc in range(OCH):
                bank = BANKS[tc]
                bw = p3bank.acquire(tc)
                P.op("tensor", lambda e, bank=bank, tc=tc, ot=ot: e.matmul(
                    bank, cT[:, tc * 128:(tc + 1) * 128],
                    ebpb[:, ot * 512:(ot + 1) * 512],
                    start=True, stop=True),
                    waits=[("dcT", DCT_ALL), ("su2", SU2_V)] + bw,
                    inc="emm")
                emv = P.c("emm")
                P.op("vector", lambda e, bank=bank, tc=tc, ot=ot: e.tensor_tensor(
                    acc[:, tc * D + ot * 512: tc * D + ot * 512 + 512],
                    acc[:, tc * D + ot * 512: tc * D + ot * 512 + 512],
                    bank, ALU.add),
                    waits=[("emm", emv)], inc="eev")
                p3bank.release(tc, "eev", P.c("eev"))

        def load_expert_slab(s):
            ei, ot = s // 4, s % 4
            G = NCT + s
            slw = wslslot.acquire(G)
            P.op("gpsimd", lambda e, ei=ei, ot=ot, G=G: e.dma_start(
                out=wsl.ap().rearrange("p (s c n) -> p s c n", s=2, n=512)[:, G % 2, 0:16],
                in_=weT[ei, :, ot * 512:(ot + 1) * 512].rearrange("(c p) n -> p c n", p=128)),
                waits=slw, inc="wsl", amt=16)
            slab_ready[G] = P.c("wsl")

        load_expert_slab(0)
        load_expert_slab(1)
        for s in range(32):
            ei, ot = s // 4, s % 4
            G = NCT + s
            slv = slab_ready[G]
            for tc in range(OCH):
                bank = BANKS[tc]
                bw = p3bank.acquire(tc)
                for dk in range(16):
                    P.op("tensor", lambda e, bank=bank, dk=dk, G=G, tc=tc: e.matmul(
                        bank,
                        xnfo[:, dk * 1024 + tc * 128: dk * 1024 + tc * 128 + 128],
                        wsl[:, (G % 2) * SLABW + dk * 512:(G % 2) * SLABW + dk * 512 + 512],
                        start=(dk == 0), stop=(dk == 15)),
                        waits=([("wsl", slv)] + bw) if dk == 0 else [],
                        inc="emm" if dk == 15 else None)
                emv = P.c("emm")
                P.op("vector", lambda e, bank=bank, tc=tc, ot=ot, ei=ei: e.scalar_tensor_tensor(
                    acc[:, tc * D + ot * 512: tc * D + ot * 512 + 512],
                    bank, c_all[:, tc * 8 + ei: tc * 8 + ei + 1],
                    acc[:, tc * D + ot * 512: tc * D + ot * 512 + 512],
                    ALU.mult, ALU.add),
                    waits=[("emm", emv)], inc="eev")
                p3bank.release(tc, "eev", P.c("eev"))
            wslslot.release(G, "emm", P.c("emm"))
            if s + 2 < 32:
                load_expert_slab(s + 2)
        EEV_ALL = P.c("eev")
        assert EEV_ALL == 32 + 256

        if debug & 4:
            P.op("gpsimd", lambda e: e.dma_start(out=dbg["acc"], in_=acc[:, :]),
                 waits=[("eev", EEV_ALL)], inc="dbgs", amt=16)
        if debug & 2:
            P.op("gpsimd", lambda e: e.dma_start(out=dbg["rs"], in_=rs_out_d.ap()),
                 waits=[("cc", CC_V)], inc="dbgs", amt=16)

        if phases < 4:
            P.op("sync", lambda e: e.dma_start(out=out, in_=xo),
                 waits=[("eev", EEV_ALL)], inc="outd", amt=16)
            _emit(nc, P, [("outd", P.c("outd")), ("dbgs", P.c("dbgs")), ("cc", CC_V)])
            _close_all()
            return nc

        p3_scope.close()
        p4_scope = ExitStack()
        inner_scopes.append(p4_scope)
        rso = p4_scope.enter_context(nc.sbuf_tensor("rso", [128, 2 * D], FP16))
        fout = p4_scope.enter_context(nc.sbuf_tensor("fout", [128, 2 * D], F32))
        rsoslot = Slots(2)
        for tc in range(OCH):
            sl = tc % 2
            P.op("sync", lambda e, tc=tc, sl=sl: e.dma_start(
                out=rso[:, sl * D:(sl + 1) * D],
                in_=rs_out_d.ap()[tc * 128:(tc + 1) * 128, :]),
                waits=[("cc", CC_V), ("eev", EEV_ALL)] + rsoslot.acquire(tc),
                inc="rsod", amt=16)
            rv = P.c("rsod")
            P.opd("vector", lambda e, tc=tc, sl=sl: e.tensor_tensor(
                fout[:, sl * D:(sl + 1) * D], rso[:, sl * D:(sl + 1) * D],
                acc[:, tc * D:(tc + 1) * D], ALU.add),
                waits=[("rsod", rv), ("eev", EEV_ALL)], inc="fev")
            fv = P.c("fev")
            P.op("sync", lambda e, tc=tc, sl=sl: e.dma_start(
                out=out[tc * 128:(tc + 1) * 128, :], in_=fout[:, sl * D:(sl + 1) * D]),
                waits=[("fev", fv)], inc="outd", amt=16)
            rsoslot.release(tc, "outd", P.c("outd"))
        OUTD_ALL = P.c("outd")
        assert OUTD_ALL == OCH * 16

        _emit(nc, P, [("outd", OUTD_ALL), ("dbgs", P.c("dbgs")), ("cc", CC_V)])
        _close_all()
    return nc


def _emit(nc, P, final_waits):
    global LAST_PROG
    LAST_PROG = (P, final_waits)
    with nc.Block() as block:
        @block.tensor
        def _(e):
            P.run("tensor", e)

        @block.vector
        def _(e):
            P.run("vector", e)

        @block.scalar
        def _(e):
            P.run("scalar", e)

        @block.gpsimd
        def _(e):
            P.run("gpsimd", e)

        @block.sync
        def _(e):
            P.run("sync", e, final_waits=final_waits)



def _mk_masks():
    m = np.zeros((128, 4 * 512), dtype=np.float32)
    f = np.arange(512)
    for c in range(4):
        m[:, c * 512:(c + 1) * 512] = (f[None, :] >= (c * 128 + np.arange(128))[:, None])
    return m


def kernel(L0_attn_x, L0_attn_w_qkv, L0_attn_b_qkv, L0_attn_w_out, L0_attn_b_out,
           L0_ffn_router_w, L0_ffn_router_b, L0_ffn_expert_w, L0_ffn_expert_b):
    x = np.ascontiguousarray(np.asarray(L0_attn_x, dtype=np.float32))
    w_qkv = np.asarray(L0_attn_w_qkv, dtype=np.float32)
    b_qkv = np.asarray(L0_attn_b_qkv, dtype=np.float32)
    w_out = np.asarray(L0_attn_w_out, dtype=np.float32)
    b_out = np.asarray(L0_attn_b_out, dtype=np.float32)
    router_w = np.asarray(L0_ffn_router_w, dtype=np.float32)
    router_b = np.asarray(L0_ffn_router_b, dtype=np.float32)
    expert_w = np.asarray(L0_ffn_expert_w, dtype=np.float32)
    expert_b = np.asarray(L0_ffn_expert_b, dtype=np.float32)

    wrT = np.concatenate([router_w.T, router_b[None, :],
                          np.zeros((127, E * D), dtype=np.float32)], axis=0)
    wrT = np.ascontiguousarray(wrT.astype(np.float16))
    weT = np.ascontiguousarray(expert_w.transpose(0, 2, 1).astype(np.float16))
    ebp = np.zeros((128, D), dtype=np.float32)
    ebp[0:8] = expert_b
    ebp[64] = b_out
    woT = np.ascontiguousarray(w_out.T)
    masks = _mk_masks()
    ident = np.eye(128, dtype=np.float32)
    consts = np.full((128, 1), -CEXP, dtype=np.float32)

    in_maps = []
    for core in range(8):
        b, r = core // 4, core % 4
        hs0 = r * HPC
        q_rows = w_qkv[hs0 * HD:(hs0 + HPC) * HD]
        k_rows = w_qkv[2048 + hs0 * HD: 2048 + (hs0 + HPC) * HD]
        v_rows = w_qkv[4096 + hs0 * HD: 4096 + (hs0 + HPC) * HD]
        wcat = np.concatenate([q_rows, k_rows, v_rows], axis=0)
        bvpad = np.zeros((128, 1536), dtype=np.float32)
        bvpad[0, 1024:1536] = b_qkv[4096 + hs0 * HD: 4096 + (hs0 + HPC) * HD]
        wqkvT = np.ascontiguousarray(np.concatenate([wcat.T, bvpad], axis=0).astype(np.float16))
        bq = b_qkv[hs0 * HD:(hs0 + HPC) * HD].reshape(4, 128).T
        bk = b_qkv[2048 + hs0 * HD: 2048 + (hs0 + HPC) * HD].reshape(4, 128).T
        bqkv_fm = np.ascontiguousarray(np.concatenate([bq, bk], axis=1))
        woTs = np.ascontiguousarray(woT[hs0 * HD:(hs0 + HPC) * HD, :].astype(np.float16))
        xb_c = np.ascontiguousarray(x[b])
        xo_c = np.ascontiguousarray(x[b, r * 1024:(r + 1) * 1024])
        in_maps.append(dict(
            xb=xb_c, xo=xo_c, wqkvT=wqkvT, bqkv_fm=bqkv_fm,
            woTs=woTs, wrT=wrT, weT=weT, ebp=ebp,
            masks=masks, ident=ident, consts=consts,
        ))

    nc = build(debug=DEBUG, phases=PHASES)
    res = run_bass_kernel_spmd(nc, in_maps, core_ids=list(range(8)))

    full = np.empty((2, SEQ, D), dtype=np.float32)
    for core in range(8):
        b, r = core // 4, core % 4
        full[b, r * 1024:(r + 1) * 1024] = res.results[core]["out"]
    kernel.last_results = res
    return full


# version 13
# speedup vs baseline: 1.0975x; 1.0260x over previous
import math
from contextlib import ExitStack

import numpy as np
import ml_dtypes

import concourse.bass as bass
import concourse.mybir as mybir
from concourse.bass_utils import run_bass_kernel_spmd

F32 = mybir.dt.float32
BF16 = mybir.dt.bfloat16
FP16 = mybir.dt.float16
AF = mybir.ActivationFunctionType
ALU = mybir.AluOpType
AX = mybir.AxisListType

D = 2048
SEQ = 4096
HD = 128
HPC = 4
E = 8
NCH = 32
NG = 8
OCH = 8
NCT = 32
CEXP = 12.0
BIG = 60000.0
EPS = 1e-5
ISCALE = 1.0 / math.sqrt(HD)

ENGS = ("tensor", "vector", "scalar", "gpsimd", "sync")

PHASES = 4
LAST_PROG = None
NO_SMALLK = False
DEBUG = 0


class Prog:
    def __init__(self, nc, ctx):
        self.nc = nc
        self.q = {e: [] for e in ENGS}
        self.cnt = {}
        self.sems = {}
        self.ctx = ctx

    def sem(self, name):
        h = self.ctx.enter_context(self.nc.semaphore(name))
        self.sems[name] = h
        self.cnt[name] = 0

    def op(self, eng, fn, waits=(), inc=None, amt=1):
        ws = tuple((s, v) for (s, v) in waits if s is not None and v and v > 0)
        if inc is not None:
            self.cnt[inc] += amt
        self.q[eng].append((ws, fn, inc, amt))
        return self.cnt[inc] if inc is not None else None

    def c(self, sem):
        return self.cnt[sem]

    def opd(self, eng, fn, waits=(), inc=None, amt=1):
        self.op(eng, fn, waits=waits)
        return self.op(eng, lambda e: e.drain(), inc=inc, amt=amt)

    def run(self, eng, e, final_waits=()):
        seen = {}
        for ws, fn, inc, amt in self.q[eng]:
            for s, v in ws:
                if seen.get(s, 0) >= v:
                    continue
                seen[s] = v
                e.wait_ge(self.sems[s], v)
            ins = fn(e)
            if inc is not None:
                ins.then_inc(self.sems[inc], amt)
        for s, v in final_waits:
            if v > 0 and seen.get(s, 0) < v:
                e.wait_ge(self.sems[s], v)


class Slots:
    def __init__(self, n):
        self.n = n
        self.free_at = [None] * n

    def acquire(self, i):
        w = self.free_at[i % self.n]
        return [w] if w is not None else []

    def release(self, i, sem, val):
        self.free_at[i % self.n] = (sem, val)


def build(debug=0, phases=4, no_smallk=None):
    if no_smallk is None:
        no_smallk = NO_SMALLK
    nc = bass.Bass("TRN2", target_bir_lowering=False, debug=False)

    xb = nc.dram_tensor("xb", [SEQ, D], F32, kind="ExternalInput").ap()
    xo = nc.dram_tensor("xo", [1024, D], F32, kind="ExternalInput").ap()
    wqkvT = nc.dram_tensor("wqkvT", [D + 128, 1536], FP16, kind="ExternalInput").ap()
    bqkv_fm = nc.dram_tensor("bqkv_fm", [128, 8], F32, kind="ExternalInput").ap()
    woTs = nc.dram_tensor("woTs", [512, D], FP16, kind="ExternalInput").ap()
    wrT = nc.dram_tensor("wrT", [D + 128, E * D], FP16, kind="ExternalInput").ap()
    weT = nc.dram_tensor("weT", [E, D, D], FP16, kind="ExternalInput").ap()
    ebp_i = nc.dram_tensor("ebp", [128, D], F32, kind="ExternalInput").ap()
    masks_i = nc.dram_tensor("masks", [128, 4 * 512], F32, kind="ExternalInput").ap()
    ident_i = nc.dram_tensor("ident", [128, 128], F32, kind="ExternalInput").ap()
    consts_i = nc.dram_tensor("consts", [128, 1], F32, kind="ExternalInput").ap()

    out = nc.dram_tensor("out", [1024, D], F32, kind="ExternalOutput").ap()

    attn_tm = nc.dram_tensor("attn_tm", [SEQ, HPC * HD], FP16)
    xn_own = nc.dram_tensor("xn_own", [1024, D], FP16)
    rs_in = nc.dram_tensor("rs_in", [SEQ, D], FP16)
    rs_out_d = nc.dram_tensor("rs_out", [1024, D], FP16)

    dbg = {}
    if debug & 1:
        dbg["q"] = nc.dram_tensor("q_dbg", [512, SEQ], F32, kind="ExternalOutput").ap()
        dbg["k"] = nc.dram_tensor("k_dbg", [512, SEQ], F32, kind="ExternalOutput").ap()
        dbg["v"] = nc.dram_tensor("v_dbg", [512, 32 * 129], F32, kind="ExternalOutput").ap()
    if debug & 2:
        dbg["attn"] = nc.dram_tensor("attn_dbg", [SEQ, HPC * HD], F32, kind="ExternalOutput").ap()
        dbg["rs"] = nc.dram_tensor("rs_dbg", [1024, D], F32, kind="ExternalOutput").ap()
    if debug & 16:
        dbg["lg"] = nc.dram_tensor("lg_dbg", [128, 1024], F32, kind="ExternalOutput").ap()
        dbg["t2"] = nc.dram_tensor("t2_dbg", [128, 1536], F32, kind="ExternalOutput").ap()
    if debug & 8:
        dbg["p"] = nc.dram_tensor("p_dbg", [128, 1024], F32, kind="ExternalOutput").ap()
        dbg["l"] = nc.dram_tensor("l_dbg", [128, 8], F32, kind="ExternalOutput").ap()
        dbg["s"] = nc.dram_tensor("s_dbg", [128, 1024], F32, kind="ExternalOutput").ap()
        dbg["av"] = nc.dram_tensor("av_dbg", [128, 1024], F32, kind="ExternalOutput").ap()
    if debug & 4:
        dbg["c"] = nc.dram_tensor("c_dbg", [128, 64], F32, kind="ExternalOutput").ap()
        dbg["acc"] = nc.dram_tensor("acc_dbg", [128, OCH * D], F32, kind="ExternalOutput").ap()

    ctx = ExitStack()
    inner_scopes = []

    def _close_all():
        for sc in reversed(inner_scopes):
            sc.close()

    with ctx:
        P = Prog(nc, ctx)
        SBP = lambda n, s, d: ctx.enter_context(nc.sbuf_tensor(n, s, d))

        identb = SBP("identb", [128, 128], FP16)
        onesM = SBP("onesM", [128, 128], FP16)
        bqv = SBP("bqv", [128, 8], F32)
        negC = SBP("negC", [128, 1], F32)
        c_all = SBP("c_all", [128, 64], F32)
        cbf = SBP("cbf", [128, 64], FP16)
        recs = SBP("recs", [128, 2], F32)
        if debug & 8:
            lsave = SBP("lsave", [128, 8], F32)
            ssave = SBP("ssave", [128, 1024], F32)

        qkv_scope = ExitStack()
        inner_scopes.append(qkv_scope)
        SBq = lambda n, s, d: qkv_scope.enter_context(nc.sbuf_tensor(n, s, d))
        Qh = [SBq(f"Qh{h}", [128, SEQ], FP16) for h in range(HPC)]
        Kh = [SBq(f"Kh{h}", [128, SEQ], FP16) for h in range(HPC)]
        Vh = [SBq(f"Vh{h}", [128, 32 * 129], FP16) for h in range(HPC)]

        p1_scope = ExitStack()
        inner_scopes.append(p1_scope)
        SB1 = lambda n, s, d: p1_scope.enter_context(nc.sbuf_tensor(n, s, d, side="right"))
        x_sl = SB1("x_sl", [128, 2 * D], F32)
        stat1 = SB1("stat1", [128, 2 * 24], F32)
        mv1 = SB1("mv1", [128, 2 * 2], F32)
        sm1 = SB1("sm1", [128, 2 * 6], F32)
        xn_tm = SB1("xn_tm", [128, 3 * D], FP16)
        wqT = SB1("wqT", [128, 16 * 1536], FP16)
        wvb = SB1("wvb", [128, 512], FP16)
        xnfm = SB1("xnfm", [128, 2 * 8192], FP16)

        SAp = ctx.enter_context(nc.psum_tensor("SA", [128, 1024], F32))
        SBb = ctx.enter_context(nc.psum_tensor("SBb", [128, 1024], F32))
        B4 = [ctx.enter_context(nc.psum_tensor(f"B{i}", [128, 512], F32)) for i in range(4)]
        SAb16 = SAp.ap().bitcast(FP16)
        SBb16 = SBb.ap().bitcast(FP16)
        BANKS = [SAp[:, 0:512], SAp[:, 512:1024], SBb[:, 0:512], SBb[:, 512:1024],
                 B4[0][:, :], B4[1][:, :], B4[2][:, :], B4[3][:, :]]

        for s in ("su", "sx0", "sx1", "dxn", "arstd", "trc", "cpc", "mmq", "evq",
                  "vone", "smm", "sexp", "smsk", "avp", "ep", "sto", "su2",
                  "sxo", "dxo", "aro", "xns", "trx",
                  "p25", "wsl", "omm", "oev", "rsst", "cc",
                  "rmm", "rev", "lrel", "du", "asig", "dge2", "pw", "dpw", "dself",
                  "dcb", "tcT", "dcT", "emm", "eev",
                  "fev", "outd", "dbgs", "rsod", "dbgp"):
            P.sem(s)

        P.op("gpsimd", lambda e: e.dma_start(out=identb[:, :], in_=ident_i), inc="su", amt=16)
        P.op("sync", lambda e: e.dma_start(out=bqv[:, :], in_=bqkv_fm), inc="su", amt=16)
        P.op("sync", lambda e: e.dma_start(out=negC[:, :], in_=consts_i), inc="su", amt=16)
        P.op("gpsimd", lambda e: e.dma_start(
            out=wqT.ap().rearrange("p (c n) -> p c n", n=1536),
            in_=wqkvT[0:2048, :].rearrange("(c p) n -> p c n", p=128)), inc="su", amt=16)
        P.op("gpsimd", lambda e: e.dma_start(
            out=wvb[:, :], in_=wqkvT[2048:2176, 1024:1536]), inc="su", amt=16)
        SU_ALL = P.c("su")

        for h in range(HPC):
            P.op("vector", lambda e, h=h: e.memset(
                Vh[h].ap().rearrange("p (k c) -> p k c", c=129)[:, :, 128:129], 1.0),
                inc="vone")
        P.op("vector", lambda e: e.memset(c_all[:, :], 0.0), inc="vone")
        P.op("vector", lambda e: e.memset(onesM[:, :], 1.0), inc="vone")
        VONE_ALL = P.c("vone")

        xslot = Slots(2)
        xnslot = Slots(3)
        trslot = Slots(4)
        fmslot = Slots(2)
        qkv_bank = Slots(4)

        TRB = [SAb16, SAb16, SBb16, SBb16]
        xv_of, av_of, dxn_of, cp_done = {}, {}, {}, {}

        def st_load(ch):
            sl = ch % 2
            xsem = "sx0" if sl == 0 else "sx1"
            P.op("sync", lambda e, ch=ch, sl=sl: e.dma_start(
                out=x_sl[:, sl * D:(sl + 1) * D], in_=xb[ch * 128:(ch + 1) * 128, :]),
                waits=xslot.acquire(ch), inc=xsem, amt=16)
            xv_of[ch] = (xsem, P.c(xsem))

        def st_stats(ch):
            sl = ch % 2
            xsem, xv = xv_of[ch]

            def ln_stats(e, sl=sl):
                for i in range(4):
                    ins = e.bn_stats(stat1[:, sl * 24 + 6 * i: sl * 24 + 6 * i + 6],
                                     x_sl[:, sl * D + 512 * i: sl * D + 512 * i + 512])
                return ins
            P.op("vector", ln_stats, waits=[(xsem, xv)], inc="dself")
            sv = P.c("dself")
            P.op("vector", lambda e, sl=sl: e.bn_aggr(
                mv1[:, sl * 2: sl * 2 + 2], stat1[:, sl * 24: sl * 24 + 24]),
                waits=[("dself", sv)], inc="dself")
            sv = P.c("dself")
            P.op("vector", lambda e, sl=sl: e.tensor_scalar(
                sm1[:, sl * 6: sl * 6 + 1], mv1[:, sl * 2: sl * 2 + 1], -1.0, None, ALU.mult),
                waits=[("dself", sv)])
            P.op("vector", lambda e, sl=sl: e.tensor_scalar(
                sm1[:, sl * 6 + 1: sl * 6 + 2], mv1[:, sl * 2 + 1: sl * 2 + 2], EPS, None,
                ALU.add), inc="dself")
            sv = P.c("dself")
            P.op("scalar", lambda e, sl=sl: e.activation(
                sm1[:, sl * 6 + 2: sl * 6 + 3], sm1[:, sl * 6 + 1: sl * 6 + 2], AF.Ln),
                waits=[("dself", sv)])
            P.op("scalar", lambda e, sl=sl: e.activation(
                sm1[:, sl * 6 + 3: sl * 6 + 4], sm1[:, sl * 6 + 2: sl * 6 + 3], AF.Exp,
                scale=-0.5))
            P.op("scalar", lambda e, sl=sl: e.activation(
                sm1[:, sl * 6 + 4: sl * 6 + 5], sm1[:, sl * 6: sl * 6 + 1], AF.Identity,
                scale=sm1[:, sl * 6 + 3: sl * 6 + 4]), inc="arstd")
            av_of[ch] = P.c("arstd")

        def st_norm(ch):
            sl = ch % 2
            nsl = ch % 3
            P.op("scalar", lambda e, sl=sl, nsl=nsl: e.activation(
                xn_tm[:, nsl * D:(nsl + 1) * D], x_sl[:, sl * D:(sl + 1) * D], AF.Identity,
                bias=sm1[:, sl * 6 + 4: sl * 6 + 5], scale=sm1[:, sl * 6 + 3: sl * 6 + 4]),
                waits=[("arstd", av_of[ch])] + xnslot.acquire(ch), inc="dxn")
            xslot.release(ch, "dxn", P.c("dxn"))
            dxn_of[ch] = P.c("dxn")

        def st_tr(ch):
            g, sub = ch // 4, ch % 4
            sl = ch % 3
            dv = dxn_of[ch]
            fw = fmslot.acquire(g) if sub == 0 else []
            for dk in range(16):
                gj = ch * 16 + dk
                j = gj % 4
                buf = TRB[j]
                off = (j % 2) * 1024
                P.op("tensor", lambda e, sl=sl, dk=dk, buf=buf, off=off: e.transpose(
                    buf[:, off:off + 128],
                    xn_tm[:, sl * D + dk * 128: sl * D + dk * 128 + 128], identb[:, :]),
                    waits=[("dxn", dv), ("su", SU_ALL)] + trslot.acquire(gj), inc="trc")
                tv = P.c("trc")
                dst = xnfm[:, (g % 2) * 8192 + dk * 512 + sub * 128:
                           (g % 2) * 8192 + dk * 512 + sub * 128 + 128]
                cw_ = [("trc", tv)] + (fw if dk in (0, 1) else [])
                if gj % 2 == 0:
                    P.op("vector", lambda e, dst=dst, buf=buf, off=off: e.tensor_copy(
                        dst, buf[:, off:off + 128]), waits=cw_, inc="cpc")
                else:
                    P.op("scalar", lambda e, dst=dst, buf=buf, off=off: e.activation(
                        dst, buf[:, off:off + 128], AF.Identity), waits=cw_, inc="cpc")
                trslot.release(gj, "cpc", P.c("cpc"))
            xnslot.release(ch, "trc", P.c("trc"))
            if sub == 3:
                cp_done[g] = P.c("cpc")

        def st_mm(g, parts):
            cpv = cp_done[g]
            for part in parts:
                bank = B4[part % 4]
                bw = qkv_bank.acquire(part)
                kind, p = part // 4, part % 4
                for dk in range(16):
                    if kind < 2:
                        col0 = kind * 512 + p * 128
                        P.op("tensor", lambda e, bank=bank, dk=dk, col0=col0, g=g: e.matmul(
                            bank[:, :],
                            wqT[:, dk * 1536 + col0: dk * 1536 + col0 + 128],
                            xnfm[:, (g % 2) * 8192 + dk * 512:(g % 2) * 8192 + dk * 512 + 512],
                            start=(dk == 0), stop=(dk == 15)),
                            waits=([("cpc", cpv)] + bw) if dk == 0 else [],
                            inc="mmq" if dk == 15 else None)
                    else:
                        P.op("tensor", lambda e, bank=bank, dk=dk, p=p, g=g: e.matmul(
                            bank[:, :],
                            xnfm[:, (g % 2) * 8192 + dk * 512 + p * 128:
                                 (g % 2) * 8192 + dk * 512 + p * 128 + 128],
                            wqT[:, dk * 1536 + 1024: dk * 1536 + 1536],
                            start=(dk == 0), stop=False),
                            waits=([("cpc", cpv)] + bw) if dk == 0 else [])
                if kind == 2:
                    P.op("tensor", lambda e, bank=bank: e.matmul(
                        bank[:, :], onesM[:, :],
                        wvb[:, :],
                        start=False, stop=True),
                        waits=[("vone", VONE_ALL)],
                        inc="mmq")
                mv_ = P.c("mmq")
                if kind < 2:
                    dst = (Qh if kind == 0 else Kh)[p]
                    bcol = kind * 4 + p
                    P.op("scalar", lambda e, dst=dst, g=g, bank=bank, bcol=bcol: e.activation(
                        dst[:, g * 512:(g + 1) * 512], bank[:, :], AF.Identity,
                        bias=bqv[:, bcol:bcol + 1], scale=1.0),
                        waits=[("mmq", mv_), ("su", SU_ALL)], inc="evq")
                else:
                    kc = g * 4 + p
                    for h in range(HPC):
                        P.op("vector", lambda e, h=h, kc=kc, bank=bank: e.tensor_copy(
                            Vh[h][:, kc * 129: kc * 129 + 128],
                            bank[:, h * 128:(h + 1) * 128]),
                            waits=[("mmq", mv_)] if h == 0 else [],
                            inc="evq" if h == 3 else None)
                qkv_bank.release(part, "evq", P.c("evq"))
            if parts[-1] == 11:
                fmslot.release(g, "mmq", P.c("mmq"))

        st_load(0)
        st_load(1)
        st_stats(0)
        st_stats(1)
        st_norm(0)
        for ch in range(NCH):
            g, sub = ch // 4, ch % 4
            if ch + 2 < NCH:
                st_load(ch + 2)
                st_stats(ch + 2)
            if ch + 1 < NCH:
                st_norm(ch + 1)
            st_tr(ch)
            if g > 0:
                st_mm(g - 1, list(range(sub * 3, sub * 3 + 3)))
        st_mm(NCH // 4 - 1, list(range(12)))
        EVQ_ALL = P.c("evq")
        assert EVQ_ALL == 96
        MMQ_ALL = P.c("mmq")
        TRC_ALL = P.c("trc")
        DXN_ALL = P.c("dxn")

        if debug & 1:
            for h in range(HPC):
                P.op("gpsimd", lambda e, h=h: e.dma_start(
                    out=dbg["q"][h * 128:(h + 1) * 128, :], in_=Qh[h][:, :]),
                    waits=[("evq", EVQ_ALL)], inc="dbgs", amt=16)
                P.op("gpsimd", lambda e, h=h: e.dma_start(
                    out=dbg["k"][h * 128:(h + 1) * 128, :], in_=Kh[h][:, :]),
                    inc="dbgs", amt=16)
                P.op("gpsimd", lambda e, h=h: e.dma_start(
                    out=dbg["v"][h * 128:(h + 1) * 128, :], in_=Vh[h][:, :]),
                    waits=[("vone", VONE_ALL)], inc="dbgs", amt=16)

        p1_scope.close()
        r1_scope = ExitStack()
        inner_scopes.append(r1_scope)
        SBr1 = lambda n, s, d: r1_scope.enter_context(nc.sbuf_tensor(n, s, d, side="right"))
        maskb = SBr1("maskb", [128, 4 * 512], FP16)
        acc = SBr1("acc", [128, OCH * D], F32)
        Pb = SBr1("Pb", [128, 4 * 1024], FP16)
        stg = SBr1("stg", [128, 4 * 128], FP16)
        cT = SBr1("cT", [128, 1024], FP16)
        xo_tm = SBr1("xo_tm", [128, 2 * D], FP16)
        stat2 = SBr1("stat2", [128, 2 * 24], F32)
        mv2 = SBr1("mv2", [128, 2 * 2], F32)
        sm2 = SBr1("sm2", [128, 2 * 4], F32)

        R1GATE = [("mmq", MMQ_ALL), ("trc", TRC_ALL), ("dxn", DXN_ALL)]
        P.op("gpsimd", lambda e: e.dma_start(out=maskb[:, :], in_=masks_i),
             waits=R1GATE, inc="su2", amt=16)
        MASK_V = P.c("su2")

        if phases < 2:
            P.op("sync", lambda e: e.dma_start(out=out, in_=xo),
                 waits=[("evq", EVQ_ALL)], inc="outd", amt=16)
            _emit(nc, P, [("outd", P.c("outd")), ("dbgs", P.c("dbgs"))])
            _close_all()
            return nc

        for tc in range(OCH):
            P.op("sync", lambda e, tc=tc: e.dma_start(
                out=acc[:, tc * D:(tc + 1) * D], in_=xo[tc * 128:(tc + 1) * 128, :]),
                waits=R1GATE if tc == 0 else [], inc="sxo", amt=16)

        sbank = Slots(2)
        avbank = Slots(4)
        stslot = Slots(4)
        pbslot = Slots(4)
        xoslot = Slots(2)

        def emit_own_ln(tc):
            sl = tc % 2

            def ln2(e, sl=sl, tc=tc):
                for i in range(4):
                    ins = e.bn_stats(stat2[:, sl * 24 + 6 * i: sl * 24 + 6 * i + 6],
                                     acc[:, tc * D + 512 * i: tc * D + 512 * i + 512])
                return ins
            P.op("vector", ln2, waits=[("sxo", 16 * (tc + 1))], inc="dself")
            sv = P.c("dself")
            P.op("vector", lambda e, sl=sl: e.bn_aggr(
                mv2[:, sl * 2: sl * 2 + 2], stat2[:, sl * 24: sl * 24 + 24]),
                waits=[("dself", sv)], inc="dself")
            sv = P.c("dself")
            P.op("vector", lambda e, sl=sl: e.tensor_scalar(
                sm2[:, sl * 4: sl * 4 + 1], mv2[:, sl * 2: sl * 2 + 1], -1.0, None, ALU.mult),
                waits=[("dself", sv)])
            P.op("vector", lambda e, sl=sl: e.tensor_scalar(
                sm2[:, sl * 4 + 1: sl * 4 + 2], mv2[:, sl * 2 + 1: sl * 2 + 2], EPS, None, ALU.add),
                inc="dself")
            sv = P.c("dself")
            P.op("scalar", lambda e, sl=sl: e.activation(
                sm2[:, sl * 4 + 2: sl * 4 + 3], sm2[:, sl * 4 + 1: sl * 4 + 2], AF.Ln),
                waits=[("dself", sv)])
            P.op("scalar", lambda e, sl=sl: e.activation(
                sm2[:, sl * 4 + 3: sl * 4 + 4], sm2[:, sl * 4 + 2: sl * 4 + 3], AF.Exp, scale=-0.5),
                inc="aro")
            av = P.c("aro")
            P.opd("vector", lambda e, sl=sl, tc=tc: e.tensor_scalar(
                xo_tm[:, sl * D:(sl + 1) * D], acc[:, tc * D:(tc + 1) * D],
                sm2[:, sl * 4: sl * 4 + 1], sm2[:, sl * 4 + 3: sl * 4 + 4],
                ALU.add, ALU.mult),
                waits=[("aro", av)] + xoslot.acquire(tc), inc="dxo")
            dv = P.c("dxo")
            P.op("scalar", lambda e, tc=tc, sl=sl: e.dma_start(
                out=xn_own.ap()[tc * 128:(tc + 1) * 128, :],
                in_=xo_tm[:, sl * D:(sl + 1) * D]),
                waits=[("dxo", dv)], inc="xns", amt=16)
            xoslot.release(tc, "xns", P.c("xns"))

        pairs = []
        for h in range(HPC):
            for qt in range(NG):
                nkc = 4 * (qt + 1)
                for kp in range(nkc // 2):
                    pairs.append((h, qt, kp, nkc))

        def emit_av(job):
            h, qt, kp, nkc, pidx, sev, mwaits = job
            kc0, kc1 = 2 * kp, 2 * kp + 1
            par = pidx % 4
            for j, kc in enumerate((kc0, kc1)):
                for qs in range(4):
                    first = (kc == 0)
                    lastmm = (kc == nkc - 1)
                    w = []
                    if qs == 0:
                        w = [("sexp", sev)] + mwaits
                    if first and qs == 0:
                        w = w + [("vone", VONE_ALL)]
                    if first:
                        w = w + avbank.acquire(qs)
                    P.op("tensor", lambda e, qs=qs, par=par, j=j, kc=kc, h=h,
                         first=first, lastmm=lastmm: e.matmul(
                             B4[qs][:, 0:129],
                             Pb[:, par * 1024 + j * 512 + qs * 128:
                                par * 1024 + j * 512 + qs * 128 + 128],
                             Vh[h][:, kc * 129: kc * 129 + 129],
                             start=first, stop=lastmm),
                        waits=w,
                        inc="avp" if (j == 1 and qs == 3) else None)
            pbslot.release(pidx, "avp", P.c("avp"))

        def emit_epilogue(h, qt):
            n = h * NG + qt
            apv = P.c("avp")
            for qs in range(4):
                ssl = (n * 4 + qs) % 2
                sidx = (n * 4 + qs) % 4
                P.op("vector", lambda e, qs=qs, ssl=ssl: e.reciprocal(
                    recs[:, ssl:ssl + 1], B4[qs][:, 128:129]),
                    waits=[("avp", apv)], inc="dself")
                sv2 = P.c("dself")
                stw = stslot.acquire(n * 4 + qs)
                P.opd("vector", lambda e, qs=qs, ssl=ssl, sidx=sidx: e.tensor_scalar(
                    stg[:, sidx * 128:(sidx + 1) * 128], B4[qs][:, 0:128],
                    recs[:, ssl:ssl + 1], None, ALU.mult),
                    waits=[("dself", sv2)] + stw, inc="ep")
                epv = P.c("ep")
                avbank.release(qs, "ep", epv)
                P.op("sync", lambda e, h=h, qt=qt, qs=qs, sidx=sidx: e.dma_start(
                    out=attn_tm.ap()[qt * 512 + qs * 128: qt * 512 + qs * 128 + 128,
                                     h * 128:(h + 1) * 128],
                    in_=stg[:, sidx * 128:(sidx + 1) * 128]),
                    waits=[("ep", epv)], inc="sto", amt=16)
                stslot.release(n * 4 + qs, "sto", P.c("sto"))

        pair_idx = 0
        pending = None
        for h, qt, kp, nkc in pairs:
            kc0, kc1 = 2 * kp, 2 * kp + 1
            par = pair_idx % 4
            Sp = SAp if pair_idx % 2 == 0 else SBb
            bw = sbank.acquire(pair_idx)
            for j, kc in enumerate((kc0, kc1)):
                P.op("tensor", lambda e, Sp=Sp, j=j, kc=kc, h=h, qt=qt: e.matmul(
                    Sp[:, j * 512:(j + 1) * 512],
                    Kh[h][:, kc * 128: kc * 128 + 128],
                    Qh[h][:, qt * 512: qt * 512 + 512],
                    start=True, stop=True),
                    waits=([("evq", EVQ_ALL)] + bw) if j == 0 else [],
                    inc="smm" if j == 1 else None)
            smv = P.c("smm")
            pw_ = pbslot.acquire(pair_idx)
            P.op("scalar", lambda e, Sp=Sp, par=par: e.activation(
                Pb[:, par * 1024:(par + 1) * 1024], Sp[:, :], AF.Exp,
                bias=negC[:, :], scale=ISCALE),
                waits=[("smm", smv), ("su", SU_ALL), ("su2", MASK_V)] + pw_,
                inc="sexp")
            sev = P.c("sexp")
            sbank.release(pair_idx, "sexp", sev)
            mwaits = []
            for j, kc in enumerate((kc0, kc1)):
                c = kc - 4 * qt
                if 0 <= c <= 3:
                    P.op("vector", lambda e, par=par, j=j, c=c: e.tensor_tensor(
                        Pb[:, par * 1024 + j * 512: par * 1024 + j * 512 + 512],
                        Pb[:, par * 1024 + j * 512: par * 1024 + j * 512 + 512],
                        maskb[:, c * 512:(c + 1) * 512], ALU.mult),
                        waits=[("sexp", sev), ("su2", MASK_V)], inc="smsk")
                    mwaits = [("smsk", P.c("smsk"))]
            if pending is not None:
                emit_av(pending)
                if (pending[0], pending[1]) != (h, qt):
                    emit_epilogue(pending[0], pending[1])
            pending = (h, qt, kp, nkc, pair_idx, sev, mwaits)
            pair_idx += 1
        emit_av(pending)
        emit_epilogue(pending[0], pending[1])

        if phases >= 2.2:
            for tc_ in range(OCH):
                emit_own_ln(tc_)
        STO_ALL = P.c("sto")
        assert STO_ALL == 32 * 4 * 16
        XNS_ALL = P.c("xns")
        EP_ALL = P.c("ep")
        if debug & 8:
            P.op("gpsimd", lambda e: e.dma_start(out=dbg["l"], in_=lsave[:, :]),
                 waits=[("dbgp", P.c("dbgp")), ("ep", EP_ALL)], inc="dbgs", amt=16)
        if phases < 2.2:
            P.op("sync", lambda e: e.dma_start(out=out, in_=xo),
                 waits=[("sto", STO_ALL), ("ep", EP_ALL)], inc="outd", amt=16)
            _emit(nc, P, [("outd", P.c("outd")), ("dbgs", P.c("dbgs"))])
            _close_all()
            return nc
        assert XNS_ALL == OCH * 16

        qkv_scope.close()
        r2_scope = ExitStack()
        inner_scopes.append(r2_scope)
        xnfo = r2_scope.enter_context(nc.sbuf_tensor("xnfo", [128, 16 * 1024], FP16))
        p25_scope = ExitStack()
        inner_scopes.append(p25_scope)
        SB25 = lambda n, s, d: p25_scope.enter_context(nc.sbuf_tensor(n, s, d))
        attnfm = SB25("attnfm", [128, 4 * SEQ], FP16)
        woslab = SB25("woslab", [128, 4 * D], FP16)
        rsbuf = SB25("rsbuf", [128, 4 * 512], FP16)

        for dk in range(16):
            P.op("scalar", lambda e, dk=dk: e.dma_start_transpose(
                out=xnfo[:, dk * 1024:(dk + 1) * 1024],
                in_=xn_own.ap()[:, dk * 128:(dk + 1) * 128]),
                waits=[("xns", XNS_ALL), ("ep", EP_ALL)], inc="trx", amt=16)
        TRX_ALL = P.c("trx")
        if phases < 2.3:
            P.op("sync", lambda e: e.dma_start(out=out, in_=xo),
                 waits=[("sto", STO_ALL), ("trx", TRX_ALL)], inc="outd", amt=16)
            _emit(nc, P, [("outd", P.c("outd")), ("dbgs", P.c("dbgs"))])
            _close_all()
            return nc

        for fk in range(4):
            P.op("sync", lambda e, fk=fk: e.dma_start_transpose(
                out=attnfm[:, fk * SEQ:(fk + 1) * SEQ],
                in_=attn_tm.ap()[:, fk * 128:(fk + 1) * 128]),
                waits=[("sto", STO_ALL), ("ep", EP_ALL)], inc="p25", amt=16)
        P25_ALL = P.c("p25")
        if debug & 2:
            P.op("gpsimd", lambda e: e.dma_start(out=dbg["attn"], in_=attn_tm.ap()),
                 waits=[("sto", STO_ALL)], inc="dbgs", amt=16)

        P.op("gpsimd", lambda e: e.dma_start(
            out=woslab.ap().rearrange("p (c n) -> p c n", n=D),
            in_=woTs.rearrange("(c p) n -> p c n", p=128)),
            waits=[("ep", EP_ALL)], inc="wsl", amt=16)
        WO_V = P.c("wsl")

        obank = avbank
        oslot = Slots(4)
        for tt in range(NCH):
            for ot in range(4):
                oi = tt * 4 + ot
                bank = B4[oi % 4]
                bw = obank.acquire(oi)
                for fk in range(4):
                    P.op("tensor", lambda e, bank=bank, fk=fk, tt=tt, ot=ot: e.matmul(
                        bank[:, :],
                        attnfm[:, fk * SEQ + tt * 128: fk * SEQ + tt * 128 + 128],
                        woslab[:, fk * D + ot * 512: fk * D + ot * 512 + 512],
                        start=(fk == 0), stop=(fk == 3)),
                        waits=([("p25", P25_ALL), ("wsl", WO_V)] + bw) if fk == 0 else [],
                        inc="omm" if fk == 3 else None)
                omv = P.c("omm")
                sw = oslot.acquire(oi)
                if oi % 2 == 0:
                    P.opd("scalar", lambda e, bank=bank, osl=oi % 4: e.activation(
                        rsbuf[:, osl * 512:(osl + 1) * 512], bank[:, :], AF.Identity),
                        waits=[("omm", omv)] + sw, inc="oev")
                else:
                    P.opd("vector", lambda e, bank=bank, osl=oi % 4: e.tensor_copy(
                        rsbuf[:, osl * 512:(osl + 1) * 512], bank[:, :]),
                        waits=[("omm", omv)] + sw, inc="oev")
                oev = P.c("oev")
                obank.release(oi, "oev", oev)
                P.op("sync", lambda e, tt=tt, ot=ot, osl=oi % 4: e.dma_start(
                    out=rs_in.ap()[tt * 128:(tt + 1) * 128, ot * 512:(ot + 1) * 512],
                    in_=rsbuf[:, osl * 512:(osl + 1) * 512]),
                    waits=[("oev", oev)], inc="rsst", amt=16)
                oslot.release(oi, "rsst", P.c("rsst"))
        RSST_ALL = P.c("rsst")
        OMM_ALL = P.c("omm")
        if phases < 2.4:
            P.op("sync", lambda e: e.dma_start(out=out, in_=xo),
                 waits=[("rsst", RSST_ALL)], inc="outd", amt=16)
            _emit(nc, P, [("outd", P.c("outd")), ("dbgs", P.c("dbgs"))])
            _close_all()
            return nc

        if phases < 3:
            P.op("gpsimd", lambda e: e.collective_compute(
                "ReduceScatter", ALU.add,
                replica_groups=[[0, 1, 2, 3], [4, 5, 6, 7]],
                ins=[rs_in.ap().opt()], outs=[rs_out_d.ap().opt()]),
                waits=[("rsst", RSST_ALL)], inc="cc")
            CC_V = P.c("cc")
        if phases < 3:
            if debug & 2:
                P.op("gpsimd", lambda e: e.dma_start(out=dbg["rs"], in_=rs_out_d.ap()),
                     waits=[("cc", CC_V)], inc="dbgs", amt=16)
            P.op("sync", lambda e: e.dma_start(out=out, in_=xo),
                 waits=[("rsst", RSST_ALL)], inc="outd", amt=16)
            _emit(nc, P, [("outd", P.c("outd")), ("dbgs", P.c("dbgs")), ("cc", CC_V)])
            _close_all()
            return nc

        p25_scope.close()
        p3_scope = ExitStack()
        inner_scopes.append(p3_scope)
        SB3 = lambda n, s, d: p3_scope.enter_context(nc.sbuf_tensor(n, s, d))
        SLABW = 17 * 512
        wsl = SB3("wsl", [128, 2 * SLABW], FP16)
        NT2 = 4
        lb = SB3("lb", [128, 4 * 1024], FP16)
        m1b = SB3("m1b", [128, NT2 * 128], FP16)
        m2b = SB3("m2b", [128, NT2 * 128], FP16)
        s12 = SB3("s12", [128, NT2 * 128], FP16)
        ge1b = SB3("ge1b", [128, NT2 * 1024], FP16)
        lmb = SB3("lmb", [128, 1024], FP16)
        ub = SB3("ub", [128, NT2 * 1024], FP16)
        sgb = SB3("sgb", [128, NT2 * 1024], FP16)
        wb_ = SB3("wb", [128, 1024], FP16)
        tsc = SB3("tsc", [128, 1024], FP16)
        cw = SB3("cw", [128, 4 * 1024], FP16)
        rtmp = SB3("rtmp", [128, 2 * 16], F32)
        ebpb = SB3("ebpb", [128, D], FP16)

        P3GATE = [("omm", OMM_ALL)]
        P.op("gpsimd", lambda e: e.dma_start(out=ebpb[:, :], in_=ebp_i),
             waits=P3GATE, inc="su2", amt=16)
        SU2_V = P.c("su2")
        P.op("gpsimd", lambda e: e.memset(cw[:, :], 0.0), waits=P3GATE, inc="pw")

        p3bank = Slots(8)
        p3bank.free_at[0] = p3bank.free_at[1] = sbank.free_at[0]
        p3bank.free_at[2] = p3bank.free_at[3] = sbank.free_at[1]
        for i in range(4):
            p3bank.free_at[4 + i] = obank.free_at[i]

        wslslot = Slots(2)
        slab_ready = {}

        def load_router_slab(ct):
            slw = wslslot.acquire(ct) + (P3GATE if ct < 2 else [])
            P.op("gpsimd", lambda e, ct=ct: e.dma_start(
                out=wsl.ap().rearrange("p (s c n) -> p s c n", s=2, n=512)[:, ct % 2],
                in_=wrT[:, ct * 512:(ct + 1) * 512].rearrange("(c p) n -> p c n", p=128)),
                waits=slw, inc="wsl", amt=16)
            slab_ready[ct] = P.c("wsl")

        load_router_slab(0)
        load_router_slab(1)
        CC_V = None

        lbslot = Slots(4)
        t2slot = Slots(NT2)
        tile_idx = 0
        pair_cnt = 0

        for ct in range(NCT):
            for tc in range(OCH):
                bank = BANKS[tc]
                bw = p3bank.acquire(tc)
                slv = slab_ready[ct]
                for dk in range(16):
                    P.op("tensor", lambda e, bank=bank, dk=dk, ct=ct, tc=tc: e.matmul(
                        bank,
                        xnfo[:, dk * 1024 + tc * 128: dk * 1024 + tc * 128 + 128],
                        wsl[:, (ct % 2) * SLABW + dk * 512:(ct % 2) * SLABW + dk * 512 + 512],
                        start=(dk == 0), stop=False),
                        waits=([("trx", TRX_ALL), ("wsl", slv)] + bw) if dk == 0 else [])
                P.op("tensor", lambda e, bank=bank, ct=ct: e.matmul(
                    bank, onesM[:, :],
                    wsl[:, (ct % 2) * SLABW + 16 * 512:(ct % 2) * SLABW + 17 * 512],
                    start=False, stop=True),
                    inc="rmm")
                rmv = P.c("rmm")
                half = tile_idx % 2
                psl = (tile_idx // 2) % 4
                lw = lbslot.acquire(tile_idx // 2) if half == 0 else []
                P.opd("scalar", lambda e, bank=bank, psl=psl, half=half: e.activation(
                    lb[:, psl * 1024 + half * 512: psl * 1024 + half * 512 + 512],
                    bank, AF.Identity),
                    waits=[("rmm", rmv)] + lw, inc="rev")
                p3bank.release(tc, "rev", P.c("rev"))
                tile_idx += 1

                if half == 1:
                    pi = pair_cnt
                    pair_cnt += 1
                    revv = P.c("rev")
                    dbg_lb_hold = None
                    if debug & 16 and pi == 0:
                        P.op("gpsimd", lambda e: e.dma_start(
                            out=dbg["lg"], in_=lb[:, 0:1024]),
                            waits=[("rev", 2)], inc="dbgp", amt=16)
                        dbg_lb_hold = P.c("dbgp")
                    ts = pi % NT2
                    L = lb[:, psl * 1024:(psl + 1) * 1024]
                    Lv = lb.ap().rearrange("p (s g e) -> p s g e", s=4, e=8)[:, psl]
                    M1 = m1b[:, ts * 128: ts * 128 + 128]
                    M2 = m2b[:, ts * 128: ts * 128 + 128]
                    S12 = s12[:, ts * 128: ts * 128 + 128]
                    GE = ge1b[:, ts * 1024: ts * 1024 + 1024]
                    LM = lmb[:, 0:1024]
                    U = ub[:, ts * 1024: ts * 1024 + 1024]
                    SG = sgb[:, ts * 1024: ts * 1024 + 1024]
                    W = wb_[:, 0:1024]
                    TSC = tsc[:, 0:1024]
                    tw = t2slot.acquire(pi)

                    P.op("vector", lambda e, Lv=Lv, M1=M1: e.reduce_max(M1, Lv, axis=AX.X),
                         waits=[("rev", revv)] + tw, inc="dself")
                    sv = P.c("dself")
                    P.op("vector", lambda e, L=L, M1=M1, GE=GE: e.tensor_tensor(
                        GE.rearrange("p (g e) -> p g e", e=8),
                        L.rearrange("p (g e) -> p g e", e=8),
                        M1.unsqueeze(2).broadcast_to([128, 128, 8]), ALU.is_ge),
                        waits=[("dself", sv)])
                    P.op("vector", lambda e, GE=GE, TSC=TSC: e.tensor_scalar(
                        TSC, GE, -BIG, None, ALU.mult))
                    P.op("vector", lambda e, L=L, TSC=TSC, LM=LM: e.tensor_tensor(
                        LM, TSC, L, ALU.add), inc="dself")
                    sv = P.c("dself")
                    P.op("vector", lambda e, LM=LM, M2=M2: e.reduce_max(
                        M2, LM.rearrange("p (g e) -> p g e", e=8), axis=AX.X),
                        waits=[("dself", sv)], inc="dself")
                    sv = P.c("dself")
                    P.op("vector", lambda e, M1=M1, M2=M2, S12=S12: e.tensor_tensor(
                        S12, M1, M2, ALU.add),
                        waits=[("dself", sv)], inc="dself")
                    sv = P.c("dself")
                    P.op("vector", lambda e, L=L, TSC=TSC: e.tensor_scalar(
                        TSC, L, 2.0, None, ALU.mult),
                        waits=[("dself", sv)])
                    P.opd("vector", lambda e, TSC=TSC, S12=S12, U=U: e.tensor_tensor(
                        U.rearrange("p (g e) -> p g e", e=8),
                        TSC.rearrange("p (g e) -> p g e", e=8),
                        S12.unsqueeze(2).broadcast_to([128, 128, 8]), ALU.subtract),
                        inc="du")
                    duv = P.c("du")
                    P.opd("vector", lambda e, L=L, M2=M2, GE=GE: e.tensor_tensor(
                        GE.rearrange("p (g e) -> p g e", e=8),
                        L.rearrange("p (g e) -> p g e", e=8),
                        M2.unsqueeze(2).broadcast_to([128, 128, 8]), ALU.is_ge),
                        inc="dge2")
                    gev = P.c("dge2")
                    P.opd("scalar", lambda e, U=U, SG=SG: e.activation(SG, U, AF.Sigmoid),
                          waits=[("du", duv)], inc="asig")
                    sgv = P.c("asig")
                    P.op("gpsimd", lambda e, SG=SG, GE=GE, W=W: e.tensor_tensor(
                        W, SG, GE, ALU.mult),
                        waits=[("asig", sgv), ("dge2", gev)])
                    P.opd("gpsimd", lambda e, W=W, tcp=(tc // 2): e.tensor_tensor(
                        cw[:, tcp * 1024:(tcp + 1) * 1024],
                        cw[:, tcp * 1024:(tcp + 1) * 1024], W, ALU.add),
                        inc="pw", amt=2)
                    pwv = P.c("pw")
                    if debug & 16 and pi == 0:
                        P.op("gpsimd", lambda e, W=W: e.dma_start(
                            out=dbg["t2"][:, 0:1024], in_=W), waits=[("pw", pwv)],
                            inc="dbgp", amt=16)
                        P.op("gpsimd", lambda e, M1=M1: e.dma_start(
                            out=dbg["t2"][:, 1024:1152], in_=M1), inc="dbgp", amt=16)
                        P.op("gpsimd", lambda e, M2=M2: e.dma_start(
                            out=dbg["t2"][:, 1152:1280], in_=M2), inc="dbgp", amt=16)
                        P.op("gpsimd", lambda e, U=U: e.dma_start(
                            out=dbg["t2"][:, 1280:1536], in_=U[:, 0:256]), inc="dbgp", amt=16)
                        t2_hold = P.c("dbgp")
                    t2slot.release(pi, "pw", pwv)
                    lbslot.release(pi, "pw", pwv)
                    if dbg_lb_hold is not None:
                        lbslot.release(pi, "dbgp", dbg_lb_hold)
                    if debug & 16 and pi == 0:
                        t2slot.release(pi, "dbgp", t2_hold)
            wslslot.release(ct, "rmm", P.c("rmm"))
            if ct + 2 < NCT:
                load_router_slab(ct + 2)
            if ct == 3:
                P.op("gpsimd", lambda e: e.collective_compute(
                    "ReduceScatter", ALU.add,
                    replica_groups=[[0, 1, 2, 3], [4, 5, 6, 7]],
                    ins=[rs_in.ap().opt()], outs=[rs_out_d.ap().opt()]),
                    waits=[("rsst", RSST_ALL)], inc="cc")
        RMM_ALL = P.c("rmm")
        assert RMM_ALL == NCT * OCH
        CC_V = P.c("cc")
        PW_ALL = P.c("pw")
        for tcp in range(4):
            for hh in range(2):
                P.op("vector", lambda e, tcp=tcp, hh=hh: e.reduce_sum(
                    rtmp[:, hh * 8: hh * 8 + 8],
                    cw.ap().rearrange("p (q s g e) -> p q s e g", q=4, s=2, e=8)[:, tcp, hh],
                    axis=AX.X),
                    waits=[("pw", PW_ALL)], inc="dself")
                sv = P.c("dself")
                P.op("vector", lambda e, tcp=tcp, hh=hh: e.tensor_tensor(
                    c_all[:, (tcp * 2 + hh) * 8: (tcp * 2 + hh) * 8 + 8],
                    c_all[:, (tcp * 2 + hh) * 8: (tcp * 2 + hh) * 8 + 8],
                    rtmp[:, hh * 8: hh * 8 + 8], ALU.add),
                    waits=[("dself", sv), ("vone", VONE_ALL)], inc="lrel")
        LREL_ALL = P.c("lrel")

        if debug & 4:
            P.op("gpsimd", lambda e: e.dma_start(out=dbg["c"], in_=c_all[:, :]),
                 waits=[("lrel", LREL_ALL)], inc="dbgs", amt=16)

        P.op("vector", lambda e: e.memset(cT[:, :], 0.0), waits=R1GATE)
        P.op("vector", lambda e: e.memset(cT[64:65, :], 1.0))
        P.op("vector", lambda e: e.tensor_copy(cbf[:, :], c_all[:, :]),
             waits=[("lrel", LREL_ALL)], inc="dcb")
        DCB_V = P.c("dcb")
        REV_ALL = P.c("rev")
        ctslot = Slots(2)
        for tc in range(OCH):
            P.op("tensor", lambda e, tc=tc: e.transpose(
                SBb16[0:8, (tc % 2) * 1024:(tc % 2) * 1024 + 128],
                cbf[:, tc * 8: tc * 8 + 8], identb[:, :]),
                waits=[("dcb", DCB_V), ("rev", REV_ALL)] + ctslot.acquire(tc), inc="tcT")
            tv = P.c("tcT")
            P.op("vector", lambda e, tc=tc: e.tensor_copy(
                cT[0:8, tc * 128:(tc + 1) * 128],
                SBb.ap().bitcast(FP16)[0:8, (tc % 2) * 1024:(tc % 2) * 1024 + 128]),
                waits=[("tcT", tv)], inc="dcT")
            ctslot.release(tc, "dcT", P.c("dcT"))
        DCT_ALL = P.c("dcT")

        for ot in range(4):
            for tc in range(OCH):
                bank = BANKS[tc]
                bw = p3bank.acquire(tc)
                P.op("tensor", lambda e, bank=bank, tc=tc, ot=ot: e.matmul(
                    bank, cT[:, tc * 128:(tc + 1) * 128],
                    ebpb[:, ot * 512:(ot + 1) * 512],
                    start=True, stop=True),
                    waits=[("dcT", DCT_ALL), ("su2", SU2_V)] + bw,
                    inc="emm")
                emv = P.c("emm")
                P.op("vector", lambda e, bank=bank, tc=tc, ot=ot: e.tensor_tensor(
                    acc[:, tc * D + ot * 512: tc * D + ot * 512 + 512],
                    acc[:, tc * D + ot * 512: tc * D + ot * 512 + 512],
                    bank, ALU.add),
                    waits=[("emm", emv)], inc="eev")
                p3bank.release(tc, "eev", P.c("eev"))

        def load_expert_slab(s):
            ei, ot = s // 4, s % 4
            G = NCT + s
            slw = wslslot.acquire(G)
            P.op("gpsimd", lambda e, ei=ei, ot=ot, G=G: e.dma_start(
                out=wsl.ap().rearrange("p (s c n) -> p s c n", s=2, n=512)[:, G % 2, 0:16],
                in_=weT[ei, :, ot * 512:(ot + 1) * 512].rearrange("(c p) n -> p c n", p=128)),
                waits=slw, inc="wsl", amt=16)
            slab_ready[G] = P.c("wsl")

        load_expert_slab(0)
        load_expert_slab(1)
        rso_v = {}
        out_v = {}
        for s in range(32):
            ei, ot = s // 4, s % 4
            G = NCT + s
            slv = slab_ready[G]
            if s == 31 and phases >= 4:
                EMM_S30 = P.c("emm")
                OUTD_BASE = P.c("outd")
                for tc in range(OCH):
                    sl = tc % 2
                    w = [("cc", CC_V), ("emm", EMM_S30)]
                    if tc >= 2:
                        w.append(("outd", out_v[tc - 2]))
                    P.op("sync", lambda e, tc=tc, sl=sl: e.dma_start(
                        out=wsl[:, sl * D:(sl + 1) * D],
                        in_=rs_out_d.ap()[tc * 128:(tc + 1) * 128, :]),
                        waits=w, inc="rsod", amt=16)
                    rso_v[tc] = P.c("rsod")
                    out_v[tc] = OUTD_BASE + (tc + 1) * 16
            for tc in range(OCH):
                bank = BANKS[tc]
                bw = p3bank.acquire(tc)
                for dk in range(16):
                    P.op("tensor", lambda e, bank=bank, dk=dk, G=G, tc=tc: e.matmul(
                        bank,
                        xnfo[:, dk * 1024 + tc * 128: dk * 1024 + tc * 128 + 128],
                        wsl[:, (G % 2) * SLABW + dk * 512:(G % 2) * SLABW + dk * 512 + 512],
                        start=(dk == 0), stop=(dk == 15)),
                        waits=([("wsl", slv)] + bw) if dk == 0 else [],
                        inc="emm" if dk == 15 else None)
                emv = P.c("emm")
                P.op("vector", lambda e, bank=bank, tc=tc, ot=ot, ei=ei: e.scalar_tensor_tensor(
                    acc[:, tc * D + ot * 512: tc * D + ot * 512 + 512],
                    bank, c_all[:, tc * 8 + ei: tc * 8 + ei + 1],
                    acc[:, tc * D + ot * 512: tc * D + ot * 512 + 512],
                    ALU.mult, ALU.add),
                    waits=[("emm", emv)], inc="eev")
                p3bank.release(tc, "eev", P.c("eev"))
                if s == 31 and phases >= 4:
                    sl = tc % 2
                    P.opd("vector", lambda e, tc=tc, sl=sl: e.tensor_tensor(
                        acc[:, tc * D:(tc + 1) * D], acc[:, tc * D:(tc + 1) * D],
                        wsl[:, sl * D:(sl + 1) * D], ALU.add),
                        waits=[("rsod", rso_v[tc])], inc="fev")
                    fv = P.c("fev")
                    P.op("gpsimd", lambda e, tc=tc: e.dma_start(
                        out=out[tc * 128:(tc + 1) * 128, :],
                        in_=acc[:, tc * D:(tc + 1) * D]),
                        waits=[("fev", fv)], inc="outd", amt=16)
            wslslot.release(G, "emm", P.c("emm"))
            if s + 2 < 32:
                load_expert_slab(s + 2)
        EEV_ALL = P.c("eev")
        assert EEV_ALL == 32 + 256

        if debug & 4:
            P.op("gpsimd", lambda e: e.dma_start(out=dbg["acc"], in_=acc[:, :]),
                 waits=[("eev", EEV_ALL)], inc="dbgs", amt=16)
        if debug & 2:
            P.op("gpsimd", lambda e: e.dma_start(out=dbg["rs"], in_=rs_out_d.ap()),
                 waits=[("cc", CC_V)], inc="dbgs", amt=16)

        if phases < 4:
            P.op("sync", lambda e: e.dma_start(out=out, in_=xo),
                 waits=[("eev", EEV_ALL)], inc="outd", amt=16)
            _emit(nc, P, [("outd", P.c("outd")), ("dbgs", P.c("dbgs")), ("cc", CC_V)])
            _close_all()
            return nc

        OUTD_ALL = P.c("outd")
        assert OUTD_ALL == OCH * 16

        _emit(nc, P, [("outd", OUTD_ALL), ("dbgs", P.c("dbgs")), ("cc", CC_V)])
        _close_all()
    return nc


def _emit(nc, P, final_waits):
    global LAST_PROG
    LAST_PROG = (P, final_waits)
    with nc.Block() as block:
        @block.tensor
        def _(e):
            P.run("tensor", e)

        @block.vector
        def _(e):
            P.run("vector", e)

        @block.scalar
        def _(e):
            P.run("scalar", e)

        @block.gpsimd
        def _(e):
            P.run("gpsimd", e)

        @block.sync
        def _(e):
            P.run("sync", e, final_waits=final_waits)



def _mk_masks():
    m = np.zeros((128, 4 * 512), dtype=np.float32)
    f = np.arange(512)
    for c in range(4):
        m[:, c * 512:(c + 1) * 512] = (f[None, :] >= (c * 128 + np.arange(128))[:, None])
    return m


def kernel(L0_attn_x, L0_attn_w_qkv, L0_attn_b_qkv, L0_attn_w_out, L0_attn_b_out,
           L0_ffn_router_w, L0_ffn_router_b, L0_ffn_expert_w, L0_ffn_expert_b):
    x = np.ascontiguousarray(np.asarray(L0_attn_x, dtype=np.float32))
    w_qkv = np.asarray(L0_attn_w_qkv, dtype=np.float32)
    b_qkv = np.asarray(L0_attn_b_qkv, dtype=np.float32)
    w_out = np.asarray(L0_attn_w_out, dtype=np.float32)
    b_out = np.asarray(L0_attn_b_out, dtype=np.float32)
    router_w = np.asarray(L0_ffn_router_w, dtype=np.float32)
    router_b = np.asarray(L0_ffn_router_b, dtype=np.float32)
    expert_w = np.asarray(L0_ffn_expert_w, dtype=np.float32)
    expert_b = np.asarray(L0_ffn_expert_b, dtype=np.float32)

    wrT = np.concatenate([router_w.T, router_b[None, :],
                          np.zeros((127, E * D), dtype=np.float32)], axis=0)
    wrT = np.ascontiguousarray(wrT.astype(np.float16))
    weT = np.ascontiguousarray(expert_w.transpose(0, 2, 1).astype(np.float16))
    ebp = np.zeros((128, D), dtype=np.float32)
    ebp[0:8] = expert_b
    ebp[64] = b_out
    woT = np.ascontiguousarray(w_out.T)
    masks = _mk_masks()
    ident = np.eye(128, dtype=np.float32)
    consts = np.full((128, 1), -CEXP, dtype=np.float32)

    in_maps = []
    for core in range(8):
        b, r = core // 4, core % 4
        hs0 = r * HPC
        q_rows = w_qkv[hs0 * HD:(hs0 + HPC) * HD]
        k_rows = w_qkv[2048 + hs0 * HD: 2048 + (hs0 + HPC) * HD]
        v_rows = w_qkv[4096 + hs0 * HD: 4096 + (hs0 + HPC) * HD]
        wcat = np.concatenate([q_rows, k_rows, v_rows], axis=0)
        bvpad = np.zeros((128, 1536), dtype=np.float32)
        bvpad[0, 1024:1536] = b_qkv[4096 + hs0 * HD: 4096 + (hs0 + HPC) * HD]
        wqkvT = np.ascontiguousarray(np.concatenate([wcat.T, bvpad], axis=0).astype(np.float16))
        bq = b_qkv[hs0 * HD:(hs0 + HPC) * HD].reshape(4, 128).T
        bk = b_qkv[2048 + hs0 * HD: 2048 + (hs0 + HPC) * HD].reshape(4, 128).T
        bqkv_fm = np.ascontiguousarray(np.concatenate([bq, bk], axis=1))
        woTs = np.ascontiguousarray(woT[hs0 * HD:(hs0 + HPC) * HD, :].astype(np.float16))
        xb_c = np.ascontiguousarray(x[b])
        xo_c = np.ascontiguousarray(x[b, r * 1024:(r + 1) * 1024])
        in_maps.append(dict(
            xb=xb_c, xo=xo_c, wqkvT=wqkvT, bqkv_fm=bqkv_fm,
            woTs=woTs, wrT=wrT, weT=weT, ebp=ebp,
            masks=masks, ident=ident, consts=consts,
        ))

    nc = build(debug=DEBUG, phases=PHASES)
    res = run_bass_kernel_spmd(nc, in_maps, core_ids=list(range(8)))

    full = np.empty((2, SEQ, D), dtype=np.float32)
    for core in range(8):
        b, r = core // 4, core % 4
        full[b, r * 1024:(r + 1) * 1024] = res.results[core]["out"]
    kernel.last_results = res
    return full
